# version 10
# baseline (speedup 1.0000x reference)
import numpy as np
import ml_dtypes
from contextlib import ExitStack
import concourse.bass as bass
import concourse.mybir as mybir
from concourse.bass_utils import run_bass_kernel_spmd

F32 = mybir.dt.float32
BF16 = mybir.dt.bfloat16
I32 = mybir.dt.int32
AF = mybir.ActivationFunctionType
ALU = mybir.AluOpType
AX = mybir.AxisListType.X

B, S, D = 4, 4096, 2048
NCORE = 8
TOK = (B * S) // NCORE
DC = D // 128
NTT = TOK // 512
LNEPS = 1e-5


def _dm(x2d):
    return np.ascontiguousarray(np.asarray(x2d).T)


def _pk(w, ocw=128):
    w = np.asarray(w)
    din, dout = w.shape
    dc, noc = din // 128, dout // ocw
    return np.ascontiguousarray(
        w.reshape(dc, 128, noc, ocw).transpose(2, 1, 0, 3).reshape(noc, 128, dc * ocw))


class P:
    def __init__(self, name):
        self.nc = bass.Bass("TRN2", target_bir_lowering=False, debug=False)
        self.es = ExitStack()
        self.name = name
        self.tag = ""
        self.scopes = [self.es]
        self.keep = []

    def push_scope(self, tag=""):
        self.tag = tag
        self.scopes.append(ExitStack())

    def pop_scope(self):
        self.scopes.pop().close()
        self.tag = ""

    def inp(self, name, shape, dt=F32):
        return self.nc.dram_tensor(name, shape, dt, kind="ExternalInput").ap()

    def out(self, name, shape, dt=F32):
        return self.nc.dram_tensor(name, shape, dt, kind="ExternalOutput").ap()

    def dram(self, name, shape, dt=F32, **kw):
        return self.nc.dram_tensor(self.tag + name, shape, dt, kind="Internal", **kw).ap()

    def sb(self, name, shape, dt=F32):
        h = self.scopes[-1].enter_context(self.nc.sbuf_tensor(self.tag + name, shape, dt))
        self.keep.append(h)
        return h

    def ps(self, name, shape, dt=F32):
        h = self.scopes[-1].enter_context(self.nc.psum_tensor(self.tag + name, shape, dt))
        self.keep.append(h)
        return h

    def sem(self, name):
        return self.es.enter_context(self.nc.semaphore(self.tag + name))

    def done(self):
        while len(self.scopes) > 1:
            self.pop_scope()
        self.es.close()
        return self.nc


def emit_p1(p, sc, io):
    nc = p.nc
    x_dm = io["x_dm"]
    Wq = io["Wq"]; Wk = io["Wk"]; Wv = io["Wv"]; Wf = io["Wf"]
    q_dm = io["q_dm"]; k_dm = io["k_dm"]; v_tm = io["v_tm"]; ld_dm = io["ld_dm"]

    xt = p.sb("xt", [128, 2, DC, 512], F32)
    xsq = p.sb("xsq", [128, 2, 512], F32)
    xn = p.sb("xn", [128, DC, TOK], BF16)
    ones_c = p.sb("ones_c", [128, 1], F32)
    ones_r = p.sb("ones_r", [1, 128], F32)
    srow = p.sb("srow", [1, 6, 512], F32)
    wsl = p.sb("wsl", [128, 2, DC, 128], BF16)
    wvsl = p.sb("wvsl", [128, DC, 512], BF16)
    wfsl = p.sb("wfsl", [128, DC, 16], BF16)
    ostg = p.sb("ostg", [128, 2, TOK], BF16)
    vstg = p.sb("vstg", [128, 2, 512], BF16)
    ldsb = p.sb("ldsb", [16, TOK], F32)
    lds1 = p.sb("lds1", [16, NTT, 512], F32)
    lds2 = p.sb("lds2", [16, NTT, 512], F32)

    p_s = p.ps("p_s", [1, 512], F32)
    p_q = p.ps("p_q", [1, 512], F32)
    p_r = p.ps("p_r", [128, 512], F32)
    p_m = p.ps("p_m", [128, 512], F32)
    p_o = p.ps("p_o", [128, 2, 512], F32)
    p_l = p.ps("p_l", [16, 512], F32)

    s_x = [p.sem("s_x0"), p.sem("s_x1")]
    s_w = [p.sem("s_w0"), p.sem("s_w1")]
    s_wv = p.sem("s_wv")
    s_wf = p.sem("s_wf")
    s_sq = p.sem("s_sq")
    s_stat = p.sem("s_stat")
    s_fva = p.sem("s_fva")
    s_fsc = p.sem("s_fsc")
    s_fin = p.sem("s_fin")
    s_bc = p.sem("s_bc")
    s_xn = p.sem("s_xn")
    s_pj = p.sem("s_pj")
    s_evS = p.sem("s_evS")
    s_evV = p.sem("s_evV")
    s_stq = [p.sem("s_stq0"), p.sem("s_stq1")]
    s_stl = p.sem("s_stl")
    s_stv = [p.sem("s_stv0"), p.sem("s_stv1")]
    s_one = p.sem("s_one")

    NQK = 2 * 16 * NTT

    if True:
        def _f_sync(sync):
            for tt in range(NTT):
                if tt >= 2:
                    sync.wait_ge(s_xn, tt - 1)
                sync.dma_start(
                    out=xt[:, tt % 2],
                    in_=x_dm.rearrange("(c p) (t f) -> p c t f", p=128, f=512)[:, :, tt],
                ).then_inc(s_x[tt % 2], 16)
            for wi, od in enumerate([q_dm, k_dm]):
                for oc in range(16):
                    idx = wi * 16 + oc
                    sync.wait_ge(s_evS, (idx + 1) * NTT)
                    sync.dma_start(out=od[oc * 128:(oc + 1) * 128, :], in_=ostg[:, idx % 2])\
                        .then_inc(s_stq[idx % 2], 16)
            sync.wait_ge(s_evV, NTT)
            sync.dma_start(out=ld_dm[:], in_=ldsb[:]).then_inc(s_stl, 16)
            for vc in range(4):
                for to in range(16):
                    vi = vc * 16 + to
                    sync.wait_ge(s_evV, NTT + vi + 1)
                    sync.dma_start(
                        out=v_tm[to * 128:(to + 1) * 128, vc * 512:(vc + 1) * 512],
                        in_=vstg[:, vi % 2],
                    ).then_inc(s_stv[vi % 2], 16)

            return sync.nop()

        def _f_gpsimd(gpsimd):
            gpsimd.dma_start(out=wfsl[:], in_=Wf[0].rearrange("p (k f) -> p k f", f=16)).then_inc(s_wf, 16)
            for wi, W in enumerate([Wq, Wk]):
                for oc in range(16):
                    idx = wi * 16 + oc
                    if idx >= 2:
                        gpsimd.wait_ge(s_pj, (idx - 1) * NTT)
                    gpsimd.dma_start(
                        out=wsl[:, idx % 2],
                        in_=W[oc].rearrange("p (k f) -> p k f", f=128),
                    ).then_inc(s_w[idx % 2], 16)
            for vc in range(4):
                if vc > 0:
                    gpsimd.wait_ge(s_pj, NQK + NTT + vc * 16)
                gpsimd.dma_start(
                    out=wvsl[:],
                    in_=Wv[vc].rearrange("p (k f) -> p k f", f=512),
                ).then_inc(s_wv, 16)
            gpsimd.wait_ge(s_stq[0], 16 * 16)
            gpsimd.wait_ge(s_stq[1], 16 * 16)
            gpsimd.wait_ge(s_stl, 16)
            gpsimd.wait_ge(s_stv[0], 32 * 16)
            gpsimd.wait_ge(s_stv[1], 32 * 16)

            return gpsimd.nop()

        def _f_scalar(scalar):
            for tt in range(NTT):
                scalar.wait_ge(s_x[tt % 2], 16 * (tt // 2 + 1))
                for c in range(DC):
                    n = tt * DC + c
                    if n >= 2:
                        scalar.wait_ge(s_stat, n - 1)
                    scalar.activation(out=xsq[:, n % 2], in_=xt[:, tt % 2, c], func=AF.Square).then_inc(s_sq, 1)
                scalar.wait_ge(s_fva, tt + 1)
                scalar.activation(out=srow[:, 3], in_=srow[:, 2], func=AF.Ln)
                scalar.drain()
                scalar.activation(out=srow[:, 3], in_=srow[:, 3], func=AF.Exp, scale=-0.5).then_inc(s_fsc, 1)
            for wi in range(2):
                for oc in range(16):
                    idx = wi * 16 + oc
                    for tt in range(NTT):
                        i = idx * NTT + tt
                        scalar.wait_ge(s_pj, i + 1)
                        if idx >= 2 and tt == 0:
                            scalar.wait_ge(s_stq[idx % 2], 16 * (idx // 2))
                        scalar.activation(
                            out=ostg[:, idx % 2, tt * 512:(tt + 1) * 512], in_=p_o[:, i % 2], func=AF.Silu,
                        ).then_inc(s_evS, 1)
            for tt in range(NTT):
                i = NQK + tt
                scalar.wait_ge(s_pj, i + 1)
                scalar.activation(out=lds1[:, tt], in_=p_l[:], func=AF.Relu, scale=-1.0)
                scalar.activation(out=lds2[:, tt], in_=p_l[:], func=AF.Abs)
                scalar.drain()
                scalar.activation(out=lds2[:, tt], in_=lds2[:, tt], func=AF.Exp, scale=-1.0)
                scalar.drain()
                scalar.activation(out=lds2[:, tt], in_=lds2[:, tt], func=AF.Ln, bias=1.0).then_inc(s_evS, 1)

            return scalar.nop()

        def _f_tensor(tensor):
            tensor.wait_ge(s_one, 1)
            for tt in range(NTT):
                if tt > 0:
                    tensor.wait_ge(s_fin, tt)
                for c in range(DC):
                    n = tt * DC + c
                    tensor.wait_ge(s_sq, n + 1)
                    tensor.matmul(p_s[:], ones_c[:], xt[:, tt % 2, c], start=(c == 0), stop=(c == DC - 1))
                    tensor.matmul(p_q[:], ones_c[:], xsq[:, n % 2], start=(c == 0), stop=(c == DC - 1)).then_inc(s_stat, 1)
                tensor.wait_ge(s_fin, tt + 1)
                if tt > 0:
                    tensor.wait_ge(s_xn, tt)
                tensor.matmul(p_r[:], ones_r[:], srow[:, 3], start=True, stop=True)
                tensor.matmul(p_m[:], ones_r[:], srow[:, 4], start=True, stop=True).then_inc(s_bc, 1)
            tensor.wait_ge(s_xn, NTT)
            for wi in range(2):
                for oc in range(16):
                    idx = wi * 16 + oc
                    tensor.wait_ge(s_w[idx % 2], 16 * (idx // 2 + 1))
                    for tt in range(NTT):
                        i = idx * NTT + tt
                        if i >= 2:
                            tensor.wait_ge(s_evS, i - 1)
                        for c in range(DC):
                            mi = tensor.matmul(p_o[:, i % 2], wsl[:, idx % 2, c],
                                               xn[:, c, tt * 512:(tt + 1) * 512],
                                               start=(c == 0), stop=(c == DC - 1))
                        mi.then_inc(s_pj, 1)
            tensor.wait_ge(s_wf, 16)
            for tt in range(NTT):
                if tt >= 1:
                    tensor.wait_ge(s_evS, NQK + tt)
                for c in range(DC):
                    mi = tensor.matmul(p_l[:], wfsl[:, c], xn[:, c, tt * 512:(tt + 1) * 512],
                                  start=(c == 0), stop=(c == DC - 1))
                mi.then_inc(s_pj, 1)
            for vc in range(4):
                tensor.wait_ge(s_wv, 16 * (vc + 1))
                for to in range(16):
                    vi = vc * 16 + to
                    i = NQK + NTT + vi
                    if vi == 0:
                        tensor.wait_ge(s_evS, NQK - 1)
                    elif vi == 1:
                        tensor.wait_ge(s_evS, NQK)
                    else:
                        tensor.wait_ge(s_evV, NTT + vi - 1)
                    for c in range(DC):
                        mi = tensor.matmul(p_o[:, i % 2], xn[:, c, to * 128:(to + 1) * 128], wvsl[:, c],
                                      start=(c == 0), stop=(c == DC - 1))
                    mi.then_inc(s_pj, 1)

            return tensor.nop()

        def _f_vector(vector):
            vector.memset(ones_c[:], 1.0)
            vector.memset(ones_r[:], 1.0).then_inc(s_one, 1)
            for tt in range(NTT):
                vector.wait_ge(s_stat, (tt + 1) * DC)
                vector.tensor_scalar(out=srow[:, 0], in0=p_s[:], scalar1=1.0 / D, scalar2=None, op0=ALU.mult)
                vector.tensor_scalar(out=srow[:, 1], in0=p_q[:], scalar1=1.0 / D, scalar2=None, op0=ALU.mult)
                vector.drain()
                vector.tensor_tensor(out=srow[:, 2], in0=srow[:, 0], in1=srow[:, 0], op=ALU.mult)
                vector.drain()
                vector.tensor_tensor(out=srow[:, 2], in0=srow[:, 1], in1=srow[:, 2], op=ALU.subtract)
                vector.drain()
                vector.tensor_scalar(out=srow[:, 2], in0=srow[:, 2], scalar1=LNEPS, scalar2=None, op0=ALU.add)\
                    .then_inc(s_fva, 1)
                vector.wait_ge(s_fsc, tt + 1)
                vector.tensor_scalar(out=srow[:, 4], in0=srow[:, 0], scalar1=-1.0, scalar2=None,
                                     op0=ALU.mult).then_inc(s_fin, 1)
                vector.wait_ge(s_bc, tt + 1)
                for c in range(DC):
                    vector.tensor_tensor(out=xt[:, tt % 2, c], in0=xt[:, tt % 2, c], in1=p_m[:], op=ALU.add)
                    vector.drain()
                    ti = vector.tensor_tensor(out=xn[:, c, tt * 512:(tt + 1) * 512], in0=xt[:, tt % 2, c],
                                              in1=p_r[:], op=ALU.mult)
                    if c == DC - 1:
                        ti.then_inc(s_xn, 1)
            for tt in range(NTT):
                vector.wait_ge(s_evS, NQK + tt + 1)
                vector.scalar_tensor_tensor(out=ldsb[:, tt * 512:(tt + 1) * 512], in0=lds1[:, tt], scalar=-1.0,
                                            in1=lds2[:, tt], op0=ALU.mult, op1=ALU.subtract).then_inc(s_evV, 1)
            for vc in range(4):
                for to in range(16):
                    vi = vc * 16 + to
                    i = NQK + NTT + vi
                    vector.wait_ge(s_pj, i + 1)
                    if vi >= 2:
                        vector.wait_ge(s_stv[vi % 2], 16 * (vi // 2))
                    vector.tensor_copy(out=vstg[:, vi % 2], in_=p_o[:, i % 2]).then_inc(s_evV, 1)
            return vector.nop()

        deps = io.get("deps", [])
        for nm, f in [("sync", _f_sync), ("gpsimd", _f_gpsimd), ("scalar", _f_scalar),
                      ("tensor", _f_tensor), ("vector", _f_vector)]:
            sc.op(nm, f, reads=deps)


def build_p1():
    p = P("p1")
    sc = Sched(p)
    io = dict(
        x_dm=p.inp("x_dm", [D, TOK]),
        Wq=p.inp("Wq", [16, 128, DC * 128]), Wk=p.inp("Wk", [16, 128, DC * 128]),
        Wv=p.inp("Wv", [4, 128, DC * 512]),
        Wf=p.inp("Wf", [1, 128, DC * 16]),
        q_dm=p.out("q_dm", [D, TOK], BF16),
        k_dm=p.out("k_dm", [D, TOK], BF16),
        v_tm=p.out("v_tm", [TOK, D], BF16),
        ld_dm=p.out("ld_dm", [16, TOK]),
    )
    emit_p1(p, sc, io)
    sc.emit()
    return p.done()


def host_p1_inputs(inputs, core):
    b, hh = core // 2, core % 2
    xs = inputs["L0_attn_x"][b, hh * TOK:(hh + 1) * TOK]
    return {
        "x_dm": _dm(xs),
        "Wq": _pk(inputs["L0_attn_Wq"]), "Wk": _pk(inputs["L0_attn_Wk"]),
        "Wv": _pk(inputs["L0_attn_Wv"], 512), "Wf": _pk(inputs["L0_attn_Wf"], 16),
    }


def emit_p2(p, sc, io):
    NCH = S // 128
    nc = p.nc
    q_srcs = io["q_srcs"]; k_srcs = io["k_srcs"]; v_srcs = io["v_srcs"]; ld_srcs = io["ld_srcs"]
    seg_in = io["seg0"]
    o_out = io["o_dm"]

    qh = p.sb("qh", [128, 2, S], BF16)
    kh = p.sb("kh", [128, 2, S], BF16)
    vh = p.sb("vh", [128, 2, NCH, 128], BF16)
    oh = p.sb("oh", [128, 2, S], F32)
    ldrow = p.sb("ldrow", [1, 2, S], F32)
    seg0 = p.sb("seg0_sb", [1, S], F32)
    relc = p.sb("relc", [1, 2, S], F32)
    expq = p.sb("expq", [1, 2, S], F32)
    rowK = p.sb("rowK", [1, 2, 128], F32)
    tiny = p.sb("tiny", [1, 2, 4], F32)
    relj_sb = p.sb("relj_sb", [128, 2, 1], F32)
    argm_sb = p.sb("argm_sb", [128, 2, 128], F32)
    D_sb = p.sb("D_sb", [128, 2, 128], F32)
    qd_sb = p.sb("qd_sb", [128, 2, 128], BF16)
    kd_sb = p.sb("kd_sb", [128, 2, 128], BF16)
    kdT_sb = p.sb("kdT_sb", [128, 2, 128], BF16)
    probs_sb = p.sb("probs_sb", [128, 2, 128], BF16)
    etc_sb = p.sb("etc_sb", [128, 2, 1], F32)
    S_sb = p.sb("S_sb", [128, 128], F32)
    S_bf = p.sb("S_bf", [128, 128], BF16)
    ident = p.sb("ident", [128, 128], BF16)
    identf = p.sb("identf", [1, 1], F32)
    ones_r = p.sb("ones_r", [1, 128], F32)

    p_sc = [p.ps("p_sc0", [128, 128], F32), p.ps("p_sc1", [128, 128], F32)]
    p_msc = [p.ps("p_msc0", [128, 4, 128], F32), p.ps("p_msc1", [128, 4, 128], F32)]
    p_o = [p.ps("p_o0", [128, 128], F32), p.ps("p_o1", [128, 128], F32)]
    p_T = p.ps("p_T", [128, 128], F32)
    p_kdT = p.ps("p_kdT", [128, 128], BF16)

    s_ld = [p.sem("s_ld0"), p.sem("s_ld1")]
    s_ldr = [p.sem("s_ldr0"), p.sem("s_ldr1")]
    s_seg = p.sem("s_seg")
    s_one = p.sem("s_one")
    s_rel = p.sem("s_rel")
    s_expq = p.sem("s_expq")
    s_row = p.sem("s_row")
    s_msc = p.sem("s_msc")
    s_arg = p.sem("s_arg")
    s_dsb = p.sem("s_dsb")
    s_dm = p.sem("s_dm")
    s_sc = p.sem("s_sc")
    s_pb = p.sem("s_pb")
    s_qd = p.sem("s_qd")
    s_kd = p.sem("s_kd")
    s_etc = p.sem("s_etc")
    s_kdT = p.sem("s_kdT")
    s_kdte = p.sem("s_kdte")
    s_o = p.sem("s_o")
    s_T = p.sem("s_T")
    s_S = p.sem("s_S")
    s_oe = p.sem("s_oe")
    s_id = p.sem("s_id")
    s_st = [p.sem("s_st0"), p.sem("s_st1")]

    def IT(h, m):
        return h * NCH + m

    NLD = {}
    if True:
        def _f_sync(sync):
            sync.dma_start(out=seg0[:], in_=seg_in[:]).then_inc(s_seg, 16)
            for h in range(8):
                if h >= 2:
                    sync.wait_ge(s_oe, NCH * (h - 1))
                    sync.wait_ge(s_T, NCH * (h - 1))
                    sync.wait_ge(s_sc, NCH * (h - 1))
                    sync.wait_ge(s_qd, NCH * (h - 1))
                    sync.wait_ge(s_kd, NCH * (h - 1))
                    sync.wait_ge(s_rel, h - 1)
                n = 0
                for col0, w, src in ld_srcs(h):
                    sync.dma_start(out=ldrow[0:1, h % 2, col0:col0 + w], in_=src).then_inc(s_ldr[h % 2], 16)
                    n += 0
                for col0, w, src in q_srcs(h):
                    sync.dma_start(out=qh[:, h % 2, col0:col0 + w], in_=src).then_inc(s_ld[h % 2], 16)
                    n += 1
                for col0, w, src in k_srcs(h):
                    sync.dma_start(out=kh[:, h % 2, col0:col0 + w], in_=src).then_inc(s_ld[h % 2], 16)
                    n += 1
                for m0, mc, src in v_srcs(h):
                    sync.dma_start(out=vh[:, h % 2, m0:m0 + mc], in_=src).then_inc(s_ld[h % 2], 16)
                    n += 1
                NLD.setdefault("n", n)
                NLD.setdefault("nldr", len(ld_srcs(h)))
            return sync.nop()

        def _f_gpsimd(gpsimd):
            gpsimd.memset(ident[:], 1.0)
            gpsimd.drain()
            gpsimd.affine_select(out=ident[:], in_=ident[:], pattern=[[-1, 128]],
                                 compare_op=ALU.is_equal, fill=0.0,
                                 base=0, channel_multiplier=1).then_inc(s_id, 1)
            for h in range(8):
                for m in range(NCH):
                    it = IT(h, m)
                    gpsimd.wait_ge(s_dsb, it + 1)
                    gpsimd.affine_select(out=D_sb[:, m % 2], in_=D_sb[:, m % 2], pattern=[[1, 128]],
                                         compare_op=ALU.is_ge, fill=0.0,
                                         base=0, channel_multiplier=-1).then_inc(s_dm, 1)
                gpsimd.wait_ge(s_oe, NCH * (h + 1))
                gpsimd.dma_start(out=o_out[h * 128:(h + 1) * 128, :], in_=oh[:, h % 2]).then_inc(s_st[h % 2], 16)
            gpsimd.wait_ge(s_st[0], 16 * 4)
            gpsimd.wait_ge(s_st[1], 16 * 4)
            return gpsimd.nop()

        def _f_vector(vector):
            vector.memset(ones_r[:], 1.0)
            vector.memset(identf[:], 1.0).then_inc(s_one, 1)
            vector.wait_ge(s_seg, 16)
            for h in range(8):
                vector.wait_ge(s_ldr[h % 2], 16 * NLD["nldr"] * (h // 2 + 1))
                if h >= 2:
                    vector.wait_ge(s_row, NCH * (h - 1))
                    vector.wait_ge(s_msc, NCH * (h - 1))
                vector.tensor_tensor_scan(
                    out=relc[:, h % 2], data0=seg0[:], data1=ldrow[:, h % 2], initial=0.0,
                    op0=ALU.mult, op1=ALU.add,
                ).then_inc(s_rel, 1)
                vector.drain()
                for m in range(NCH):
                    it = IT(h, m)
                    i0 = m * 128
                    vector.wait_ge(s_msc, it + 1)
                    if it >= 2:
                        vector.wait_ge(s_dsb, it - 1)
                    vector.tensor_copy(out=relj_sb[:, m % 2], in_=p_msc[m % 2][:, 3, 1:2])
                    vector.drain()
                    vector.tensor_scalar(out=argm_sb[:, m % 2], in0=p_msc[m % 2][:, 0],
                                         scalar1=relj_sb[:, m % 2], scalar2=0.0,
                                         op0=ALU.subtract, op1=ALU.min).then_inc(s_arg, 1)
                    if it >= 2:
                        vector.wait_ge(s_o, it - 1)
                    vector.tensor_tensor(out=qd_sb[:, m % 2], in0=qh[:, h % 2, i0:i0 + 128],
                                         in1=p_msc[m % 2][:, 1], op=ALU.mult).then_inc(s_qd, 1)
                    if it >= 2:
                        vector.wait_ge(s_kdT, it - 1)
                    vector.tensor_tensor(out=kd_sb[:, m % 2], in0=kh[:, h % 2, i0:i0 + 128],
                                         in1=p_msc[m % 2][:, 2], op=ALU.mult).then_inc(s_kd, 1)
                    vector.tensor_copy(out=etc_sb[:, m % 2], in_=p_msc[m % 2][:, 3, 0:1]).then_inc(s_etc, 1)
                    vector.wait_ge(s_sc, it + 1)
                    vector.wait_ge(s_dm, it + 1)
                    vector.tensor_tensor(out=probs_sb[:, m % 2], in0=p_sc[m % 2][:],
                                         in1=D_sb[:, m % 2], op=ALU.mult).then_inc(s_pb, 1)
                    vector.wait_ge(s_kdT, it + 1)
                    vector.tensor_copy(out=kdT_sb[:, m % 2], in_=p_kdT[:]).then_inc(s_kdte, 1)
                    vector.wait_ge(s_o, it + 1)
                    if m == 0 and h >= 2:
                        vector.wait_ge(s_st[h % 2], 16 * (h // 2))
                    vector.tensor_copy(out=oh[:, h % 2, i0:i0 + 128], in_=p_o[m % 2][:]).then_inc(s_oe, 1)
                    vector.wait_ge(s_T, it + 1)
                    if m == 0:
                        vector.tensor_copy(out=S_sb[:], in_=p_T[:])
                        vector.tensor_copy(out=S_bf[:], in_=p_T[:]).then_inc(s_S, 1)
                    else:
                        vector.drain()
                        vector.scalar_tensor_tensor(out=S_sb[:], in0=S_sb[:], scalar=etc_sb[:, m % 2],
                                                    in1=p_T[:], op0=ALU.mult, op1=ALU.add)
                        vector.drain()
                        vector.tensor_copy(out=S_bf[:], in_=S_sb[:]).then_inc(s_S, 1)

            return vector.nop()

        def _f_scalar(scalar):
            for h in range(8):
                scalar.wait_ge(s_rel, h + 1)
                if h >= 2:
                    scalar.wait_ge(s_msc, NCH * (h - 1))
                scalar.activation(out=expq[:, h % 2], in_=relc[:, h % 2], func=AF.Exp).then_inc(s_expq, 1)
                for m in range(NCH):
                    it = IT(h, m)
                    i0 = m * 128
                    if it >= 2:
                        scalar.wait_ge(s_msc, it - 1)
                    scalar.activation(out=rowK[:, m % 2], in_=relc[:, h % 2, i0:i0 + 128], func=AF.Exp,
                                      scale=-1.0, bias=relc[:, h % 2, i0 + 127:i0 + 128])
                    scalar.activation(out=tiny[:, m % 2, 2:3], in_=relc[:, h % 2, i0 + 127:i0 + 128],
                                      func=AF.Exp).then_inc(s_row, 1)
                    scalar.wait_ge(s_arg, it + 1)
                    if it >= 2:
                        scalar.wait_ge(s_dm, it - 1)
                    scalar.activation(out=D_sb[:, m % 2], in_=argm_sb[:, m % 2], func=AF.Exp).then_inc(s_dsb, 1)

            return scalar.nop()

        def _f_tensor(tensor):
            tensor.wait_ge(s_one, 1)
            tensor.wait_ge(s_id, 1)
            for h in range(8):
                tensor.wait_ge(s_ld[h % 2], 16 * NLD["n"] * (h // 2 + 1))
                tensor.wait_ge(s_expq, h + 1)
                for m in range(NCH):
                    it = IT(h, m)
                    i0 = m * 128
                    if it >= 2:
                        tensor.wait_ge(s_pb, it - 1)
                    tensor.matmul(p_sc[m % 2][:], kh[:, h % 2, i0:i0 + 128], qh[:, h % 2, i0:i0 + 128],
                                  start=True, stop=True).then_inc(s_sc, 1)
                    tensor.wait_ge(s_row, it + 1)
                    if it >= 2:
                        tensor.wait_ge(s_qd, it - 1)
                        tensor.wait_ge(s_kd, it - 1)
                        tensor.wait_ge(s_etc, it - 1)
                        tensor.wait_ge(s_arg, it - 1)
                    tensor.matmul(p_msc[m % 2][:, 0], ones_r[:], relc[:, h % 2, i0:i0 + 128],
                                  start=True, stop=True)
                    tensor.matmul(p_msc[m % 2][:, 1], ones_r[:], expq[:, h % 2, i0:i0 + 128],
                                  start=True, stop=True)
                    tensor.matmul(p_msc[m % 2][:, 2], ones_r[:], rowK[:, m % 2], start=True, stop=True)
                    tensor.matmul(p_msc[m % 2][:, 3, 0:1], ones_r[:], tiny[:, m % 2, 2:3],
                                  start=True, stop=True)
                    tensor.transpose(p_msc[m % 2][:, 3, 1:2], relc[:, h % 2, i0:i0 + 128],
                                     identf[:]).then_inc(s_msc, 1)
                    tensor.wait_ge(s_kd, it + 1)
                    if it >= 1:
                        tensor.wait_ge(s_kdte, it)
                    tensor.transpose(p_kdT[:], kd_sb[:, m % 2], ident[:]).then_inc(s_kdT, 1)
                    tensor.wait_ge(s_pb, it + 1)
                    if it >= 2:
                        tensor.wait_ge(s_oe, it - 1)
                    if m == 0:
                        tensor.matmul(p_o[m % 2][:], vh[:, h % 2, m], probs_sb[:, m % 2],
                                      start=True, stop=True).then_inc(s_o, 1)
                    else:
                        tensor.matmul(p_o[m % 2][:], vh[:, h % 2, m], probs_sb[:, m % 2],
                                      start=True, stop=False)
                        tensor.wait_ge(s_qd, it + 1)
                        tensor.wait_ge(s_S, it)
                        tensor.matmul(p_o[m % 2][:], S_bf[:], qd_sb[:, m % 2],
                                      start=False, stop=True).then_inc(s_o, 1)
                    tensor.wait_ge(s_kdte, it + 1)
                    tensor.wait_ge(s_S, it)
                    tensor.matmul(p_T[:], kdT_sb[:, m % 2], vh[:, h % 2, m],
                                  start=True, stop=True).then_inc(s_T, 1)
            return tensor.nop()

        deps = io.get("deps", [])
        for nm, f in [("sync", _f_sync), ("gpsimd", _f_gpsimd), ("scalar", _f_scalar),
                      ("tensor", _f_tensor), ("vector", _f_vector)]:
            sc.op(nm, f, reads=deps)


def build_p2():
    p = P("p2")
    sc = Sched(p)
    q_in = p.inp("q_dm", [1024, S], BF16)
    k_in = p.inp("k_dm", [1024, S], BF16)
    v_in = p.inp("v_tm", [S, 1024], BF16)
    ld_in = p.inp("ld8", [8, S])
    io = dict(
        q_srcs=lambda h: [(0, S, q_in[h * 128:(h + 1) * 128, :])],
        k_srcs=lambda h: [(0, S, k_in[h * 128:(h + 1) * 128, :])],
        v_srcs=lambda h: [(0, S // 128,
                           v_in.rearrange("(m p) e -> p m e", p=128)[:, :, h * 128:(h + 1) * 128])],
        ld_srcs=lambda h: [(0, S, ld_in[h:h + 1, :])],
        seg0=p.inp("seg0", [1, S]),
        o_dm=p.out("o_dm", [1024, S], BF16),
    )
    emit_p2(p, sc, io)
    sc.emit()
    return p.done()


def host_p2_inputs(p1res, inputs, core):
    b, hs = core // 2, (core % 2) * 8
    cA, cB = 2 * b, 2 * b + 1
    q = np.concatenate([p1res[cA]["q_dm"], p1res[cB]["q_dm"]], axis=1)
    k = np.concatenate([p1res[cA]["k_dm"], p1res[cB]["k_dm"]], axis=1)
    v = np.concatenate([p1res[cA]["v_tm"], p1res[cB]["v_tm"]], axis=0)
    ld = np.concatenate([p1res[cA]["ld_dm"], p1res[cB]["ld_dm"]], axis=1)
    seg = np.ones((1, S), np.float32)
    seg[0, ::128] = 0.0
    return {
        "q_dm": np.ascontiguousarray(q[hs * 128:(hs + 8) * 128]),
        "k_dm": np.ascontiguousarray(k[hs * 128:(hs + 8) * 128]),
        "v_tm": np.ascontiguousarray(v[:, hs * 128:(hs + 8) * 128]),
        "ld8": np.ascontiguousarray(ld[hs:hs + 8]),
        "seg0": seg,
    }


class Rr:
    def __init__(self, t, key=None):
        self.k = (id(t), key)


class Sched:
    CENG = ("tensor", "vector", "scalar", "gpsimd", "sync")
    DRAIN_ENG = ("vector", "scalar", "gpsimd")

    def __init__(self, p):
        self.p = p
        self.ops = []
        self.excl_ids = set()

    def exclusive(self, *tensors):
        for t in tensors:
            self.excl_ids.add(id(t))

    def op(self, engine, fn, reads=(), writes=()):
        self.ops.append(dict(
            eng=engine, fn=fn, reads=[r.k for r in reads], writes=[w.k for w in writes],
            dma=False, cc=False, barrier=False))

    def dma(self, queue, fn, reads=(), writes=(), slot=None):
        rs = [r.k for r in reads]
        ws = [w.k for w in writes]
        sk = slot.k if slot is not None else (ws[0] if ws else rs[0])
        self.ops.append(dict(
            eng=queue, fn=fn, reads=rs, writes=ws, dma=True, cc=False, barrier=False,
            slot=(sk, queue)))

    def cc(self, fn, reads=(), writes=()):
        self.ops.append(dict(
            eng="gpsimd", fn=fn, reads=[r.k for r in reads], writes=[w.k for w in writes],
            dma=False, cc=True, barrier=False))

    def barrier(self):
        self.ops.append(dict(eng="gpsimd", fn=None, reads=[], writes=[], dma=False,
                             cc=False, barrier=True))

    def emit(self):
        import os
        frac = float(os.environ.get("MK_TRUNC", "1.0"))
        if frac > 1.0:
            self.ops = self.ops[:int(frac)]
        elif frac < 1.0:
            self.ops = self.ops[:int(len(self.ops) * frac)]
        ops = self.ops
        last_writer = {}
        readers_since = {}
        prev_dma = {}
        last_reader = {}
        last_eng_op = {}
        all_since_barrier = []
        for i, o in enumerate(ops):
            deps = set()
            if o["barrier"]:
                deps.update(all_since_barrier)
                o["deps"] = deps
                all_since_barrier = [i]
                last_eng_op[o["eng"]] = i
                continue
            if all_since_barrier and ops[all_since_barrier[0]]["barrier"]:
                deps.add(all_since_barrier[0])
            if o["dma"]:
                if o["slot"] in prev_dma:
                    deps.add(prev_dma[o["slot"]])
                prev_dma[o["slot"]] = i
            if o["cc"]:
                if "cc_prev" in last_eng_op:
                    deps.add(last_eng_op["cc_prev"])
                last_eng_op["cc_prev"] = i
            for r in o["reads"]:
                if r in last_writer:
                    deps.add(last_writer[r])
                if r[0] in self.excl_ids:
                    if r in last_reader:
                        deps.add(last_reader[r])
                    last_reader[r] = i
            for w in o["writes"]:
                if w in last_writer:
                    deps.add(last_writer[w])
                for rd in readers_since.get(w, ()):
                    deps.add(rd)
            o["deps"] = deps
            for r in o["reads"]:
                readers_since.setdefault(r, []).append(i)
            for w in o["writes"]:
                last_writer[w] = i
                readers_since[w] = []
            last_eng_op[o["eng"]] = i
            all_since_barrier.append(i)
        needed = set()
        for i, o in enumerate(ops):
            if o["barrier"]:
                needed.update(o["deps"])
                continue
            for d in o["deps"]:
                od = ops[d]
                if od["dma"] or od["cc"] or od["barrier"] or od["eng"] != o["eng"]:
                    needed.add(d)
                elif o["dma"]:
                    needed.add(d)
        eng_sems = {e: self.p.sem(f"sch_{e}") for e in self.CENG}
        eng_rank = {e: 0 for e in self.CENG}
        cc_sem = self.p.sem("sch_cc")
        cc_cnt = 0
        slot_sems = {}
        free_sems = {"sync": [], "gpsimd": []}
        n_slot_sems = 0
        token = {}
        for i, o in enumerate(ops):
            if o["barrier"]:
                for sk, (sem, cnt) in slot_sems.items():
                    free_sems[sk[1]].append((sem, cnt))
                slot_sems = {}
                e = o["eng"]
                eng_rank[e] += 1
                token[i] = (eng_sems[e], eng_rank[e], 1)
            elif o["dma"]:
                sk = o["slot"]
                if sk not in slot_sems:
                    if free_sems[sk[1]]:
                        slot_sems[sk] = free_sems[sk[1]].pop()
                    else:
                        n_slot_sems += 1
                        slot_sems[sk] = (self.p.sem(f"sdma_{n_slot_sems}"), 0)
                sem, cnt = slot_sems[sk]
                cnt += 1
                slot_sems[sk] = (sem, cnt)
                token[i] = (sem, 16 * cnt, 16)
            elif o["cc"]:
                cc_cnt += 1
                token[i] = (cc_sem, cc_cnt, 1)
            elif i in needed:
                e = o["eng"]
                eng_rank[e] += 1
                token[i] = (eng_sems[e], eng_rank[e], 1)
        comp = [None] * len(ops)
        for i, o in enumerate(ops):
            C = {}
            for d in o["deps"]:
                od = ops[d]
                waited = (od["dma"] or od["cc"] or od["barrier"]
                          or od["eng"] != o["eng"] or o["barrier"] or o["dma"])
                if not waited or d not in token:
                    continue
                for e2, v2 in comp[d].items():
                    if C.get(e2, -1) < v2:
                        C[e2] = v2
                if od["barrier"]:
                    for e2 in self.CENG:
                        if C.get(e2, -1) < d:
                            C[e2] = d
                elif not od["dma"] and not od["cc"]:
                    if C.get(od["eng"], -1) < d:
                        C[od["eng"]] = d
            comp[i] = C

        watermark = {}
        final_dma = {}
        for i, o in enumerate(ops):
            if o["dma"]:
                final_dma[token[i][0]] = token[i][1]

        import os as _os
        _stats = {"drain": {}, "ops": {}}

        def emit_for(eng_name, eng):
            floor = -1
            last_own = -1
            lastW = {}
            lastR = {}
            for i, o in enumerate(ops):
                if o["eng"] != eng_name and not o["barrier"]:
                    continue
                if o["barrier"] and eng_name != o["eng"]:
                    continue
                want = {}
                for d in sorted(o["deps"]):
                    od = ops[d]
                    if od["dma"] or od["cc"] or od["barrier"] or od["eng"] != eng_name \
                            or o["barrier"] or o["dma"]:
                        if d not in token:
                            raise RuntimeError(f"dep {d} of op {i} has no token")
                        sem, val, _ = token[d]
                        cur = want.get(id(sem))
                        if cur is None or cur[0] < val:
                            want[id(sem)] = (val, sem)
                for sid, (val, sem) in want.items():
                    key = (eng_name, sid)
                    if watermark.get(key, -1) < val:
                        eng.wait_ge(sem, val)
                        watermark[key] = val
                if o["barrier"]:
                    if i in token:
                        sem, val, inc = token[i]
                        eng.nop().then_inc(sem, inc)
                    floor = last_own
                    last_own = i
                    continue
                fc = comp[i].get(eng_name, -1)
                if fc > floor:
                    floor = fc
                if not o["dma"] and not o["cc"] and eng_name in self.DRAIN_ENG:
                    hz = -1
                    for r in o["reads"]:
                        v = lastW.get(r, -1)
                        if v > hz:
                            hz = v
                    for w in o["writes"]:
                        v = lastW.get(w, -1)
                        if v > hz:
                            hz = v
                        v = lastR.get(w, -1)
                        if v > hz:
                            hz = v
                    if hz > floor:
                        eng.drain()
                        _stats["drain"][eng_name] = _stats["drain"].get(eng_name, 0) + 1
                        floor = last_own
                _stats["ops"][eng_name] = _stats["ops"].get(eng_name, 0) + 1
                inst = o["fn"](eng)
                if i in token:
                    sem, val, inc = token[i]
                    assert inst is not None, f"op {i} on {eng_name} must return its instruction"
                    inst.then_inc(sem, inc)
                if not o["dma"] and not o["cc"]:
                    for w in o["writes"]:
                        lastW[w] = i
                    for r in o["reads"]:
                        lastR[r] = i
                last_own = i
            if eng_name == "gpsimd":
                for sem, val in final_dma.items():
                    eng.wait_ge(sem, val)

        nc = self.p.nc
        with nc.Block() as block:
            @block.sync
            def _(sync):
                emit_for("sync", sync)

            @block.gpsimd
            def _(gpsimd):
                emit_for("gpsimd", gpsimd)

            @block.tensor
            def _(tensor):
                emit_for("tensor", tensor)

            @block.vector
            def _(vector):
                emit_for("vector", vector)

            @block.scalar
            def _(scalar):
                emit_for("scalar", scalar)
        if _os.environ.get("MK_STATS"):
            print("SCHED STATS", _stats)



HID = 5632
HC = HID // 128


def emit_p3(p, sc, io):
    nc = p.nc
    o_ap = io["o_ap"]
    x_in = io["x_dm"]
    Wo = io["Wo"]; normw = io["norm_w"]
    wqgT = io["wqgT"]; wquT = io["wquT"]; wqdT = io["wqdT"]
    sg_in = io["s_gate"]; su_in = io["s_up"]; sd_in = io["s_down"]
    dtwT = io["dtwT"]; dtb_in = io["dtb"]; xpwT = io["xpwT"]; xpb_in = io["xpb"]
    x2_out = io["x2_dm"]; xn2_out = io["xn2_dm"]; dt_out = io["dt_dm"]; bc_out = io["bc_dm"]

    ao_dm = p.dram("ao_dram", [D, TOK], BF16)
    x1_dm = p.dram("x1_dram", [D, TOK])
    gu_dm = p.dram("gu_dram", [HID, TOK], BF16)

    A = p.sb("A", [128, DC, TOK], BF16)
    xsm = p.sb("xsm", [128, DC, 512], BF16)
    gut = p.sb("gut", [128, HC, 512], BF16)
    xc = p.sb("xc", [128, 2, 512], F32)
    xcb = p.sb("xcb", [128, 2, 512], BF16)
    osq = p.sb("osq", [128, 2, 512], F32)
    t1b = p.sb("t1b", [128, 2, 512], F32)
    wdl = p.sb("wdl", [128, 2, HC, 128], BF16)
    wsl = wdl[:, :, 0:DC]
    wg2 = wdl[:, :, 0:DC]
    wu2 = wdl[:, :, DC:2 * DC]
    wxp = wdl[:, 0, DC:DC + 4].rearrange("p a b -> p (a b)").rearrange(
        "p (c e) -> p c e", e=32)
    stg = p.sb("stg", [128, 3, 512], BF16)
    ustg = p.sb("ustg", [128, 2, 512], BF16)
    asq = p.sb("asq", [128, 2, 512], F32)
    srow = p.sb("srow", [1, 5, 512], F32)
    accS = p.sb("accS", [1, NTT, 512], F32)
    accQ = p.sb("accQ", [1, NTT, 512], F32)
    gsum = p.sb("gsum", [1, 512], BF16)
    rbc = p.sb("rbc", [128, 2, 512], F32)
    mrbc = p.sb("mrbc", [128, 2, 512], F32)
    vrow = p.sb("vrow", [1, 512], F32)
    nw_c = p.sb("nw_c", [128, DC], F32)
    sg_c = p.sb("sg_c", [128, HC], F32)
    su_c = p.sb("su_c", [128, HC], F32)
    sd_c = p.sb("sd_c", [128, DC], F32)
    db_c = p.sb("db_c", [128, DC], F32)
    xpb_c = p.sb("xpb_c", [32, 1], F32)
    ones_c = p.sb("ones_c", [128, 1], F32)
    ones_cb = p.sb("ones_cb", [128, 1], BF16)
    ones_r = p.sb("ones_r", [1, 128], F32)
    neg128 = p.sb("neg128", [1, 128], BF16)
    identf = p.sb("identf", [1, 1], F32)

    p_mm = [p.ps("p_mm%d" % k, [128, 512], F32) for k in range(4)]
    p_sq = p.ps("p_sq", [128, 512], F32)
    p_r = p.ps("p_r", [128, 512], F32)
    p_m = p.ps("p_m", [128, 512], F32)
    p_t = p.ps("p_t", [128, 512], F32)

    ps_s = p_sq[0:1, 0:512]
    ps_q = p_sq[64:65, 0:512]
    rs_s = Rr(p_sq, "s")
    rs_q = Rr(p_sq, "q")

    sc.exclusive(*p_mm, p_sq, p_r, p_m, p_t)
    AF_, AL = AF, ALU

    sc.op("vector", lambda v: v.memset(ones_c[:], 1.0), writes=[Rr(ones_c)])
    sc.op("vector", lambda v: v.memset(ones_cb[:], 1.0), writes=[Rr(ones_cb)])
    sc.op("vector", lambda v: v.memset(ones_r[:], 1.0), writes=[Rr(ones_r)])
    sc.op("vector", lambda v: v.memset(neg128[:], -128.0), writes=[Rr(neg128)])
    sc.op("vector", lambda v: v.memset(identf[:], 1.0), writes=[Rr(identf)])
    sc.op("vector", lambda v: v.memset(accS[:], 0.0), writes=[Rr(accS)])
    sc.op("vector", lambda v: v.memset(accQ[:], 0.0), writes=[Rr(accQ)])

    def vec_cols(name, src, n, dst):
        for seg in range(0, n, 512):
            w = min(512, n - seg)
            sc.dma("sync", lambda s, src=src, seg=seg, w=w: s.dma_start(
                out=vrow[0:1, 0:w], in_=src.unsqueeze(0)[:, seg:seg + w]),
                reads=[], writes=[Rr(vrow)])
            for cc in range(w // 128):
                c = seg // 128 + cc
                sc.op("tensor", lambda t, cc=cc: t.transpose(p_t[:, 0:1],
                                                             vrow[0:1, cc * 128:(cc + 1) * 128], identf[:]),
                      reads=[Rr(vrow), Rr(identf)], writes=[Rr(p_t)])
                sc.op("vector", lambda v, c=c, dst=dst: v.tensor_copy(out=dst[:, c:c + 1],
                                                                      in_=p_t[:, 0:1]),
                      reads=[Rr(p_t)], writes=[Rr(dst)])

    vec_cols("nw", normw, D, nw_c)
    vec_cols("sg", sg_in, HID, sg_c)
    vec_cols("su", su_in, HID, su_c)
    vec_cols("sd", sd_in, D, sd_c)
    vec_cols("db", dtb_in, D, db_c)
    sc.dma("sync", lambda s: s.dma_start(out=vrow[0:1, 0:32], in_=xpb_in.unsqueeze(0)),
           reads=[], writes=[Rr(vrow)])
    sc.op("tensor", lambda t: t.transpose(p_t[0:32, 0:1], vrow[0:1, 0:32], identf[:]),
          reads=[Rr(vrow), Rr(identf)], writes=[Rr(p_t)])
    sc.op("vector", lambda v: v.tensor_copy(out=xpb_c[:], in_=p_t[0:32, 0:1]),
          reads=[Rr(p_t)], writes=[Rr(xpb_c)])

    fzc = [0]

    def finalize(mean, aps_s, aps_q, eps, denom=D):
        sl = fzc[0] % 2
        fzc[0] += 1
        if mean:
            sc.op("vector", lambda v: v.tensor_scalar(out=srow[:, 0], in0=aps_s[0], scalar1=1.0 / denom,
                                                      scalar2=None, op0=AL.mult),
                  reads=[aps_s[1]], writes=[Rr(srow, 0)])
            sc.op("vector", lambda v: v.tensor_scalar(out=srow[:, 1], in0=aps_q[0], scalar1=1.0 / denom,
                                                      scalar2=None, op0=AL.mult),
                  reads=[aps_q[1]], writes=[Rr(srow, 1)])
            sc.op("vector", lambda v: v.tensor_tensor(out=srow[:, 2], in0=srow[:, 0], in1=srow[:, 0],
                                                      op=AL.mult),
                  reads=[Rr(srow, 0)], writes=[Rr(srow, 2)])
            sc.op("vector", lambda v: v.tensor_tensor(out=srow[:, 2], in0=srow[:, 1], in1=srow[:, 2],
                                                      op=AL.subtract),
                  reads=[Rr(srow, 1), Rr(srow, 2)], writes=[Rr(srow, 2)])
            sc.op("vector", lambda v: v.tensor_scalar(out=srow[:, 2], in0=srow[:, 2], scalar1=eps,
                                                      scalar2=None, op0=AL.add),
                  reads=[Rr(srow, 2)], writes=[Rr(srow, 2)])
        else:
            sc.op("vector", lambda v: v.tensor_scalar(out=srow[:, 2], in0=aps_q[0], scalar1=1.0 / denom,
                                                      scalar2=eps, op0=AL.mult, op1=AL.add),
                  reads=[aps_q[1]], writes=[Rr(srow, 2)])
        sc.op("scalar", lambda s: s.activation(out=srow[:, 3], in_=srow[:, 2], func=AF_.Ln),
              reads=[Rr(srow, 2)], writes=[Rr(srow, 3)])
        sc.op("scalar", lambda s: s.activation(out=srow[:, 3], in_=srow[:, 3], func=AF_.Exp, scale=-0.5),
              reads=[Rr(srow, 3)], writes=[Rr(srow, 3)])
        sc.op("tensor", lambda t: t.matmul(p_r[:], ones_r[:], srow[:, 3], start=True, stop=True),
              reads=[Rr(srow, 3), Rr(ones_r)], writes=[Rr(p_r)])
        sc.op("scalar", lambda s, sl=sl: s.activation(out=rbc[:, sl], in_=p_r[:], func=AF_.Copy),
              reads=[Rr(p_r)], writes=[Rr(rbc, sl)])
        if mean:
            sc.op("vector", lambda v: v.scalar_tensor_tensor(out=srow[:, 4], in0=srow[:, 0],
                                                             scalar=-1.0, in1=srow[:, 3],
                                                             op0=AL.mult, op1=AL.mult),
                  reads=[Rr(srow, 0), Rr(srow, 3)], writes=[Rr(srow, 4)])
            sc.op("tensor", lambda t: t.matmul(p_m[:], ones_r[:], srow[:, 4], start=True, stop=True),
                  reads=[Rr(srow, 4), Rr(ones_r)], writes=[Rr(p_m)])
            sc.op("scalar", lambda s, sl=sl: s.activation(out=mrbc[:, sl], in_=p_m[:], func=AF_.Copy),
                  reads=[Rr(p_m)], writes=[Rr(mrbc, sl)])
        return sl

    for tt in range(NTT):
        for c in range(DC):
            i = tt * DC + c
            sc.dma("sync", lambda s, tt=tt, c=c, i=i: s.dma_start(
                out=xcb[:, i % 2], in_=o_ap(c, tt)),
                reads=[Rr(o_ap, (c, tt))], writes=[Rr(xcb, i % 2)])
            sc.op("scalar", lambda s, i=i: s.activation(out=osq[:, i % 2], in_=xcb[:, i % 2],
                                                        func=AF_.Square),
                  reads=[Rr(xcb, i % 2)], writes=[Rr(osq, i % 2)])
            sc.op("tensor", lambda t, c=c, i=i: t.matmul(ps_q, ones_c[:], osq[:, i % 2],
                                                         start=(c == 0), stop=(c == DC - 1)),
                  reads=[Rr(osq, i % 2), Rr(ones_c)], writes=[rs_q])
        sl = finalize(False, None, (ps_q, rs_q), 1e-6)
        for c in range(DC):
            i = tt * DC + c
            sc.dma("sync", lambda s, tt=tt, c=c, i=i: s.dma_start(
                out=xcb[:, i % 2], in_=o_ap(c, tt)),
                reads=[Rr(o_ap, (c, tt))], writes=[Rr(xcb, i % 2)])
            sc.op("vector", lambda v, tt=tt, c=c, i=i, sl=sl: v.scalar_tensor_tensor(
                out=A[:, c, tt * 512:(tt + 1) * 512], in0=xcb[:, i % 2],
                scalar=nw_c[:, c:c + 1], in1=rbc[:, sl], op0=AL.mult, op1=AL.mult),
                reads=[Rr(xcb, i % 2), Rr(nw_c), Rr(rbc, sl)], writes=[Rr(A, (c, tt))])

    for oc in range(16):
        sc.dma("gpsimd", lambda g, oc=oc: g.dma_start(
            out=wsl[:, oc % 2],
            in_=Wo[oc].rearrange("p (k f) -> p k f", f=128)),
            writes=[Rr(wdl, ("g", oc % 2))])
        for tt in range(NTT):
            i = oc * NTT + tt
            pm = p_mm[i % 4]
            for c in range(DC):
                sc.op("tensor", lambda t, oc=oc, tt=tt, c=c, pm=pm: t.matmul(
                    pm[:], wsl[:, oc % 2, c], A[:, c, tt * 512:(tt + 1) * 512],
                    start=(c == 0), stop=(c == DC - 1)),
                    reads=[Rr(wdl, ("g", oc % 2)), Rr(A, (c, tt))], writes=[Rr(pm)])
            sc.op("scalar", lambda s, i=i, pm=pm: s.activation(out=stg[:, i % 3], in_=pm[:],
                                                               func=AF_.Copy),
                  reads=[Rr(pm)], writes=[Rr(stg, i % 3)])
            sc.op("scalar", lambda s, i=i, pm=pm: s.activation(out=osq[:, i % 2], in_=pm[:],
                                                               func=AF_.Square),
                  reads=[Rr(pm)], writes=[Rr(osq, i % 2)])
            sc.op("tensor", lambda t, i=i: t.matmul(ps_s, ones_cb[:], stg[:, i % 3],
                                                    start=True, stop=True),
                  reads=[Rr(stg, i % 3), Rr(ones_cb)], writes=[rs_s])
            sc.op("tensor", lambda t, i=i: t.matmul(ps_q, ones_c[:], osq[:, i % 2],
                                                    start=True, stop=True),
                  reads=[Rr(osq, i % 2), Rr(ones_c)], writes=[rs_q])
            sc.op("vector", lambda v, tt=tt: v.tensor_tensor(out=accS[:, tt], in0=accS[:, tt],
                                                             in1=ps_s, op=AL.add),
                  reads=[rs_s, Rr(accS)], writes=[Rr(accS)])
            sc.op("vector", lambda v, tt=tt: v.tensor_tensor(out=accQ[:, tt], in0=accQ[:, tt],
                                                             in1=ps_q, op=AL.add),
                  reads=[rs_q, Rr(accQ)], writes=[Rr(accQ)])
            sc.dma("sync", lambda s, oc=oc, tt=tt, i=i: s.dma_start(
                out=ao_dm.rearrange("(c p) (t f) -> p c t f", p=128, f=512)[:, oc, tt],
                in_=stg[:, i % 3]),
                reads=[Rr(stg, i % 3)], writes=[Rr(ao_dm, (oc, tt))], slot=Rr(stg, i % 3))
    for tt in range(NTT):
        sl1 = finalize(True, (accS[:, tt], Rr(accS)), (accQ[:, tt], Rr(accQ)), LNEPS)
        for c in range(DC):
            i = tt * DC + c
            sc.dma("sync", lambda s, tt=tt, c=c, i=i: s.dma_start(
                out=xc[:, i % 2],
                in_=x_in.rearrange("(c p) (t f) -> p c t f", p=128, f=512)[:, c, tt]),
                writes=[Rr(xc, i % 2)])
            sc.dma("sync", lambda s, tt=tt, c=c, i=i: s.dma_start(
                out=xcb[:, i % 2],
                in_=ao_dm.rearrange("(c p) (t f) -> p c t f", p=128, f=512)[:, c, tt]),
                reads=[Rr(ao_dm, (c, tt))], writes=[Rr(xcb, i % 2)])
            sc.op("vector", lambda v, i=i, sl1=sl1: v.tensor_tensor(
                out=asq[:, i % 2], in0=xcb[:, i % 2], in1=rbc[:, sl1], op=AL.mult),
                reads=[Rr(xcb, i % 2), Rr(rbc, sl1)], writes=[Rr(asq, i % 2)])
            sc.op("gpsimd", lambda g, i=i, sl1=sl1: g.tensor_tensor(
                out=t1b[:, i % 2], in0=asq[:, i % 2], in1=mrbc[:, sl1], op=AL.add),
                reads=[Rr(asq, i % 2), Rr(mrbc, sl1)], writes=[Rr(t1b, i % 2)])
            sc.op("vector", lambda v, i=i: v.tensor_tensor(out=xc[:, i % 2], in0=xc[:, i % 2],
                                                           in1=t1b[:, i % 2], op=AL.add),
                  reads=[Rr(xc, i % 2), Rr(t1b, i % 2)], writes=[Rr(xc, i % 2)])
            sc.dma("sync", lambda s, tt=tt, c=c, i=i: s.dma_start(
                out=x1_dm.rearrange("(c p) (t f) -> p c t f", p=128, f=512)[:, c, tt],
                in_=xc[:, i % 2]),
                reads=[Rr(xc, i % 2)], writes=[Rr(x1_dm, (c, tt))], slot=Rr(xc, i % 2))
            sc.op("scalar", lambda s, i=i: s.activation(out=osq[:, i % 2], in_=xc[:, i % 2],
                                                        func=AF_.Square),
                  reads=[Rr(xc, i % 2)], writes=[Rr(osq, i % 2)])
            sc.op("tensor", lambda t, c=c, i=i: t.matmul(ps_s, ones_c[:], xc[:, i % 2],
                                                         start=(c == 0), stop=(c == DC - 1)),
                  reads=[Rr(xc, i % 2), Rr(ones_c)], writes=[rs_s])
            sc.op("tensor", lambda t, c=c, i=i: t.matmul(ps_q, ones_c[:], osq[:, i % 2],
                                                         start=(c == 0), stop=(c == DC - 1)),
                  reads=[Rr(osq, i % 2), Rr(ones_c)], writes=[rs_q])
        sl2 = finalize(True, (ps_s, rs_s), (ps_q, rs_q), LNEPS)
        for c in range(DC):
            i = tt * DC + c
            sc.dma("sync", lambda s, tt=tt, c=c, i=i: s.dma_start(
                out=xc[:, i % 2],
                in_=x1_dm.rearrange("(c p) (t f) -> p c t f", p=128, f=512)[:, c, tt]),
                reads=[Rr(x1_dm, (c, tt))], writes=[Rr(xc, i % 2)])
            sc.op("gpsimd", lambda g, i=i, sl2=sl2: g.tensor_tensor(
                out=t1b[:, i % 2], in0=xc[:, i % 2], in1=rbc[:, sl2], op=AL.mult),
                reads=[Rr(xc, i % 2), Rr(rbc, sl2)], writes=[Rr(t1b, i % 2)])
            sc.op("vector", lambda v, tt=tt, c=c, i=i, sl2=sl2: v.tensor_tensor(
                out=A[:, c, tt * 512:(tt + 1) * 512], in0=t1b[:, i % 2], in1=mrbc[:, sl2],
                op=AL.add),
                reads=[Rr(t1b, i % 2), Rr(mrbc, sl2)], writes=[Rr(A, (c, tt))])

    for hc in range(HC):
        sc.dma("gpsimd", lambda g, hc=hc: g.dma_start(
            out=wg2[:, hc % 2],
            in_=wqgT[hc].rearrange("p (k f) -> p k f", f=128)),
            writes=[Rr(wdl, ("g", hc % 2))])
        sc.dma("gpsimd", lambda g, hc=hc: g.dma_start(
            out=wu2[:, hc % 2],
            in_=wquT[hc].rearrange("p (k f) -> p k f", f=128)),
            writes=[Rr(wdl, ("u", hc % 2))])
        for tt in range(NTT):
            i = hc * NTT + tt
            pg = p_mm[(2 * i) % 4]
            pu = p_mm[(2 * i + 1) % 4]
            for c in range(DC):
                sc.op("tensor", lambda t, hc=hc, tt=tt, c=c, pg=pg: t.matmul(
                    pg[:], wg2[:, hc % 2, c], A[:, c, tt * 512:(tt + 1) * 512],
                    start=(c == 0), stop=(c == DC - 1)),
                    reads=[Rr(wdl, ("g", hc % 2)), Rr(A, (c, tt))], writes=[Rr(pg)])
            for c in range(DC):
                sc.op("tensor", lambda t, hc=hc, tt=tt, c=c, pu=pu: t.matmul(
                    pu[:], wu2[:, hc % 2, c], A[:, c, tt * 512:(tt + 1) * 512],
                    start=(c == 0), stop=(c == DC - 1)),
                    reads=[Rr(wdl, ("u", hc % 2)), Rr(A, (c, tt))], writes=[Rr(pu)])
            sc.op("scalar", lambda s, hc=hc, i=i, pg=pg: s.activation(
                out=stg[:, i % 3], in_=pg[:], func=AF_.Silu, scale=sg_c[:, hc:hc + 1]),
                reads=[Rr(pg), Rr(sg_c)], writes=[Rr(stg, i % 3)])
            sc.op("scalar", lambda s, hc=hc, i=i, pu=pu: s.activation(
                out=ustg[:, i % 2], in_=pu[:], func=AF_.Copy, scale=su_c[:, hc:hc + 1]),
                reads=[Rr(pu), Rr(su_c)], writes=[Rr(ustg, i % 2)])
            sc.op("vector", lambda v, i=i: v.tensor_tensor(out=ustg[:, i % 2], in0=stg[:, i % 3],
                                                           in1=ustg[:, i % 2], op=AL.mult),
                  reads=[Rr(stg, i % 3), Rr(ustg, i % 2)], writes=[Rr(ustg, i % 2)])
            sc.dma("sync", lambda s, hc=hc, tt=tt, i=i: s.dma_start(
                out=gu_dm.rearrange("(c p) (t f) -> p c t f", p=128, f=512)[:, hc, tt],
                in_=ustg[:, i % 2]),
                reads=[Rr(ustg, i % 2)], writes=[Rr(gu_dm, (hc, tt))], slot=Rr(ustg, i % 2))

    for tt in range(NTT):
        sc.dma("sync", lambda s, tt=tt: s.dma_start(
            out=gut[:], in_=gu_dm.rearrange("(h p) (t f) -> p h t f", p=128, f=512)[:, :, tt]),
            reads=[Rr(gu_dm, (hc, tt)) for hc in range(HC)], writes=[Rr(gut)])
        for hc in range(HC):
            sc.op("tensor", lambda t, hc=hc: t.matmul(ps_s, ones_cb[:], gut[:, hc],
                                                      start=(hc == 0), stop=(hc == HC - 1)),
                  reads=[Rr(gut), Rr(ones_cb)], writes=[rs_s])
        sc.op("scalar", lambda s, tt=tt: s.activation(out=gsum[:], in_=ps_s,
                                                      func=AF_.Copy),
              reads=[rs_s], writes=[Rr(gsum)])
        for oc in range(16):
            i = tt * 16 + oc
            sc.dma("gpsimd", lambda g, oc=oc, i=i: g.dma_start(
                out=wdl[:, i % 2],
                in_=wqdT[oc].rearrange("p (k f) -> p k f", f=128)),
                writes=[Rr(wdl, ("g", i % 2)), Rr(wdl, ("u", i % 2))])
            pm = p_mm[i % 4]
            for hc in range(HC):
                sc.op("tensor", lambda t, hc=hc, i=i, pm=pm: t.matmul(
                    pm[:], wdl[:, i % 2, hc], gut[:, hc],
                    start=(hc == 0), stop=False),
                    reads=[Rr(wdl, ("g", i % 2)), Rr(wdl, ("u", i % 2)), Rr(gut)],
                    writes=[Rr(pm)])
            sc.op("tensor", lambda t, tt=tt, pm=pm: t.matmul(
                pm[:], neg128[:], gsum[:], start=False, stop=True),
                reads=[Rr(neg128), Rr(gsum)], writes=[Rr(pm)])
            sc.op("scalar", lambda s, oc=oc, pm=pm: s.activation(
                out=xsm[:, oc], in_=pm[:], func=AF_.Copy, scale=sd_c[:, oc:oc + 1]),
                reads=[Rr(pm), Rr(sd_c)], writes=[Rr(xsm, oc)])
            sc.op("scalar", lambda s, oc=oc, i=i, pm=pm: s.activation(
                out=osq[:, i % 2], in_=pm[:], func=AF_.Square, scale=sd_c[:, oc:oc + 1]),
                reads=[Rr(pm), Rr(sd_c)], writes=[Rr(osq, i % 2)])
            sc.op("tensor", lambda t, oc=oc: t.matmul(ps_s, ones_cb[:], xsm[:, oc],
                                                      start=(oc == 0), stop=(oc == 15)),
                  reads=[Rr(xsm, oc), Rr(ones_cb)], writes=[rs_s])
            sc.op("tensor", lambda t, oc=oc, i=i: t.matmul(ps_q, ones_c[:], osq[:, i % 2],
                                                           start=(oc == 0), stop=(oc == 15)),
                  reads=[Rr(osq, i % 2), Rr(ones_c)], writes=[rs_q])
        sl1 = finalize(True, (ps_s, rs_s), (ps_q, rs_q), LNEPS)
        for c in range(DC):
            i = tt * DC + c
            sc.dma("sync", lambda s, tt=tt, c=c, i=i: s.dma_start(
                out=xc[:, i % 2],
                in_=x1_dm.rearrange("(c p) (t f) -> p c t f", p=128, f=512)[:, c, tt]),
                reads=[Rr(x1_dm, (c, tt))], writes=[Rr(xc, i % 2)])
            sc.op("vector", lambda v, c=c, i=i, sl1=sl1: v.tensor_tensor(
                out=asq[:, i % 2], in0=xsm[:, c], in1=rbc[:, sl1], op=AL.mult),
                reads=[Rr(xsm, c), Rr(rbc, sl1)], writes=[Rr(asq, i % 2)])
            sc.op("gpsimd", lambda g, i=i, sl1=sl1: g.tensor_tensor(
                out=t1b[:, i % 2], in0=asq[:, i % 2], in1=mrbc[:, sl1], op=AL.add),
                reads=[Rr(asq, i % 2), Rr(mrbc, sl1)], writes=[Rr(t1b, i % 2)])
            sc.op("vector", lambda v, i=i: v.tensor_tensor(out=xc[:, i % 2], in0=xc[:, i % 2],
                                                           in1=t1b[:, i % 2], op=AL.add),
                  reads=[Rr(xc, i % 2), Rr(t1b, i % 2)], writes=[Rr(xc, i % 2)])
            sc.dma("sync", lambda s, tt=tt, c=c, i=i: s.dma_start(
                out=x2_out.rearrange("(c p) (t f) -> p c t f", p=128, f=512)[:, c, tt],
                in_=xc[:, i % 2]),
                reads=[Rr(xc, i % 2)], writes=[Rr(x2_out, (c, tt))], slot=Rr(xc, i % 2))
            sc.op("scalar", lambda s, i=i: s.activation(out=osq[:, i % 2], in_=xc[:, i % 2],
                                                        func=AF_.Square),
                  reads=[Rr(xc, i % 2)], writes=[Rr(osq, i % 2)])
            sc.op("tensor", lambda t, c=c, i=i: t.matmul(ps_s, ones_c[:], xc[:, i % 2],
                                                         start=(c == 0), stop=(c == DC - 1)),
                  reads=[Rr(xc, i % 2), Rr(ones_c)], writes=[rs_s])
            sc.op("tensor", lambda t, c=c, i=i: t.matmul(ps_q, ones_c[:], osq[:, i % 2],
                                                         start=(c == 0), stop=(c == DC - 1)),
                  reads=[Rr(osq, i % 2), Rr(ones_c)], writes=[rs_q])
        sl2 = finalize(True, (ps_s, rs_s), (ps_q, rs_q), LNEPS)
        for c in range(DC):
            i = tt * DC + c
            sc.dma("sync", lambda s, tt=tt, c=c, i=i: s.dma_start(
                out=xc[:, i % 2],
                in_=x2_out.rearrange("(c p) (t f) -> p c t f", p=128, f=512)[:, c, tt]),
                reads=[Rr(x2_out, (c, tt))], writes=[Rr(xc, i % 2)])
            sc.op("gpsimd", lambda g, i=i, sl2=sl2: g.tensor_tensor(
                out=t1b[:, i % 2], in0=xc[:, i % 2], in1=rbc[:, sl2], op=AL.mult),
                reads=[Rr(xc, i % 2), Rr(rbc, sl2)], writes=[Rr(t1b, i % 2)])
            sc.op("vector", lambda v, tt=tt, c=c, i=i, sl2=sl2: v.tensor_tensor(
                out=A[:, c, tt * 512:(tt + 1) * 512], in0=t1b[:, i % 2], in1=mrbc[:, sl2],
                op=AL.add),
                reads=[Rr(t1b, i % 2), Rr(mrbc, sl2)], writes=[Rr(A, (c, tt))])
            sc.dma("sync", lambda s, tt=tt, c=c: s.dma_start(
                out=xn2_out.rearrange("(c p) (t f) -> p c t f", p=128, f=512)[:, c, tt],
                in_=A[:, c, tt * 512:(tt + 1) * 512]),
                reads=[Rr(A, (c, tt))], writes=[Rr(xn2_out, (c, tt))], slot=Rr(A, "st"))

    for oc in range(16):
        sc.dma("gpsimd", lambda g, oc=oc: g.dma_start(
            out=wsl[:, oc % 2],
            in_=dtwT[oc].rearrange("p (k f) -> p k f", f=128)),
            writes=[Rr(wdl, ("g", oc % 2))])
        for tt in range(NTT):
            i = oc * NTT + tt
            pm = p_mm[i % 4]
            for c in range(DC):
                sc.op("tensor", lambda t, oc=oc, tt=tt, c=c, pm=pm: t.matmul(
                    pm[:], wsl[:, oc % 2, c], A[:, c, tt * 512:(tt + 1) * 512],
                    start=(c == 0), stop=(c == DC - 1)),
                    reads=[Rr(wdl, ("g", oc % 2)), Rr(A, (c, tt))], writes=[Rr(pm)])
            sc.op("scalar", lambda s, oc=oc, i=i, pm=pm: s.activation(
                out=osq[:, i % 2], in_=pm[:], func=AF_.Relu, bias=db_c[:, oc:oc + 1]),
                reads=[Rr(pm), Rr(db_c)], writes=[Rr(osq, i % 2)])
            sc.op("scalar", lambda s, oc=oc, i=i, pm=pm: s.activation(
                out=asq[:, i % 2], in_=pm[:], func=AF_.Abs, bias=db_c[:, oc:oc + 1]),
                reads=[Rr(pm), Rr(db_c)], writes=[Rr(asq, i % 2)])
            sc.op("scalar", lambda s, i=i: s.activation(out=asq[:, i % 2], in_=asq[:, i % 2],
                                                        func=AF_.Exp, scale=-1.0),
                  reads=[Rr(asq, i % 2)], writes=[Rr(asq, i % 2)])
            sc.op("scalar", lambda s, i=i: s.activation(out=asq[:, i % 2], in_=asq[:, i % 2],
                                                        func=AF_.Ln, bias=1.0),
                  reads=[Rr(asq, i % 2)], writes=[Rr(asq, i % 2)])
            sc.op("gpsimd", lambda g, i=i: g.tensor_tensor(out=asq[:, i % 2], in0=osq[:, i % 2],
                                                           in1=asq[:, i % 2], op=AL.add),
                  reads=[Rr(osq, i % 2), Rr(asq, i % 2)], writes=[Rr(asq, i % 2)])
            sc.dma("sync", lambda s, oc=oc, tt=tt, i=i: s.dma_start(
                out=dt_out.rearrange("(c p) (t f) -> p c t f", p=128, f=512)[:, oc, tt],
                in_=asq[:, i % 2]),
                reads=[Rr(asq, i % 2)], writes=[Rr(dt_out, (oc, tt))], slot=Rr(asq, i % 2))
    sc.dma("gpsimd", lambda g: g.dma_start(
        out=wxp[:], in_=xpwT[0].rearrange("p (k f) -> p k f", f=32)),
        writes=[Rr(wdl, ("u", 0))])
    for tt in range(NTT):
        for c in range(DC):
            sc.op("tensor", lambda t, tt=tt, c=c: t.matmul(
                p_t[0:32, 0:512], wxp[:, c], A[:, c, tt * 512:(tt + 1) * 512],
                start=(c == 0), stop=(c == DC - 1)),
                reads=[Rr(wdl, ("u", 0)), Rr(A, (c, tt))], writes=[Rr(p_t)])
        sc.op("vector", lambda v, tt=tt: v.tensor_scalar(
            out=ustg[0:32, 1], in0=p_t[0:32, 0:512], scalar1=xpb_c[:, 0:1], scalar2=None,
            op0=AL.add),
            reads=[Rr(p_t), Rr(xpb_c)], writes=[Rr(ustg, 1)])
        sc.dma("sync", lambda s, tt=tt: s.dma_start(out=bc_out[:, tt * 512:(tt + 1) * 512],
                                                    in_=ustg[0:32, 1]),
               reads=[Rr(ustg, 1)], writes=[Rr(bc_out, tt)], slot=Rr(ustg, (1, "bst")))


def build_p3():
    p = P("p3")
    o_in = p.inp("o_dm", [D, TOK], BF16)
    sc = Sched(p)
    io = dict(
        o_ap=lambda c, tt: o_in.rearrange("(c p) (t f) -> p c t f", p=128, f=512)[:, c, tt],
        x_dm=p.inp("x_dm", [D, TOK]),
        Wo=p.inp("Wo", [16, 128, DC * 128]),
        norm_w=p.inp("norm_w", [D]),
        wqgT=p.inp("wqgT", [HC, 128, DC * 128], I32),
        wquT=p.inp("wquT", [HC, 128, DC * 128], I32),
        wqdT=p.inp("wqdT", [16, 128, HC * 128], I32),
        s_gate=p.inp("s_gate", [HID]), s_up=p.inp("s_up", [HID]), s_down=p.inp("s_down", [D]),
        dtwT=p.inp("dtwT", [16, 128, DC * 128]), dtb=p.inp("dtb", [D]),
        xpwT=p.inp("xpwT", [1, 128, DC * 32]), xpb=p.inp("xpb", [32]),
        x2_dm=p.out("x2_dm", [D, TOK]),
        xn2_dm=p.out("xn2_dm", [D, TOK], BF16),
        dt_dm=p.out("dt_dm", [D, TOK]),
        bc_dm=p.out("bc_dm", [32, TOK]),
    )
    emit_p3(p, sc, io)
    sc.emit()
    return p.done()


def host_p3_inputs(p2res, inputs, core):
    b, hh = core // 2, core % 2
    sl = slice(hh * TOK, (hh + 1) * TOK)
    o_full = np.concatenate([p2res[2 * b]["o_dm"], p2res[2 * b + 1]["o_dm"]], axis=0)
    return {
        "o_dm": np.ascontiguousarray(o_full[:, sl]),
        "x_dm": _dm(inputs["L0_attn_x"][b, sl]),
        "Wo": _pk(inputs["L0_attn_Wo"]),
        "norm_w": np.asarray(inputs["L0_attn_norm_w"]),
        "wqgT": _pk(np.asarray(inputs["L0_ffn_wq_gate"]).T),
        "wquT": _pk(np.asarray(inputs["L0_ffn_wq_up"]).T),
        "wqdT": _pk(np.asarray(inputs["L0_ffn_wq_down"]).T),
        "s_gate": np.asarray(inputs["L0_ffn_s_gate"]),
        "s_up": np.asarray(inputs["L0_ffn_s_up"]),
        "s_down": np.asarray(inputs["L0_ffn_s_down"]),
        "dtwT": _pk(np.asarray(inputs["L1_attn_dt_proj_w"]).T),
        "dtb": np.asarray(inputs["L1_attn_dt_proj_b"]),
        "xpwT": _pk(np.asarray(inputs["L1_attn_x_proj_w"]).T, 32),
        "xpb": np.asarray(inputs["L1_attn_x_proj_b"]),
    }


def emit_p4(p, sc, io):
    nc = p.nc
    dt_ap = io["dt_ap"]; xn_ap = io["xn_ap"]; bc_ap = io["bc_ap"]
    al_in = io["A_sl"]
    y_out = io["y4"]

    dt_t = p.sb("dt_t", [128, 2, 2, 512], F32)
    x_t = p.sb("x_t", [128, 2, 2, 512], BF16)
    bcr = p.sb("bcr", [1, 2, 32, 512], BF16)
    dtx = p.sb("dtx", [128, 2, 2, 512], F32)
    dA = p.sb("dA", [128, 2, 512], F32)
    u_sb = p.sb("u_sb", [128, 2, 512], F32)
    h_sb = p.sb("h_sb", [128, 2, 512], F32)
    c_sb = p.sb("c_sb", [128, 2, 512], F32)
    tmp = p.sb("tmp", [128, 2, 2, 512], F32)
    y_st = p.sb("y_st", [128, 2, 512], BF16)
    hcol = p.sb("hcol", [128, 8, 16], F32)
    A_c = p.sb("A_c", [128, 2, 16], F32)
    ones_rb = p.sb("ones_rb", [1, 128], BF16)
    idf32 = p.sb("idf32", [128, 128], F32)

    p_b = [p.ps("p_b%d" % k, [128, 512], F32) for k in range(2)]
    p_c = [p.ps("p_c%d" % k, [128, 512], F32) for k in range(2)]
    p_y = [p.ps("p_y%d" % k, [128, 512], F32) for k in range(4)]

    sc.exclusive(*p_b, *p_c, *p_y)
    AF_, AL = AF, ALU

    sc.op("vector", lambda v: v.memset(ones_rb[:], 1.0), writes=[Rr(ones_rb)])
    sc.op("vector", lambda v: v.memset(hcol[:], 0.0), writes=[Rr(hcol)])
    sc.op("gpsimd", lambda g: g.memset(idf32[:], 1.0), writes=[Rr(idf32, "ms")])
    sc.op("gpsimd", lambda g: g.affine_select(out=idf32[:], in_=idf32[:], pattern=[[-1, 128]],
                                              compare_op=ALU.is_equal, fill=0.0,
                                              base=0, channel_multiplier=1),
          reads=[Rr(idf32, "ms")], writes=[Rr(idf32)])
    sc.dma("sync", lambda s: s.dma_start(out=A_c[:], in_=al_in.rearrange("s p n -> p s n")),
           writes=[Rr(A_c)])
    sc.op("scalar", lambda s: s.activation(out=A_c.rearrange("p s n -> p (s n)"),
                                           in_=A_c.rearrange("p s n -> p (s n)"), func=AF_.Exp),
          reads=[Rr(A_c)], writes=[Rr(A_c)])
    sc.op("vector", lambda v: v.tensor_scalar(out=A_c.rearrange("p s n -> p (s n)"),
                                              in0=A_c.rearrange("p s n -> p (s n)"),
                                              scalar1=-1.0, scalar2=None, op0=AL.mult),
          reads=[Rr(A_c)], writes=[Rr(A_c)])

    for b in range(4):
        for tc in range(8):
            it = b * 8 + tc
            sl = slice(tc * 512, (tc + 1) * 512)
            for ds in range(2):
                sc.dma("sync", lambda s, b=b, ds=ds, tc=tc, it=it: s.dma_start(
                    out=dt_t[:, ds, it % 2], in_=dt_ap(b, ds, tc)),
                    reads=[Rr(dt_ap, (b, ds, tc))], writes=[Rr(dt_t, (ds, it % 2))])
                sc.dma("sync", lambda s, b=b, ds=ds, tc=tc, it=it: s.dma_start(
                    out=x_t[:, ds, it % 2], in_=xn_ap(b, ds, tc)),
                    reads=[Rr(xn_ap, (b, ds, tc))], writes=[Rr(x_t, (ds, it % 2))])
            sc.dma("sync", lambda s, b=b, tc=tc, it=it: s.dma_start(
                out=bcr[:, it % 2], in_=bc_ap(b, tc)),
                reads=[Rr(bc_ap, (b, tc))], writes=[Rr(bcr, it % 2)])
            for ds in range(2):
                sc.op("vector", lambda v, ds=ds, it=it: v.tensor_tensor(
                    out=dtx[:, ds, it % 2], in0=dt_t[:, ds, it % 2],
                    in1=x_t[:, ds, it % 2], op=AL.mult),
                    reads=[Rr(dt_t, (ds, it % 2)), Rr(x_t, (ds, it % 2))],
                    writes=[Rr(dtx, (ds, it % 2))])
            for n in range(16):
                j = it * 16 + n
                pb = p_b[n % 2]
                pc = p_c[n % 2]
                sc.op("tensor", lambda t, n=n, it=it, pb=pb: t.matmul(
                    pb[:], ones_rb[:], bcr[0:1, it % 2, n], start=True, stop=True),
                    reads=[Rr(bcr, it % 2), Rr(ones_rb)], writes=[Rr(pb)])
                sc.op("tensor", lambda t, n=n, it=it, pc=pc: t.matmul(
                    pc[:], ones_rb[:], bcr[0:1, it % 2, 16 + n], start=True, stop=True),
                    reads=[Rr(bcr, it % 2), Rr(ones_rb)], writes=[Rr(pc)])
                sc.op("scalar", lambda s, n=n, pc=pc: s.activation(
                    out=c_sb[:, n % 2], in_=pc[:], func=AF_.Copy),
                    reads=[Rr(pc)], writes=[Rr(c_sb, n % 2)])
                for ds in range(2):
                    g = b * 2 + ds
                    jj = j * 2 + ds
                    sc.op("scalar", lambda s, ds=ds, n=n, it=it, jj=jj: s.activation(
                        out=dA[:, jj % 2], in_=dt_t[:, ds, it % 2], func=AF_.Exp,
                        scale=A_c[:, ds, n:n + 1]),
                        reads=[Rr(dt_t, (ds, it % 2)), Rr(A_c)], writes=[Rr(dA, jj % 2)])
                    sc.op("vector", lambda v, ds=ds, it=it, jj=jj, pb=pb: v.tensor_tensor(
                        out=u_sb[:, jj % 2], in0=dtx[:, ds, it % 2], in1=pb[:], op=AL.mult),
                        reads=[Rr(dtx, (ds, it % 2)), Rr(pb)], writes=[Rr(u_sb, jj % 2)])
                    sc.op("vector", lambda v, g=g, n=n, jj=jj: v.tensor_tensor_scan(
                        out=h_sb[:, jj % 2], data0=dA[:, jj % 2], data1=u_sb[:, jj % 2],
                        initial=hcol[:, g, n:n + 1], op0=AL.mult, op1=AL.add),
                        reads=[Rr(dA, jj % 2), Rr(u_sb, jj % 2), Rr(hcol, (g, n))],
                        writes=[Rr(h_sb, jj % 2)])
                    sc.op("scalar", lambda s, g=g, n=n, jj=jj: s.activation(
                        out=hcol[:, g, n:n + 1], in_=h_sb[:, jj % 2, 511:512], func=AF_.Copy),
                        reads=[Rr(h_sb, jj % 2)], writes=[Rr(hcol, (g, n))])
                    sc.op("gpsimd", lambda gp, ds=ds, n=n, jj=jj: gp.tensor_tensor(
                        out=tmp[:, ds, n % 2], in0=h_sb[:, jj % 2], in1=c_sb[:, n % 2],
                        op=AL.mult),
                        reads=[Rr(h_sb, jj % 2), Rr(c_sb, n % 2)], writes=[Rr(tmp, (ds, n % 2))])
                    py = p_y[ds * 2 + it % 2]
                    sc.op("tensor", lambda t, ds=ds, n=n, py=py: t.matmul(
                        py[:], idf32[:], tmp[:, ds, n % 2], start=(n == 0), stop=(n == 15)),
                        reads=[Rr(idf32), Rr(tmp, (ds, n % 2))], writes=[Rr(py)])
            for ds in range(2):
                py = p_y[ds * 2 + it % 2]
                sc.op("scalar", lambda s, ds=ds, py=py: s.activation(
                    out=y_st[:, ds], in_=py[:], func=AF_.Copy),
                    reads=[Rr(py)], writes=[Rr(y_st, ds)])
                sc.dma("gpsimd", lambda gq, b=b, ds=ds, sl=sl, it=it: gq.dma_start(
                    out=y_out[b, ds * 128:(ds + 1) * 128, sl], in_=y_st[:, ds]),
                    reads=[Rr(y_st, ds)], writes=[Rr(y_out, (b, ds, it))], slot=Rr(y_st, (ds, "st")))


def build_p4():
    p = P("p4")
    dt_in = p.inp("dt4", [4, 256, S])
    x_in = p.inp("xn4", [4, 256, S], BF16)
    bc_in = p.inp("bc4", [4, 32, S])
    sc = Sched(p)
    io = dict(
        dt_ap=lambda b, ds, tc: dt_in[b, ds * 128:(ds + 1) * 128, tc * 512:(tc + 1) * 512],
        xn_ap=lambda b, ds, tc: x_in[b, ds * 128:(ds + 1) * 128, tc * 512:(tc + 1) * 512],
        bc_ap=lambda b, tc: bc_in[b:b + 1, :, tc * 512:(tc + 1) * 512],
        A_sl=p.inp("A_sl", [2, 128, 16]),
        y4=p.out("y4", [4, 256, S], BF16),
    )
    emit_p4(p, sc, io)
    sc.emit()
    return p.done()


def host_p4_inputs(p3res, inputs, core):
    dsl = slice(core * 256, (core + 1) * 256)
    dt = np.stack([np.concatenate([p3res[2 * b]["dt_dm"], p3res[2 * b + 1]["dt_dm"]], axis=1)[dsl]
                   for b in range(4)])
    xn = np.stack([np.concatenate([p3res[2 * b]["xn2_dm"], p3res[2 * b + 1]["xn2_dm"]], axis=1)[dsl]
                   for b in range(4)])
    bc = np.stack([np.concatenate([p3res[2 * b]["bc_dm"], p3res[2 * b + 1]["bc_dm"]], axis=1)
                   for b in range(4)])
    al = np.asarray(inputs["L1_attn_A_log"])[dsl].reshape(2, 128, 16)
    return {"dt4": np.ascontiguousarray(dt), "xn4": np.ascontiguousarray(xn),
            "bc4": np.ascontiguousarray(bc), "A_sl": np.ascontiguousarray(al)}


LAT = 512
LC = LAT // 128


def emit_p5(p, sc, io):
    nc = p.nc
    y_ap = io["y_ap"]
    x2_in = io["x2_dm"]
    opwT = io["opwT"]; opb_in = io["opb"]; ln1_in = io["ln1_w"]
    kvdw = io["kv_d_w"]; qdw = io["q_d_w"]; ropekw = io["rope_k_w"]
    kuw = io["k_u_w"]; quw = io["q_u_w"]; vuw = io["v_u_w"]; ropeqw = io["rope_q_w"]
    cos_in = io["cos_t"]; sin_in = io["sin_t"]; rt_in = io["RT"]
    x3_out = io["x3_dm"]; xin_out = io["xin_dm"]
    q5_out = io["q5"]; k5_out = io["k5"]; v5_out = io["v5"]

    o5_dm = p.dram("o5_dram", [D, TOK], BF16)

    A5 = p.sb("A5", [128, DC, TOK], BF16)
    kvd = p.sb("kvd", [128, LC, TOK], BF16)
    qdb = p.sb("qdb", [128, LC, TOK], BF16)
    xc = p.sb("xc", [128, 2, 512], F32)
    osq = p.sb("osq", [128, 2, 512], F32)
    asq = p.sb("asq", [128, 2, 512], F32)
    stg = p.sb("stg", [128, 2, 512], BF16)
    stg2 = p.sb("stg2", [128, 2, 512], BF16)
    t1s = p.sb("t1s", [128, 2, 512], F32)
    t2s = p.sb("t2s", [128, 2, 512], F32)
    wsl = p.sb("wsl", [128, 2, DC, 128], BF16)
    wvs = p.sb("wvs", [128, 2, LC, 512], BF16)
    cosb = p.sb("cosb", [128, TOK], F32)
    sinb = p.sb("sinb", [128, TOK], F32)
    rtb = p.sb("rtb", [128, 128], BF16)
    srow = p.sb("srow", [1, 6, 512], F32)
    accS = p.sb("accS", [1, NTT, 512], F32)
    accQ = p.sb("accQ", [1, NTT, 512], F32)
    vrow = p.sb("vrow", [1, 1024], F32)
    opb_c = p.sb("opb_c", [128, DC], F32)
    ln1_c = p.sb("ln1_c", [128, DC], F32)
    ones_c = p.sb("ones_c", [128, 1], F32)
    ones_cb = p.sb("ones_cb", [128, 1], BF16)
    ones_r = p.sb("ones_r", [1, 128], F32)
    identf = p.sb("identf", [1, 1], F32)

    p_mm = [p.ps("p_mm0", [128, 512], F32), p.ps("p_mm1", [128, 512], F32)]
    p_rot = p.ps("p_rot", [128, 512], F32)
    p_s = p.ps("p_s", [1, 512], F32)
    p_q = p.ps("p_q", [1, 512], F32)
    p_r = p.ps("p_r", [128, 512], F32)
    p_m = p.ps("p_m", [128, 512], F32)
    p_t = p.ps("p_t", [128, 512], F32)

    sc.exclusive(p_mm[0], p_mm[1], p_rot, p_s, p_q, p_r, p_m, p_t)
    AF_, AL = AF, ALU

    for nm, buf, val in [("oc", ones_c, 1.0), ("ocb", ones_cb, 1.0), ("or", ones_r, 1.0),
                         ("idf", identf, 1.0), ("aS", accS, 0.0), ("aQ", accQ, 0.0)]:
        sc.op("vector", lambda v, buf=buf, val=val: v.memset(buf[:], val), writes=[Rr(buf)])
    sc.dma("sync", lambda s: s.dma_start(out=cosb[:], in_=cos_in[:]), writes=[Rr(cosb)])
    sc.dma("sync", lambda s: s.dma_start(out=sinb[:], in_=sin_in[:]), writes=[Rr(sinb)])
    sc.dma("gpsimd", lambda g: g.dma_start(out=rtb[:], in_=rt_in[:]), writes=[Rr(rtb)])
    for c in range(DC):
        sc.dma("gpsimd", lambda g, c=c: g.dma_start(
            out=A5[:, c], in_=y_ap(c)),
            reads=[Rr(y_ap, c)], writes=[Rr(A5, c)], slot=Rr(A5, "ld"))

    def vec_cols(src, n, dst):
        for seg in range(0, n, 1024):
            w = min(1024, n - seg)
            sc.dma("sync", lambda s, src=src, seg=seg, w=w: s.dma_start(
                out=vrow[0:1, 0:w], in_=src.unsqueeze(0)[:, seg:seg + w]),
                reads=[], writes=[Rr(vrow)])
            for cc in range(w // 128):
                c = seg // 128 + cc
                sc.op("tensor", lambda t, cc=cc: t.transpose(p_t[:, 0:1],
                                                             vrow[0:1, cc * 128:(cc + 1) * 128], identf[:]),
                      reads=[Rr(vrow), Rr(identf)], writes=[Rr(p_t)])
                sc.op("vector", lambda v, c=c, dst=dst: v.tensor_copy(out=dst[:, c:c + 1], in_=p_t[:, 0:1]),
                      reads=[Rr(p_t)], writes=[Rr(dst)])

    vec_cols(opb_in, D, opb_c)
    vec_cols(ln1_in, D, ln1_c)

    def finalize(mean, aps_s, aps_q, eps):
        if mean:
            sc.op("vector", lambda v: v.tensor_scalar(out=srow[:, 0], in0=aps_s[0], scalar1=1.0 / D,
                                                      scalar2=None, op0=AL.mult),
                  reads=[aps_s[1]], writes=[Rr(srow, 0)])
            sc.op("vector", lambda v: v.tensor_scalar(out=srow[:, 1], in0=aps_q[0], scalar1=1.0 / D,
                                                      scalar2=None, op0=AL.mult),
                  reads=[aps_q[1]], writes=[Rr(srow, 1)])
            sc.op("vector", lambda v: v.tensor_tensor(out=srow[:, 2], in0=srow[:, 0], in1=srow[:, 0],
                                                      op=AL.mult),
                  reads=[Rr(srow, 0)], writes=[Rr(srow, 2)])
            sc.op("vector", lambda v: v.tensor_tensor(out=srow[:, 2], in0=srow[:, 1], in1=srow[:, 2],
                                                      op=AL.subtract),
                  reads=[Rr(srow, 1), Rr(srow, 2)], writes=[Rr(srow, 2)])
            sc.op("vector", lambda v: v.tensor_scalar(out=srow[:, 2], in0=srow[:, 2], scalar1=eps,
                                                      scalar2=None, op0=AL.add),
                  reads=[Rr(srow, 2)], writes=[Rr(srow, 2)])
        sc.op("scalar", lambda s: s.activation(out=srow[:, 3], in_=srow[:, 2], func=AF_.Ln),
              reads=[Rr(srow, 2)], writes=[Rr(srow, 3)])
        sc.op("scalar", lambda s: s.activation(out=srow[:, 3], in_=srow[:, 3], func=AF_.Exp, scale=-0.5),
              reads=[Rr(srow, 3)], writes=[Rr(srow, 3)])
        if mean:
            sc.op("vector", lambda v: v.tensor_scalar(out=srow[:, 4], in0=srow[:, 0], scalar1=-1.0,
                                                      scalar2=None, op0=AL.mult),
                  reads=[Rr(srow, 0)], writes=[Rr(srow, 4)])
        sc.op("tensor", lambda t: t.matmul(p_r[:], ones_r[:], srow[:, 3], start=True, stop=True),
              reads=[Rr(srow, 3), Rr(ones_r)], writes=[Rr(p_r)])
        if mean:
            sc.op("tensor", lambda t: t.matmul(p_m[:], ones_r[:], srow[:, 4], start=True, stop=True),
                  reads=[Rr(srow, 4), Rr(ones_r)], writes=[Rr(p_m)])

    for oc in range(16):
        sc.dma("gpsimd", lambda g, oc=oc: g.dma_start(
            out=wsl[:, oc % 2],
            in_=opwT[oc].rearrange("p (k f) -> p k f", f=128)),
            writes=[Rr(wsl, oc % 2)])
        for tt in range(NTT):
            i = oc * NTT + tt
            for c in range(DC):
                sc.op("tensor", lambda t, oc=oc, tt=tt, c=c, i=i: t.matmul(
                    p_mm[i % 2][:], wsl[:, oc % 2, c], A5[:, c, tt * 512:(tt + 1) * 512],
                    start=(c == 0), stop=(c == DC - 1)),
                    reads=[Rr(wsl, oc % 2), Rr(A5, c)], writes=[Rr(p_mm[i % 2])])
            sc.op("vector", lambda v, oc=oc, i=i: v.tensor_scalar(
                out=stg[:, i % 2], in0=p_mm[i % 2][:], scalar1=opb_c[:, oc:oc + 1], scalar2=None,
                op0=AL.add),
                reads=[Rr(p_mm[i % 2]), Rr(opb_c)], writes=[Rr(stg, i % 2)])
            sc.op("scalar", lambda s, oc=oc, i=i: s.activation(
                out=osq[:, i % 2], in_=p_mm[i % 2][:], func=AF_.Square, bias=opb_c[:, oc:oc + 1]),
                reads=[Rr(p_mm[i % 2]), Rr(opb_c)], writes=[Rr(osq, i % 2)])
            sc.op("tensor", lambda t, i=i: t.matmul(p_s[:], ones_cb[:], stg[:, i % 2], start=True, stop=True),
                  reads=[Rr(stg, i % 2), Rr(ones_cb)], writes=[Rr(p_s)])
            sc.op("tensor", lambda t, i=i: t.matmul(p_q[:], ones_c[:], osq[:, i % 2], start=True, stop=True),
                  reads=[Rr(osq, i % 2), Rr(ones_c)], writes=[Rr(p_q)])
            sc.op("vector", lambda v, tt=tt: v.tensor_tensor(out=accS[:, tt], in0=accS[:, tt],
                                                             in1=p_s[:], op=AL.add),
                  reads=[Rr(p_s), Rr(accS)], writes=[Rr(accS)])
            sc.op("vector", lambda v, tt=tt: v.tensor_tensor(out=accQ[:, tt], in0=accQ[:, tt],
                                                             in1=p_q[:], op=AL.add),
                  reads=[Rr(p_q), Rr(accQ)], writes=[Rr(accQ)])
            sc.dma("sync", lambda s, oc=oc, tt=tt, i=i: s.dma_start(
                out=o5_dm.rearrange("(c p) (t f) -> p c t f", p=128, f=512)[:, oc, tt], in_=stg[:, i % 2]),
                reads=[Rr(stg, i % 2)], writes=[Rr(o5_dm, (oc, tt))], slot=Rr(stg, i % 2))

    for tt in range(NTT):
        finalize(True, (accS[:, tt], Rr(accS)), (accQ[:, tt], Rr(accQ)), LNEPS)
        for c in range(DC):
            i = tt * DC + c
            sc.dma("sync", lambda s, tt=tt, c=c, i=i: s.dma_start(
                out=xc[:, i % 2],
                in_=x2_in.rearrange("(c p) (t f) -> p c t f", p=128, f=512)[:, c, tt]),
                writes=[Rr(xc, i % 2)])
            sc.dma("gpsimd", lambda g, tt=tt, c=c, i=i: g.dma_start(
                out=stg[:, i % 2],
                in_=o5_dm.rearrange("(c p) (t f) -> p c t f", p=128, f=512)[:, c, tt]),
                reads=[Rr(o5_dm, (c, tt))], writes=[Rr(stg, i % 2)])
            sc.op("vector", lambda v, i=i: v.tensor_tensor(out=osq[:, i % 2], in0=stg[:, i % 2],
                                                           in1=p_m[:], op=AL.add),
                  reads=[Rr(stg, i % 2), Rr(p_m)], writes=[Rr(osq, i % 2)])
            sc.op("vector", lambda v, i=i: v.tensor_tensor(out=osq[:, i % 2], in0=osq[:, i % 2],
                                                           in1=p_r[:], op=AL.mult),
                  reads=[Rr(osq, i % 2), Rr(p_r)], writes=[Rr(osq, i % 2)])
            sc.op("vector", lambda v, i=i: v.tensor_tensor(out=xc[:, i % 2], in0=xc[:, i % 2],
                                                           in1=osq[:, i % 2], op=AL.add),
                  reads=[Rr(xc, i % 2), Rr(osq, i % 2)], writes=[Rr(xc, i % 2)])
            sc.dma("sync", lambda s, tt=tt, c=c, i=i: s.dma_start(
                out=x3_out.rearrange("(c p) (t f) -> p c t f", p=128, f=512)[:, c, tt], in_=xc[:, i % 2]),
                reads=[Rr(xc, i % 2)], writes=[Rr(x3_out, (c, tt))], slot=Rr(xc, i % 2))
            sc.op("scalar", lambda s, i=i: s.activation(out=asq[:, i % 2], in_=xc[:, i % 2], func=AF_.Square),
                  reads=[Rr(xc, i % 2)], writes=[Rr(asq, i % 2)])
            sc.op("tensor", lambda t, c=c, i=i: t.matmul(p_s[:], ones_c[:], xc[:, i % 2],
                                                         start=(c == 0), stop=(c == DC - 1)),
                  reads=[Rr(xc, i % 2), Rr(ones_c)], writes=[Rr(p_s)])
            sc.op("tensor", lambda t, c=c, i=i: t.matmul(p_q[:], ones_c[:], asq[:, i % 2],
                                                         start=(c == 0), stop=(c == DC - 1)),
                  reads=[Rr(asq, i % 2), Rr(ones_c)], writes=[Rr(p_q)])
        finalize(True, (p_s[:], Rr(p_s)), (p_q[:], Rr(p_q)), LNEPS)
        sc.op("vector", lambda v: v.reciprocal(out=srow[:, 5], in_=srow[:, 2]),
              reads=[Rr(srow, 2)], writes=[Rr(srow, 5)])
        sc.op("vector", lambda v: v.tensor_scalar(out=srow[:, 5], in0=srow[:, 5], scalar1=-LNEPS,
                                                  scalar2=1.0 + 1e-5, op0=AL.mult, op1=AL.add),
              reads=[Rr(srow, 5)], writes=[Rr(srow, 5)])
        sc.op("scalar", lambda s: s.activation(out=srow[:, 5], in_=srow[:, 5], func=AF_.Ln),
              reads=[Rr(srow, 5)], writes=[Rr(srow, 5)])
        sc.op("scalar", lambda s: s.activation(out=srow[:, 5], in_=srow[:, 5], func=AF_.Exp, scale=-0.5),
              reads=[Rr(srow, 5)], writes=[Rr(srow, 5)])
        sc.op("vector", lambda v: v.tensor_tensor(out=srow[:, 5], in0=srow[:, 5], in1=srow[:, 3],
                                                  op=AL.mult),
              reads=[Rr(srow, 5), Rr(srow, 3)], writes=[Rr(srow, 5)])
        sc.op("tensor", lambda t: t.matmul(p_t[:], ones_r[:], srow[:, 5], start=True, stop=True),
              reads=[Rr(srow, 5), Rr(ones_r)], writes=[Rr(p_t)])
        for c in range(DC):
            i = tt * DC + c
            sc.dma("sync", lambda s, tt=tt, c=c, i=i: s.dma_start(
                out=xc[:, i % 2],
                in_=x3_out.rearrange("(c p) (t f) -> p c t f", p=128, f=512)[:, c, tt]),
                reads=[Rr(x3_out, (c, tt))], writes=[Rr(xc, i % 2)])
            sc.op("vector", lambda v, i=i: v.tensor_tensor(out=osq[:, i % 2], in0=xc[:, i % 2],
                                                           in1=p_m[:], op=AL.add),
                  reads=[Rr(xc, i % 2), Rr(p_m)], writes=[Rr(osq, i % 2)])
            sc.op("vector", lambda v, i=i: v.tensor_tensor(out=t1s[:, i % 2], in0=osq[:, i % 2],
                                                           in1=p_r[:], op=AL.mult),
                  reads=[Rr(osq, i % 2), Rr(p_r)], writes=[Rr(t1s, i % 2)])
            sc.dma("sync", lambda s, tt=tt, c=c, i=i: s.dma_start(
                out=xin_out.rearrange("(c p) (t f) -> p c t f", p=128, f=512)[:, c, tt], in_=t1s[:, i % 2]),
                reads=[Rr(t1s, i % 2)], writes=[Rr(xin_out, (c, tt))], slot=Rr(t1s, i % 2))
            sc.op("vector", lambda v, i=i: v.tensor_tensor(out=t2s[:, i % 2], in0=osq[:, i % 2],
                                                           in1=p_t[:], op=AL.mult),
                  reads=[Rr(osq, i % 2), Rr(p_t)], writes=[Rr(t2s, i % 2)])
            sc.op("vector", lambda v, tt=tt, c=c, i=i: v.tensor_scalar(
                out=A5[:, c, tt * 512:(tt + 1) * 512], in0=t2s[:, i % 2],
                scalar1=ln1_c[:, c:c + 1], scalar2=None, op0=AL.mult),
                reads=[Rr(t2s, i % 2), Rr(ln1_c)], writes=[Rr(A5, (c, tt, "xa")), Rr(A5, c)])

    def A5r(c, tt):
        return Rr(A5, (c, tt, "xa"))

    for wi, (W, dst) in enumerate([(kvdw, kvd), (qdw, qdb)]):
        for ol in range(LC):
            oc = wi * LC + ol
            sc.dma("gpsimd", lambda g, W=W, ol=ol, oc=oc: g.dma_start(
                out=wsl[:, oc % 2],
                in_=W[ol].rearrange("p (k f) -> p k f", f=128)),
                writes=[Rr(wsl, oc % 2)])
            for tt in range(NTT):
                i = oc * NTT + tt
                for c in range(DC):
                    sc.op("tensor", lambda t, oc=oc, tt=tt, c=c, i=i: t.matmul(
                        p_mm[i % 2][:], wsl[:, oc % 2, c], A5[:, c, tt * 512:(tt + 1) * 512],
                        start=(c == 0), stop=(c == DC - 1)),
                        reads=[Rr(wsl, oc % 2), A5r(c, tt)], writes=[Rr(p_mm[i % 2])])
                sc.op("vector", lambda v, dst=dst, ol=ol, tt=tt, i=i: v.tensor_copy(
                    out=dst[:, ol, tt * 512:(tt + 1) * 512], in_=p_mm[i % 2][:]),
                    reads=[Rr(p_mm[i % 2])], writes=[Rr(dst, (ol, tt))])

    def rope_proj(Wap, nch, src_getter, src_res, out_dram, tag):
        for oc in range(8):
            sc.dma("gpsimd", lambda g, Wap=Wap, oc=oc, nch=nch, tag=tag: g.dma_start(
                out=wsl[:, oc % 2, 0:nch],
                in_=Wap[oc].rearrange("p (k f) -> p k f", f=128)),
                writes=[Rr(wsl, oc % 2)])
            for tt in range(NTT):
                i = oc * NTT + tt
                for c in range(nch):
                    sc.op("tensor", lambda t, oc=oc, tt=tt, c=c, i=i, nch=nch: t.matmul(
                        p_mm[i % 2][:], wsl[:, oc % 2, c], src_getter(c, tt),
                        start=(c == 0), stop=(c == nch - 1)),
                        reads=[Rr(wsl, oc % 2), src_res(c, tt)], writes=[Rr(p_mm[i % 2])])
                sc.op("vector", lambda v, i=i: v.tensor_copy(out=stg[:, i % 2], in_=p_mm[i % 2][:]),
                      reads=[Rr(p_mm[i % 2])], writes=[Rr(stg, i % 2)])
                sc.op("tensor", lambda t, i=i: t.matmul(p_rot[:], rtb[:], stg[:, i % 2],
                                                        start=True, stop=True),
                      reads=[Rr(rtb), Rr(stg, i % 2)], writes=[Rr(p_rot)])
                sc.op("vector", lambda v, tt=tt, i=i: v.tensor_tensor(
                    out=t1s[:, i % 2], in0=stg[:, i % 2], in1=cosb[:, tt * 512:(tt + 1) * 512],
                    op=AL.mult),
                    reads=[Rr(stg, i % 2), Rr(cosb)], writes=[Rr(t1s, i % 2)])
                sc.op("vector", lambda v, tt=tt, i=i: v.tensor_tensor(
                    out=t2s[:, i % 2], in0=p_rot[:], in1=sinb[:, tt * 512:(tt + 1) * 512],
                    op=AL.mult),
                    reads=[Rr(p_rot), Rr(sinb)], writes=[Rr(t2s, i % 2)])
                sc.op("vector", lambda v, i=i: v.tensor_tensor(
                    out=stg2[:, i % 2], in0=t1s[:, i % 2], in1=t2s[:, i % 2], op=AL.add),
                    reads=[Rr(t1s, i % 2), Rr(t2s, i % 2)], writes=[Rr(stg2, i % 2)])
                for half in range(2):
                    h = oc * 2 + half
                    sc.dma("sync", lambda s, h=h, half=half, tt=tt, i=i, out_dram=out_dram: s.dma_start(
                        out=out_dram[h * 128 + 64:(h * 128) + 128, tt * 512:(tt + 1) * 512],
                        in_=stg2[half * 64:(half + 1) * 64, i % 2]),
                        reads=[Rr(stg2, i % 2)], writes=[Rr(out_dram, (h, tt, "r"))],
                        slot=Rr(stg2, (i % 2, "st")))

    rope_proj(ropekw, DC, lambda c, tt: A5[:, c, tt * 512:(tt + 1) * 512], A5r, k5_out, "rk")
    rope_proj(ropeqw, LC, lambda c, tt: qdb[:, c, tt * 512:(tt + 1) * 512],
              lambda c, tt: Rr(qdb, (c, tt)), q5_out, "rq")

    def nope_proj(Wap, src, src_res, out_dram):
        for oc in range(8):
            sc.dma("gpsimd", lambda g, Wap=Wap, oc=oc: g.dma_start(
                out=wsl[:, oc % 2, 0:LC],
                in_=Wap[oc].rearrange("p (k f) -> p k f", f=128)),
                writes=[Rr(wsl, oc % 2)])
            for tt in range(NTT):
                i = oc * NTT + tt
                for c in range(LC):
                    sc.op("tensor", lambda t, oc=oc, tt=tt, c=c, i=i, src=src: t.matmul(
                        p_mm[i % 2][:], wsl[:, oc % 2, c], src[:, c, tt * 512:(tt + 1) * 512],
                        start=(c == 0), stop=(c == LC - 1)),
                        reads=[Rr(wsl, oc % 2), src_res(c, tt)], writes=[Rr(p_mm[i % 2])])
                sc.op("vector", lambda v, i=i: v.tensor_copy(out=stg2[:, i % 2], in_=p_mm[i % 2][:]),
                      reads=[Rr(p_mm[i % 2])], writes=[Rr(stg2, i % 2)])
                for half in range(2):
                    h = oc * 2 + half
                    sc.dma("sync", lambda s, h=h, half=half, tt=tt, i=i, out_dram=out_dram: s.dma_start(
                        out=out_dram[h * 128:h * 128 + 64, tt * 512:(tt + 1) * 512],
                        in_=stg2[half * 64:(half + 1) * 64, i % 2]),
                        reads=[Rr(stg2, i % 2)], writes=[Rr(out_dram, (h, tt, "n"))],
                        slot=Rr(stg2, (i % 2, "st")))

    nope_proj(kuw, kvd, lambda c, tt: Rr(kvd, (c, tt)), k5_out)
    nope_proj(quw, qdb, lambda c, tt: Rr(qdb, (c, tt)), q5_out)

    for vc in range(4):
        sc.dma("gpsimd", lambda g, vc=vc: g.dma_start(
            out=wvs[:, vc % 2],
            in_=vuw[vc].rearrange("p (k f) -> p k f", f=512)),
            writes=[Rr(wvs, vc % 2)])
        for to in range(16):
            i = vc * 16 + to
            for c in range(LC):
                sc.op("tensor", lambda t, vc=vc, to=to, c=c, i=i: t.matmul(
                    p_mm[i % 2][:], kvd[:, c, to * 128:(to + 1) * 128], wvs[:, vc % 2, c],
                    start=(c == 0), stop=(c == LC - 1)),
                    reads=[Rr(wvs, vc % 2), Rr(kvd, (c, to // 4))], writes=[Rr(p_mm[i % 2])])
            sc.op("vector", lambda v, i=i: v.tensor_copy(out=stg[:, i % 2], in_=p_mm[i % 2][:]),
                  reads=[Rr(p_mm[i % 2])], writes=[Rr(stg, i % 2)])
            sc.dma("sync", lambda s, vc=vc, to=to, i=i: s.dma_start(
                out=v5_out[to * 128:(to + 1) * 128, vc * 512:(vc + 1) * 512], in_=stg[:, i % 2]),
                reads=[Rr(stg, i % 2)], writes=[Rr(v5_out, (vc, to))], slot=Rr(stg, i % 2))



def build_p5():
    p = P("p5")
    y_in = p.inp("y_dm", [D, TOK], BF16)
    sc = Sched(p)
    io = dict(
        y_ap=lambda c: y_in.rearrange("(c p) t -> p c t", p=128)[:, c],
        x2_dm=p.inp("x2_dm", [D, TOK]),
        opwT=p.inp("opwT", [16, 128, DC * 128]), opb=p.inp("opb", [D]), ln1_w=p.inp("ln1_w", [D]),
        kv_d_w=p.inp("kv_d_w", [4, 128, DC * 128]), q_d_w=p.inp("q_d_w", [4, 128, DC * 128]),
        rope_k_w=p.inp("rope_k_w", [8, 128, DC * 128]),
        k_u_w=p.inp("k_u_w", [8, 128, LC * 128]), q_u_w=p.inp("q_u_w", [8, 128, LC * 128]),
        v_u_w=p.inp("v_u_w", [4, 128, LC * 512]), rope_q_w=p.inp("rope_q_w", [8, 128, LC * 128]),
        cos_t=p.inp("cos_t", [128, TOK]), sin_t=p.inp("sin_t", [128, TOK]),
        RT=p.inp("RT", [128, 128]),
        x3_dm=p.out("x3_dm", [D, TOK]),
        xin_dm=p.out("xin_dm", [D, TOK]),
        q5=p.out("q5", [D, TOK], BF16),
        k5=p.out("k5", [D, TOK], BF16),
        v5=p.out("v5", [TOK, D], BF16),
    )
    emit_p5(p, sc, io)
    sc.emit()
    return p.done()


def host_p5_inputs(p4res, p3res, inputs, core):
    b, hh = core // 2, core % 2
    sl = slice(hh * TOK, (hh + 1) * TOK)
    yfull = np.concatenate([np.asarray(p4res[cc]["y4"]).reshape(4, 256, S)[b] for cc in range(8)], axis=0)
    RD = 64
    inv_freq = 1.0 / 10000.0 ** (np.arange(0, RD, 2, dtype=np.float32) / RD)
    t_ = (np.arange(TOK, dtype=np.float32) + hh * TOK)
    ang = np.outer(t_, inv_freq)
    emb = np.concatenate([ang, ang], -1)
    cos64 = np.cos(emb).T.astype(np.float32)
    sin64 = np.sin(emb).T.astype(np.float32)
    cos_t = np.concatenate([cos64, cos64], 0)
    sin_t = np.concatenate([sin64, sin64], 0)
    R64 = np.zeros((64, 64), np.float32)
    for j in range(32):
        R64[j, j + 32] = -1.0
        R64[j + 32, j] = 1.0
    RT64 = R64.T
    RT = np.zeros((128, 128), np.float32)
    RT[0:64, 0:64] = RT64
    RT[64:128, 64:128] = RT64
    return {
        "y_dm": np.ascontiguousarray(yfull[:, sl]),
        "x2_dm": np.asarray(p3res[core]["x2_dm"]),
        "opwT": _pk(np.asarray(inputs["L1_attn_out_proj_w"]).T),
        "opb": np.asarray(inputs["L1_attn_out_proj_b"]),
        "ln1_w": np.asarray(inputs["L1_ffn_ln1_w"]),
        "kv_d_w": _pk(inputs["L1_ffn_kv_d_w"]),
        "q_d_w": _pk(inputs["L1_ffn_q_d_w"]),
        "rope_k_w": _pk(inputs["L1_ffn_rope_k_w"]),
        "k_u_w": _pk(inputs["L1_ffn_k_u_w"]),
        "q_u_w": _pk(inputs["L1_ffn_q_u_w"]),
        "v_u_w": _pk(inputs["L1_ffn_v_u_w"], 512),
        "rope_q_w": _pk(inputs["L1_ffn_rope_q_w"]),
        "cos_t": cos_t, "sin_t": sin_t, "RT": RT,
    }


def emit_p6(p, sc, io):
    nc = p.nc
    q_srcs = io["q_srcs"]; k_srcs = io["k_srcs"]; v_srcs = io["v_srcs"]
    ao_out = io["ao6"]
    nm_dm = p.dram("negm_dram", [8, 32, 128])

    qh = p.sb("qh", [128, 2, S], BF16)
    kh = p.sb("kh", [128, 2, S], BF16)
    vh = p.sb("vh", [128, 2, S // 128, 128], BF16)
    mx = p.sb("mx", [128, 32], F32)
    mt = p.sb("mt", [128, 2, 1], F32)
    mts = p.sb("mts", [128, 2, 8], F32)
    nmS = p.sb("nmS", [128, 2, 512], F32)
    sst = p.sb("sst", [128, 2, 512], F32)
    mte = p.sb("mte", [32, 128], F32)
    nmb = p.sb("nmb", [1, 2, S], BF16)
    probs = p.sb("probs", [128, 2, 512], BF16)
    rb = p.sb("rb", [128, 512], F32)
    rrow = p.sb("rrow", [1, 512], F32)
    aos = p.sb("aos", [128, 2, 512], BF16)
    ident = p.sb("ident", [128, 128], F32)
    ones_r = p.sb("ones_r", [1, 128], F32)
    ones_kb = p.sb("ones_kb", [1, 128], BF16)
    ones_cb = p.sb("ones_cb", [128, 1], BF16)

    p_s = [p.ps("p_s0", [128, 512], F32), p.ps("p_s1", [128, 512], F32)]
    p_sum = p.ps("p_sum", [1, 512], F32)
    p_pv = p.ps("p_pv", [128, 512], F32)
    p_bc = p.ps("p_bc", [128, 512], F32)
    p_mt = p.ps("p_mt", [32, 128], F32)
    p_nm = p.ps("p_nm", [128, 512], F32)

    sc.exclusive(p_s[0], p_s[1], p_sum, p_pv, p_bc, p_mt, p_nm)
    AF_, AL = AF, ALU
    SCALE = 1.0 / np.sqrt(128.0)

    sc.op("vector", lambda v: v.memset(ones_r[:], 1.0), writes=[Rr(ones_r)])
    sc.op("vector", lambda v: v.memset(ones_kb[:], 1.0), writes=[Rr(ones_kb)])
    sc.op("vector", lambda v: v.memset(ones_cb[:], 1.0), writes=[Rr(ones_cb)])
    sc.op("gpsimd", lambda g: g.memset(ident[:], 1.0), writes=[Rr(ident)])
    sc.op("gpsimd", lambda g: g.affine_select(out=ident[:], in_=ident[:], pattern=[[-1, 128]],
                                              compare_op=AL.is_equal, fill=0.0,
                                              base=0, channel_multiplier=1),
          reads=[Rr(ident)], writes=[Rr(ident)])

    for h in range(8):
        for col0, w, src in q_srcs(h):
            sc.dma("sync", lambda s, h=h, col0=col0, w=w, src=src: s.dma_start(
                out=qh[:, h % 2, col0:col0 + w], in_=src),
                reads=[Rr(q_srcs, h)], writes=[Rr(qh, h % 2)])
        for col0, w, src in k_srcs(h):
            sc.dma("sync", lambda s, h=h, col0=col0, w=w, src=src: s.dma_start(
                out=kh[:, h % 2, col0:col0 + w], in_=src),
                reads=[Rr(k_srcs, h)], writes=[Rr(kh, h % 2)])
        for m0, mc, src in v_srcs(h):
            sc.dma("sync", lambda s, h=h, m0=m0, mc=mc, src=src: s.dma_start(
                out=vh[:, h % 2, m0:m0 + mc], in_=src),
                reads=[Rr(v_srcs, h)], writes=[Rr(vh, h % 2)])
        for qt in range(32):
            kcn = qt // 4 + 1
            for kc in range(kcn):
                i = qt * 8 + kc
                sc.op("tensor", lambda t, h=h, qt=qt, kc=kc, i=i: t.matmul(
                    p_s[i % 2][:], qh[:, h % 2, qt * 128:(qt + 1) * 128],
                    kh[:, h % 2, kc * 512:(kc + 1) * 512], start=True, stop=True),
                    reads=[Rr(qh, h % 2), Rr(kh, h % 2)], writes=[Rr(p_s[i % 2])])
                sc.op("vector", lambda v, qt=qt, i=i, kc=kc: v.reduce_max(
                    out=mts[:, qt % 2, kc:kc + 1], in_=p_s[i % 2][:], axis=AX),
                    reads=[Rr(p_s[i % 2])], writes=[Rr(mts, (qt % 2, kc))])
            sc.op("vector", lambda v, qt=qt, kcn=kcn: v.reduce_max(
                out=mx[:, qt:qt + 1], in_=mts[:, qt % 2, 0:kcn], axis=AX),
                reads=[Rr(mts, (qt % 2, kc)) for kc in range(kcn)], writes=[Rr(mx, qt)])
        sc.op("vector", lambda v: v.tensor_scalar(out=mx[:], in0=mx[:], scalar1=-SCALE * 0 - 1.0,
                                                  scalar2=None, op0=AL.mult),
              reads=[Rr(mx, qt) for qt in range(32)], writes=[Rr(mx, "neg")])
        sc.op("tensor", lambda t: t.transpose(p_mt[:], mx[:], ident[:]),
              reads=[Rr(mx, "neg"), Rr(ident)], writes=[Rr(p_mt)])
        sc.op("vector", lambda v: v.tensor_copy(out=mte[:], in_=p_mt[:]),
              reads=[Rr(p_mt)], writes=[Rr(mte)])
        sc.dma("sync", lambda s, h=h: s.dma_start(out=nm_dm[h], in_=mte[:]),
               reads=[Rr(mte)], writes=[Rr(nm_dm, h)], slot=Rr(mte, "st"))
        sc.dma("gpsimd", lambda g, h=h: g.dma_start(
            out=nmb[:, h % 2], in_=nm_dm[h].rearrange("a b -> (a b)").unsqueeze(0)),
            reads=[Rr(nm_dm, h)], writes=[Rr(nmb, h % 2)])
        for qc in range(8):
            lastkt = qc * 4 + 3
            sc.op("tensor", lambda t, h=h, qc=qc: t.matmul(
                p_nm[:], ones_kb[:], nmb[0:1, h % 2, qc * 512:(qc + 1) * 512],
                start=True, stop=True),
                reads=[Rr(ones_kb), Rr(nmb, h % 2)], writes=[Rr(p_nm)])
            sc.op("vector", lambda v, qc=qc: v.tensor_copy(out=nmS[:, qc % 2], in_=p_nm[:]),
                  reads=[Rr(p_nm)], writes=[Rr(nmS, qc % 2)])
            for kt in range(lastkt + 1):
                i = kt
                sc.op("tensor", lambda t, h=h, qc=qc, kt=kt, i=i: t.matmul(
                    p_s[i % 2][:], kh[:, h % 2, kt * 128:(kt + 1) * 128],
                    qh[:, h % 2, qc * 512:(qc + 1) * 512], start=True, stop=True),
                    reads=[Rr(kh, h % 2), Rr(qh, h % 2)], writes=[Rr(p_s[i % 2])])
                sc.op("vector", lambda v, qc=qc, i=i: v.tensor_tensor(
                    out=sst[:, i % 2], in0=p_s[i % 2][:], in1=nmS[:, qc % 2], op=AL.add),
                    reads=[Rr(p_s[i % 2]), Rr(nmS, qc % 2)], writes=[Rr(sst, i % 2)])
                sc.op("scalar", lambda s, i=i: s.activation(
                    out=probs[:, i % 2], in_=sst[:, i % 2], func=AF_.Exp, scale=SCALE),
                    reads=[Rr(sst, i % 2)], writes=[Rr(probs, i % 2)])
                if kt >= qc * 4:
                    sc.op("gpsimd", lambda g, qc=qc, kt=kt, i=i: g.affine_select(
                        out=probs[:, i % 2], in_=probs[:, i % 2], pattern=[[1, 512]],
                        compare_op=AL.is_ge, fill=0.0,
                        base=qc * 512 - kt * 128, channel_multiplier=-1),
                        reads=[Rr(probs, i % 2)], writes=[Rr(probs, i % 2)])
                sc.op("tensor", lambda t, kt=kt, lastkt=lastkt, i=i: t.matmul(
                    p_sum[:], ones_cb[:], probs[:, i % 2], start=(kt == 0), stop=(kt == lastkt)),
                    reads=[Rr(probs, i % 2), Rr(ones_cb)], writes=[Rr(p_sum)])
                sc.op("tensor", lambda t, h=h, kt=kt, lastkt=lastkt, i=i: t.matmul(
                    p_pv[:], vh[:, h % 2, kt], probs[:, i % 2], start=(kt == 0), stop=(kt == lastkt)),
                    reads=[Rr(vh, h % 2), Rr(probs, i % 2)], writes=[Rr(p_pv)])
            sc.op("vector", lambda v: v.reciprocal(out=rrow[:], in_=p_sum[:]),
                  reads=[Rr(p_sum)], writes=[Rr(rrow)])
            sc.op("tensor", lambda t: t.matmul(p_bc[:], ones_r[:], rrow[:], start=True, stop=True),
                  reads=[Rr(rrow), Rr(ones_r)], writes=[Rr(p_bc)])
            sc.op("vector", lambda v: v.tensor_copy(out=rb[:], in_=p_bc[:]),
                  reads=[Rr(p_bc)], writes=[Rr(rb)])
            sc.op("vector", lambda v, qc=qc: v.tensor_tensor(out=aos[:, qc % 2], in0=p_pv[:],
                                                             in1=rb[:], op=AL.mult),
                  reads=[Rr(p_pv), Rr(rb)], writes=[Rr(aos, qc % 2)])
            sc.dma("sync", lambda s, h=h, qc=qc: s.dma_start(
                out=ao_out[h * 128:(h + 1) * 128, qc * 512:(qc + 1) * 512], in_=aos[:, qc % 2]),
                reads=[Rr(aos, qc % 2)], writes=[Rr(ao_out, (h, qc))], slot=Rr(aos, qc % 2))



def build_p6():
    p = P("p6")
    q_in = p.inp("q6", [1024, S], BF16)
    k_in = p.inp("k6", [1024, S], BF16)
    v_in = p.inp("v6", [S, 1024], BF16)
    sc = Sched(p)
    io = dict(
        q_srcs=lambda h: [(0, S, q_in[h * 128:(h + 1) * 128, :])],
        k_srcs=lambda h: [(0, S, k_in[h * 128:(h + 1) * 128, :])],
        v_srcs=lambda h: [(0, S // 128,
                           v_in.rearrange("(m p) e -> p m e", p=128)[:, :, h * 128:(h + 1) * 128])],
        ao6=p.out("ao6", [1024, S], BF16),
    )
    emit_p6(p, sc, io)
    sc.emit()
    return p.done()


def host_p6_inputs(p5res, inputs, core):
    b, hs = core // 2, (core % 2) * 8
    q = np.concatenate([p5res[2 * b]["q5"], p5res[2 * b + 1]["q5"]], axis=1)
    k = np.concatenate([p5res[2 * b]["k5"], p5res[2 * b + 1]["k5"]], axis=1)
    v = np.concatenate([p5res[2 * b]["v5"], p5res[2 * b + 1]["v5"]], axis=0)
    return {
        "q6": np.ascontiguousarray(q[hs * 128:(hs + 8) * 128]),
        "k6": np.ascontiguousarray(k[hs * 128:(hs + 8) * 128]),
        "v6": np.ascontiguousarray(v[:, hs * 128:(hs + 8) * 128]),
    }


def emit_p7(p, sc, io):
    nc = p.nc
    ao_ap = io["ao_ap"]
    xin_in = io["xin_dm"]; x3_in = io["x3_dm"]
    ow_in = io["o_w"]; ln2_in = io["ln2_w"]
    shg = io["sh_gate"]; shu = io["sh_up"]; shd = io["sh_down"]
    rg = io["r_gate"]; ru = io["r_up"]; rd = io["r_down"]
    rw_in = io["router_w"]; rb_in = io["router_b"]
    x5_out = io["x5_dm"]

    h_dm = p.dram("h_dram", [D, TOK])
    o7_dm = p.dram("o7_dram", [D, TOK], BF16)
    gu_dm = p.dram("gu_dram", [8, 1024, TOK], BF16)
    wr_dm = p.dram("wr_dram", [7, TOK])

    A7 = p.sb("A7", [128, DC, TOK], BF16)
    guh = p.sb("guh", [128, 8, 8, 512], BF16)
    wsl = p.sb("wsl", [128, 2, DC, 128], BF16)
    wu2 = p.sb("wu2", [128, 2, DC, 128], BF16)
    wdl = p.sb("wdl", [128, 2, 8, 128], BF16)
    wrb = p.sb("wrb", [128, DC, 7], BF16)
    rbr = p.sb("rbr", [1, 7], BF16)
    xc = p.sb("xc", [128, 2, 512], F32)
    osq = p.sb("osq", [128, 2, 512], F32)
    asq = p.sb("asq", [128, 2, 512], F32)
    t1b = p.sb("t1b", [128, 2, 512], F32)
    stg = p.sb("stg", [128, 2, 512], BF16)
    gstg = p.sb("gstg", [128, 2, 512], BF16)
    fstg = p.sb("fstg", [128, 2, 512], F32)
    wbc = p.sb("wbc", [128, NTT, 512], BF16)
    rbc = p.sb("rbc", [128, 2, 512], F32)
    mrbc = p.sb("mrbc", [128, 2, 512], F32)
    prr = p.sb("prr", [128, 2, 7], F32)
    m1 = p.sb("m1", [128, 2, 1], F32)
    m2 = p.sb("m2", [128, 2, 1], F32)
    eqm = p.sb("eqm", [128, 2, 7], F32)
    wr_sb = p.sb("wr_sb", [7, 128], F32)
    wrow = p.sb("wrow", [1, TOK], BF16)
    srow = p.sb("srow", [1, 5, 512], F32)
    vrow = p.sb("vrow", [1, 512], F32)
    ln2_c = p.sb("ln2_c", [128, DC], F32)
    ones_c = p.sb("ones_c", [128, 1], F32)
    ones_cb = p.sb("ones_cb", [128, 1], BF16)
    ones_r = p.sb("ones_r", [1, 128], F32)
    ones_rb = p.sb("ones_rb", [1, 128], BF16)
    identf = p.sb("identf", [1, 1], F32)
    ident = p.sb("ident", [128, 128], F32)

    p_mm = [p.ps("p_mm%d" % k, [128, 512], F32) for k in range(4)]
    p_sq = [p.ps("p_sqA", [128, 512], F32), p.ps("p_sqB", [128, 512], F32)]
    p_r = p.ps("p_r", [128, 512], F32)
    p_m = p.ps("p_m", [128, 512], F32)

    def ps_s(k):
        return p_sq[k][0:1, 0:512]

    def ps_q(k):
        return p_sq[k][64:65, 0:512]

    def rs_s(k):
        return Rr(p_sq[k], "s")

    def rs_q(k):
        return Rr(p_sq[k], "q")

    arow = [p_sq[0][0:1, 0:512], p_sq[0][32:33, 0:512], p_sq[0][64:65, 0:512],
            p_sq[1][0:1, 0:512]]
    arkey = [Rr(p_sq[0], "s"), Rr(p_sq[0], "m32"), Rr(p_sq[0], "q"), Rr(p_sq[1], "s")]

    sc.exclusive(*p_mm, p_sq[0], p_sq[1], p_r, p_m)
    AF_, AL = AF, ALU

    for buf, val in [(ones_c, 1.0), (ones_cb, 1.0), (ones_r, 1.0), (ones_rb, 1.0),
                     (identf, 1.0)]:
        sc.op("vector", lambda v, buf=buf, val=val: v.memset(buf[:], val), writes=[Rr(buf)])
    sc.op("gpsimd", lambda g: g.memset(ident[:], 1.0), writes=[Rr(ident)])
    sc.op("gpsimd", lambda g: g.affine_select(out=ident[:], in_=ident[:], pattern=[[-1, 128]],
                                              compare_op=AL.is_equal, fill=0.0,
                                              base=0, channel_multiplier=1),
          reads=[Rr(ident)], writes=[Rr(ident)])
    sc.dma("gpsimd", lambda g: g.dma_start(out=wrb[:], in_=rw_in[0].rearrange("p (k f) -> p k f", f=7)),
           writes=[Rr(wrb)])
    sc.dma("gpsimd", lambda g: g.dma_start(out=rbr[:], in_=rb_in.unsqueeze(0)), writes=[Rr(rbr)])
    for c in range(DC):
        sc.dma("sync", lambda s, c=c: s.dma_start(
            out=A7[:, c], in_=ao_ap(c)),
            reads=[Rr(ao_ap, c)], writes=[Rr(A7, (c, "ao"))], slot=Rr(A7, "ld"))

    def vec_cols(src, n, dst):
        for seg in range(0, n, 512):
            w = min(512, n - seg)
            sc.dma("sync", lambda s, src=src, seg=seg, w=w: s.dma_start(
                out=vrow[0:1, 0:w], in_=src.unsqueeze(0)[:, seg:seg + w]),
                reads=[], writes=[Rr(vrow)])
            for cc in range(w // 128):
                c = seg // 128 + cc
                sc.op("tensor", lambda t, cc=cc: t.transpose(p_m[:, 0:1],
                                                             vrow[0:1, cc * 128:(cc + 1) * 128], identf[:]),
                      reads=[Rr(vrow), Rr(identf)], writes=[Rr(p_m)])
                sc.op("vector", lambda v, c=c, dst=dst: v.tensor_copy(out=dst[:, c:c + 1], in_=p_m[:, 0:1]),
                      reads=[Rr(p_m)], writes=[Rr(dst)])

    vec_cols(ln2_in, D, ln2_c)

    fzc = [0]

    def finalize(mean, aps_s, aps_q, eps):
        sl = fzc[0] % 2
        fzc[0] += 1
        if mean:
            sc.op("vector", lambda v: v.tensor_scalar(out=srow[:, 0], in0=aps_s[0], scalar1=1.0 / D,
                                                      scalar2=None, op0=AL.mult),
                  reads=[aps_s[1]], writes=[Rr(srow, 0)])
            sc.op("vector", lambda v: v.tensor_scalar(out=srow[:, 1], in0=aps_q[0], scalar1=1.0 / D,
                                                      scalar2=None, op0=AL.mult),
                  reads=[aps_q[1]], writes=[Rr(srow, 1)])
            sc.op("vector", lambda v: v.tensor_tensor(out=srow[:, 2], in0=srow[:, 0], in1=srow[:, 0],
                                                      op=AL.mult),
                  reads=[Rr(srow, 0)], writes=[Rr(srow, 2)])
            sc.op("vector", lambda v: v.tensor_tensor(out=srow[:, 2], in0=srow[:, 1], in1=srow[:, 2],
                                                      op=AL.subtract),
                  reads=[Rr(srow, 1), Rr(srow, 2)], writes=[Rr(srow, 2)])
            sc.op("vector", lambda v: v.tensor_scalar(out=srow[:, 2], in0=srow[:, 2], scalar1=eps,
                                                      scalar2=None, op0=AL.add),
                  reads=[Rr(srow, 2)], writes=[Rr(srow, 2)])
        else:
            sc.op("vector", lambda v: v.tensor_scalar(out=srow[:, 2], in0=aps_q[0], scalar1=1.0 / D,
                                                      scalar2=eps, op0=AL.mult, op1=AL.add),
                  reads=[aps_q[1]], writes=[Rr(srow, 2)])
        sc.op("scalar", lambda s: s.activation(out=srow[:, 3], in_=srow[:, 2], func=AF_.Ln),
              reads=[Rr(srow, 2)], writes=[Rr(srow, 3)])
        sc.op("scalar", lambda s: s.activation(out=srow[:, 3], in_=srow[:, 3], func=AF_.Exp, scale=-0.5),
              reads=[Rr(srow, 3)], writes=[Rr(srow, 3)])
        sc.op("tensor", lambda t: t.matmul(p_r[:], ones_r[:], srow[:, 3], start=True, stop=True),
              reads=[Rr(srow, 3), Rr(ones_r)], writes=[Rr(p_r)])
        sc.op("scalar", lambda s, sl=sl: s.activation(out=rbc[:, sl], in_=p_r[:], func=AF_.Copy),
              reads=[Rr(p_r)], writes=[Rr(rbc, sl)])
        if mean:
            sc.op("vector", lambda v: v.scalar_tensor_tensor(out=srow[:, 4], in0=srow[:, 0],
                                                             scalar=-1.0, in1=srow[:, 3],
                                                             op0=AL.mult, op1=AL.mult),
                  reads=[Rr(srow, 0), Rr(srow, 3)], writes=[Rr(srow, 4)])
            sc.op("tensor", lambda t: t.matmul(p_m[:], ones_r[:], srow[:, 4], start=True, stop=True),
                  reads=[Rr(srow, 4), Rr(ones_r)], writes=[Rr(p_m)])
            sc.op("scalar", lambda s, sl=sl: s.activation(out=mrbc[:, sl], in_=p_m[:], func=AF_.Copy),
                  reads=[Rr(p_m)], writes=[Rr(mrbc, sl)])
        return sl

    for oc in range(16):
        sc.dma("gpsimd", lambda g, oc=oc: g.dma_start(
            out=wsl[:, oc % 2],
            in_=ow_in[oc].rearrange("p (k f) -> p k f", f=128)),
            writes=[Rr(wsl, oc % 2)])
        for tt in range(NTT):
            i = oc * NTT + tt
            pm = p_mm[i % 4]
            for c in range(DC):
                sc.op("tensor", lambda t, oc=oc, tt=tt, c=c, pm=pm: t.matmul(
                    pm[:], wsl[:, oc % 2, c], A7[:, c, tt * 512:(tt + 1) * 512],
                    start=(c == 0), stop=(c == DC - 1)),
                    reads=[Rr(wsl, oc % 2), Rr(A7, (c, "ao"))], writes=[Rr(pm)])
            sc.dma("sync", lambda s, oc=oc, tt=tt, i=i: s.dma_start(
                out=xc[:, i % 2],
                in_=xin_in.rearrange("(c p) (t f) -> p c t f", p=128, f=512)[:, oc, tt]),
                writes=[Rr(xc, i % 2)])
            sc.op("vector", lambda v, i=i, pm=pm: v.tensor_tensor(out=fstg[:, i % 2], in0=xc[:, i % 2],
                                                                  in1=pm[:], op=AL.add),
                  reads=[Rr(xc, i % 2), Rr(pm)], writes=[Rr(fstg, i % 2)])
            sc.dma("sync", lambda s, oc=oc, tt=tt, i=i: s.dma_start(
                out=h_dm.rearrange("(c p) (t f) -> p c t f", p=128, f=512)[:, oc, tt], in_=fstg[:, i % 2]),
                reads=[Rr(fstg, i % 2)], writes=[Rr(h_dm, (oc, tt))], slot=Rr(fstg, i % 2))
            sc.op("scalar", lambda s, i=i: s.activation(out=osq[:, i % 2], in_=fstg[:, i % 2],
                                                        func=AF_.Square),
                  reads=[Rr(fstg, i % 2)], writes=[Rr(osq, i % 2)])
            sc.op("tensor", lambda t, oc=oc, tt=tt, i=i: t.matmul(
                arow[tt], ones_c[:], osq[:, i % 2], start=(oc == 0), stop=(oc == 15)),
                reads=[Rr(osq, i % 2), Rr(ones_c)], writes=[arkey[tt]])

    for tt in range(NTT):
        sl = finalize(False, None, (arow[tt], arkey[tt]), 1e-5)
        for c in range(DC):
            i = tt * DC + c
            sc.dma("sync", lambda s, tt=tt, c=c, i=i: s.dma_start(
                out=xc[:, i % 2],
                in_=h_dm.rearrange("(c p) (t f) -> p c t f", p=128, f=512)[:, c, tt]),
                reads=[Rr(h_dm, (c, tt))], writes=[Rr(xc, i % 2)])
            sc.op("gpsimd", lambda g, i=i, sl=sl: g.tensor_tensor(
                out=t1b[:, i % 2], in0=xc[:, i % 2], in1=rbc[:, sl], op=AL.mult),
                reads=[Rr(xc, i % 2), Rr(rbc, sl)], writes=[Rr(t1b, i % 2)])
            sc.op("vector", lambda v, tt=tt, c=c, i=i: v.tensor_scalar(
                out=A7[:, c, tt * 512:(tt + 1) * 512], in0=t1b[:, i % 2],
                scalar1=ln2_c[:, c:c + 1], scalar2=None, op0=AL.mult),
                reads=[Rr(t1b, i % 2), Rr(ln2_c)], writes=[Rr(A7, (c, tt)), Rr(A7, (c, "ao"))])

    def A7r(c, tt):
        return Rr(A7, (c, tt))

    for t1 in range(16):
        i = t1
        tt = t1 // 4
        pm = p_mm[t1 % 4]
        for c in range(DC):
            sc.op("tensor", lambda t, t1=t1, tt=tt, c=c, pm=pm: t.matmul(
                pm[:, 0:7], A7[:, c, t1 * 128:(t1 + 1) * 128], wrb[:, c],
                start=(c == 0), stop=False),
                reads=[A7r(c, tt), Rr(wrb)], writes=[Rr(pm)])
        sc.op("tensor", lambda t, pm=pm: t.matmul(pm[:, 0:7], ones_rb[:], rbr[:],
                                                  start=False, stop=True),
              reads=[Rr(ones_rb), Rr(rbr)], writes=[Rr(pm)])
        sc.op("scalar", lambda s, i=i, pm=pm: s.activation(out=prr[:, i % 2], in_=pm[:, 0:7],
                                                           func=AF_.Sigmoid),
              reads=[Rr(pm)], writes=[Rr(prr, i % 2)])
        sc.op("vector", lambda v, i=i: v.reduce_max(out=m1[:, i % 2], in_=prr[:, i % 2], axis=AX),
              reads=[Rr(prr, i % 2)], writes=[Rr(m1, i % 2)])
        sc.op("vector", lambda v, i=i: v.tensor_scalar(out=eqm[:, i % 2], in0=prr[:, i % 2],
                                                       scalar1=m1[:, i % 2, 0:1], scalar2=None,
                                                       op0=AL.is_equal),
              reads=[Rr(prr, i % 2), Rr(m1, i % 2)], writes=[Rr(eqm, i % 2)])
        sc.op("vector", lambda v, i=i: v.scalar_tensor_tensor(out=eqm[:, i % 2], in0=eqm[:, i % 2],
                                                              scalar=-1e30, in1=prr[:, i % 2],
                                                              op0=AL.mult, op1=AL.add),
              reads=[Rr(eqm, i % 2), Rr(prr, i % 2)], writes=[Rr(eqm, i % 2)])
        sc.op("vector", lambda v, i=i: v.reduce_max(out=m2[:, i % 2], in_=eqm[:, i % 2], axis=AX),
              reads=[Rr(eqm, i % 2)], writes=[Rr(m2, i % 2)])
        sc.op("vector", lambda v, i=i: v.tensor_scalar(out=eqm[:, i % 2], in0=prr[:, i % 2],
                                                       scalar1=m2[:, i % 2, 0:1], scalar2=None,
                                                       op0=AL.is_ge),
              reads=[Rr(prr, i % 2), Rr(m2, i % 2)], writes=[Rr(eqm, i % 2)])
        sc.op("vector", lambda v, i=i: v.tensor_tensor(out=prr[:, i % 2], in0=prr[:, i % 2],
                                                       in1=eqm[:, i % 2], op=AL.mult),
              reads=[Rr(prr, i % 2), Rr(eqm, i % 2)], writes=[Rr(prr, i % 2)])
        sc.op("vector", lambda v, i=i: v.tensor_tensor(out=m1[:, i % 2], in0=m1[:, i % 2],
                                                       in1=m2[:, i % 2], op=AL.add),
              reads=[Rr(m1, i % 2), Rr(m2, i % 2)], writes=[Rr(m1, i % 2)])
        sc.op("vector", lambda v, i=i: v.reciprocal(out=m1[:, i % 2], in_=m1[:, i % 2]),
              reads=[Rr(m1, i % 2)], writes=[Rr(m1, i % 2)])
        sc.op("vector", lambda v, i=i: v.tensor_scalar(out=prr[:, i % 2], in0=prr[:, i % 2],
                                                       scalar1=m1[:, i % 2, 0:1], scalar2=0.125,
                                                       op0=AL.mult, op1=AL.mult),
              reads=[Rr(prr, i % 2), Rr(m1, i % 2)], writes=[Rr(prr, i % 2)])
        sc.op("tensor", lambda t, i=i: t.transpose(p_m[0:7, 0:128], prr[:, i % 2], ident[:]),
              reads=[Rr(prr, i % 2), Rr(ident)], writes=[Rr(p_m)])
        sc.op("vector", lambda v: v.tensor_copy(out=wr_sb[:], in_=p_m[0:7, 0:128]),
              reads=[Rr(p_m)], writes=[Rr(wr_sb)])
        sc.dma("sync", lambda s, t1=t1: s.dma_start(out=wr_dm[:, t1 * 128:(t1 + 1) * 128], in_=wr_sb[:]),
               reads=[Rr(wr_sb)], writes=[Rr(wr_dm, t1)], slot=Rr(wr_sb, "st"))

    ents_g = [shg] + [rg[e] for e in range(7)]
    ents_u = [shu] + [ru[e] for e in range(7)]
    for e in range(8):
        if e > 0:
            sc.dma("gpsimd", lambda g, e=e: g.dma_start(out=wrow[:], in_=wr_dm[e - 1:e, :]),
                   reads=[Rr(wr_dm, t1) for t1 in range(16)], writes=[Rr(wrow)])
            for tt in range(NTT):
                sc.op("tensor", lambda t, tt=tt: t.matmul(
                    p_r[:], ones_rb[:], wrow[0:1, tt * 512:(tt + 1) * 512],
                    start=True, stop=True),
                    reads=[Rr(wrow), Rr(ones_rb)], writes=[Rr(p_r)])
                sc.op("scalar", lambda s, tt=tt: s.activation(out=wbc[:, tt], in_=p_r[:],
                                                              func=AF_.Copy),
                      reads=[Rr(p_r)], writes=[Rr(wbc, tt)])
        for uc in range(8):
            j = e * 8 + uc
            sc.dma("gpsimd", lambda g, e=e, uc=uc, j=j: g.dma_start(
                out=wsl[:, j % 2],
                in_=ents_g[e][uc].rearrange("p (k f) -> p k f", f=128)),
                writes=[Rr(wsl, j % 2)])
            sc.dma("gpsimd", lambda g, e=e, uc=uc, j=j: g.dma_start(
                out=wu2[:, j % 2],
                in_=ents_u[e][uc].rearrange("p (k f) -> p k f", f=128)),
                writes=[Rr(wu2, j % 2)])
            for tt in range(NTT):
                i = j * NTT + tt
                pg = p_mm[(2 * i) % 4]
                pu = p_mm[(2 * i + 1) % 4]
                for c in range(DC):
                    sc.op("tensor", lambda t, j=j, tt=tt, c=c, pg=pg: t.matmul(
                        pg[:], wsl[:, j % 2, c], A7[:, c, tt * 512:(tt + 1) * 512],
                        start=(c == 0), stop=(c == DC - 1)),
                        reads=[Rr(wsl, j % 2), A7r(c, tt)], writes=[Rr(pg)])
                for c in range(DC):
                    sc.op("tensor", lambda t, j=j, tt=tt, c=c, pu=pu: t.matmul(
                        pu[:], wu2[:, j % 2, c], A7[:, c, tt * 512:(tt + 1) * 512],
                        start=(c == 0), stop=(c == DC - 1)),
                        reads=[Rr(wu2, j % 2), A7r(c, tt)], writes=[Rr(pu)])
                sc.op("scalar", lambda s, i=i, pg=pg: s.activation(out=stg[:, i % 2], in_=pg[:],
                                                                   func=AF_.Silu),
                      reads=[Rr(pg)], writes=[Rr(stg, i % 2)])
                upscale = 0.125 if e == 0 else 1.0
                sc.op("scalar", lambda s, i=i, pu=pu, upscale=upscale: s.activation(
                    out=gstg[:, i % 2], in_=pu[:], func=AF_.Copy, scale=upscale),
                    reads=[Rr(pu)], writes=[Rr(gstg, i % 2)])
                sc.op("vector", lambda v, i=i: v.tensor_tensor(out=gstg[:, i % 2], in0=stg[:, i % 2],
                                                               in1=gstg[:, i % 2], op=AL.mult),
                      reads=[Rr(stg, i % 2), Rr(gstg, i % 2)], writes=[Rr(gstg, i % 2)])
                if e > 0:
                    sc.op("gpsimd", lambda g, i=i, tt=tt: g.tensor_tensor(
                        out=gstg[:, i % 2], in0=gstg[:, i % 2], in1=wbc[:, tt], op=AL.mult),
                        reads=[Rr(gstg, i % 2), Rr(wbc, tt)], writes=[Rr(gstg, i % 2)])
                sc.dma("sync", lambda s, e=e, uc=uc, tt=tt, i=i: s.dma_start(
                    out=gu_dm[e, uc * 128:(uc + 1) * 128, tt * 512:(tt + 1) * 512].rearrange(
                        "r t -> r t"), in_=gstg[:, i % 2]),
                    reads=[Rr(gstg, i % 2)], writes=[Rr(gu_dm, (e, uc, tt))], slot=Rr(gstg, i % 2))

    ents_d = [shd] + [rd[e] for e in range(7)]
    for tg in range(NTT):
        tt = tg
        sc.dma("sync", lambda s, tg=tg: s.dma_start(
            out=guh[:], in_=gu_dm.rearrange("e (u p) t -> p e u t", p=128)[:, :, :,
                                                                          tg * 512:(tg + 1) * 512]),
            reads=[Rr(gu_dm, (e, uc, tg)) for e in range(8) for uc in range(8)],
            writes=[Rr(guh)])
        for oc in range(16):
            i = tg * 16 + oc
            pm = p_mm[i % 4]
            for e in range(8):
                j2 = oc * 8 + e
                sc.dma("gpsimd", lambda g, e=e, oc=oc, j2=j2: g.dma_start(
                    out=wdl[:, j2 % 2],
                    in_=ents_d[e][oc].rearrange("p (k f) -> p k f", f=128)),
                    writes=[Rr(wdl, j2 % 2)])
                for uc in range(8):
                    sc.op("tensor", lambda t, e=e, uc=uc, j2=j2, pm=pm: t.matmul(
                        pm[:], wdl[:, j2 % 2, uc], guh[:, e, uc, :],
                        start=(e == 0 and uc == 0), stop=(e == 7 and uc == 7)),
                        reads=[Rr(wdl, j2 % 2), Rr(guh)], writes=[Rr(pm)])
            sc.dma("sync", lambda s, oc=oc, tt=tt, i=i: s.dma_start(
                out=xc[:, i % 2],
                in_=h_dm.rearrange("(c p) (t f) -> p c t f", p=128, f=512)[:, oc, tt]),
                reads=[Rr(h_dm, (oc, tt))], writes=[Rr(xc, i % 2)])
            sc.op("vector", lambda v, i=i, pm=pm: v.tensor_tensor(out=fstg[:, i % 2], in0=xc[:, i % 2],
                                                                  in1=pm[:], op=AL.add),
                  reads=[Rr(xc, i % 2), Rr(pm)], writes=[Rr(fstg, i % 2)])
            sc.dma("gpsimd", lambda g, oc=oc, tt=tt, i=i: g.dma_start(
                out=o7_dm.rearrange("(c p) (t f) -> p c t f", p=128, f=512)[:, oc, tt],
                in_=fstg[:, i % 2]),
                reads=[Rr(fstg, i % 2)], writes=[Rr(o7_dm, (oc, tt))], slot=Rr(fstg, (i % 2, "st")))
            sc.op("scalar", lambda s, i=i: s.activation(out=osq[:, i % 2], in_=fstg[:, i % 2],
                                                        func=AF_.Square),
                  reads=[Rr(fstg, i % 2)], writes=[Rr(osq, i % 2)])
            sc.op("tensor", lambda t, oc=oc, tg=tg, i=i: t.matmul(
                ps_s(tg % 2), ones_c[:], fstg[:, i % 2], start=(oc == 0), stop=(oc == 15)),
                reads=[Rr(fstg, i % 2), Rr(ones_c)], writes=[rs_s(tg % 2)])
            sc.op("tensor", lambda t, oc=oc, tg=tg, i=i: t.matmul(
                ps_q(tg % 2), ones_c[:], osq[:, i % 2], start=(oc == 0), stop=(oc == 15)),
                reads=[Rr(osq, i % 2), Rr(ones_c)], writes=[rs_q(tg % 2)])
        sl = finalize(True, (ps_s(tg % 2), rs_s(tg % 2)), (ps_q(tg % 2), rs_q(tg % 2)), LNEPS)
        for c in range(DC):
            i = tt * DC + c
            sc.dma("sync", lambda s, tt=tt, c=c, i=i: s.dma_start(
                out=xc[:, i % 2],
                in_=x3_in.rearrange("(c p) (t f) -> p c t f", p=128, f=512)[:, c, tt]),
                writes=[Rr(xc, i % 2)])
            sc.dma("sync", lambda s, tt=tt, c=c, i=i: s.dma_start(
                out=stg[:, i % 2],
                in_=o7_dm.rearrange("(c p) (t f) -> p c t f", p=128, f=512)[:, c, tt]),
                reads=[Rr(o7_dm, (c, tt))], writes=[Rr(stg, i % 2)])
            sc.op("vector", lambda v, i=i, sl=sl: v.tensor_tensor(
                out=asq[:, i % 2], in0=stg[:, i % 2], in1=rbc[:, sl], op=AL.mult),
                reads=[Rr(stg, i % 2), Rr(rbc, sl)], writes=[Rr(asq, i % 2)])
            sc.op("gpsimd", lambda g, i=i, sl=sl: g.tensor_tensor(
                out=t1b[:, i % 2], in0=asq[:, i % 2], in1=mrbc[:, sl], op=AL.add),
                reads=[Rr(asq, i % 2), Rr(mrbc, sl)], writes=[Rr(t1b, i % 2)])
            sc.op("vector", lambda v, i=i: v.tensor_tensor(out=xc[:, i % 2], in0=xc[:, i % 2],
                                                           in1=t1b[:, i % 2], op=AL.add),
                  reads=[Rr(xc, i % 2), Rr(t1b, i % 2)], writes=[Rr(xc, i % 2)])
            sc.dma("sync", lambda s, tt=tt, c=c, i=i: s.dma_start(
                out=x5_out.rearrange("(c p) (t f) -> p c t f", p=128, f=512)[:, c, tt],
                in_=xc[:, i % 2]),
                reads=[Rr(xc, i % 2)], writes=[Rr(x5_out, (c, tt))], slot=Rr(xc, i % 2))


def build_p7():
    p = P("p7")
    ao_in = p.inp("ao7", [D, TOK], BF16)
    sc = Sched(p)
    io = dict(
        ao_ap=lambda c: ao_in.rearrange("(c p) t -> p c t", p=128)[:, c],
        xin_dm=p.inp("xin_dm", [D, TOK]),
        x3_dm=p.inp("x3_dm", [D, TOK]),
        o_w=p.inp("o_w", [16, 128, DC * 128]),
        ln2_w=p.inp("ln2_w", [D]),
        sh_gate=p.inp("sh_gate", [8, 128, DC * 128]), sh_up=p.inp("sh_up", [8, 128, DC * 128]),
        sh_down=p.inp("sh_down", [16, 128, 8 * 128]),
        r_gate=p.inp("r_gate", [7, 8, 128, DC * 128]), r_up=p.inp("r_up", [7, 8, 128, DC * 128]),
        r_down=p.inp("r_down", [7, 16, 128, 8 * 128]),
        router_w=p.inp("router_w", [1, 128, DC * 7]),
        router_b=p.inp("router_b", [7]),
        x5_dm=p.out("x5_dm", [D, TOK]),
    )
    emit_p7(p, sc, io)
    sc.emit()
    return p.done()


def host_p7_inputs(p6res, p5res, inputs, core):
    b, hh = core // 2, core % 2
    sl = slice(hh * TOK, (hh + 1) * TOK)
    ao = np.concatenate([p6res[2 * b]["ao6"], p6res[2 * b + 1]["ao6"]], axis=0)
    return {
        "ao7": np.ascontiguousarray(ao[:, sl]),
        "xin_dm": np.asarray(p5res[core]["xin_dm"]),
        "x3_dm": np.asarray(p5res[core]["x3_dm"]),
        "o_w": _pk(inputs["L1_ffn_o_w"]),
        "ln2_w": np.asarray(inputs["L1_ffn_ln2_w"]),
        "sh_gate": _pk(inputs["L1_ffn_sh_gate"]),
        "sh_up": _pk(inputs["L1_ffn_sh_up"]),
        "sh_down": _pk(inputs["L1_ffn_sh_down"]),
        "r_gate": np.stack([_pk(np.asarray(inputs["L1_ffn_r_gate"])[e]) for e in range(7)]),
        "r_up": np.stack([_pk(np.asarray(inputs["L1_ffn_r_up"])[e]) for e in range(7)]),
        "r_down": np.stack([_pk(np.asarray(inputs["L1_ffn_r_down"])[e]) for e in range(7)]),
        "router_w": _pk(inputs["L1_ffn_router_w"], 7),
        "router_b": np.asarray(inputs["L1_ffn_router_b"]),
    }


def build_fused():
    from concourse.bass import ds as _ds
    p = P("fz")
    nc = p.nc
    nc.cache_partition_id()
    pid = nc.partition_id()
    b2 = (pid // 2) * 2
    hh = pid % 2
    hs8 = (pid % 2) * 8
    sc = Sched(p)
    RG = [list(range(8))]

    x_dm = p.inp("x_dm", [D, TOK])
    Wq = p.inp("Wq", [16, 128, DC * 128]); Wk = p.inp("Wk", [16, 128, DC * 128])
    Wv = p.inp("Wv", [4, 128, DC * 512])
    Wf = p.inp("Wf", [1, 128, DC * 16])
    seg_in = p.inp("seg0", [1, S])
    Wo = p.inp("Wo", [16, 128, DC * 128]); normw = p.inp("norm_w", [D])
    wqgT = p.inp("wqgT", [HC, 128, DC * 128], I32); wquT = p.inp("wquT", [HC, 128, DC * 128], I32)
    wqdT = p.inp("wqdT", [16, 128, HC * 128], I32)
    sg_in = p.inp("s_gate", [HID]); su_in = p.inp("s_up", [HID]); sd_in = p.inp("s_down", [D])
    dtwT = p.inp("dtwT", [16, 128, DC * 128]); dtb_in = p.inp("dtb", [D])
    xpwT = p.inp("xpwT", [1, 128, DC * 32]); xpb_in = p.inp("xpb", [32])
    al_in = p.inp("A_sl", [2, 128, 16])
    opwT = p.inp("opwT", [16, 128, DC * 128]); opb_in = p.inp("opb", [D]); ln1_in = p.inp("ln1_w", [D])
    kvdw = p.inp("kv_d_w", [4, 128, DC * 128]); qdw = p.inp("q_d_w", [4, 128, DC * 128])
    ropekw = p.inp("rope_k_w", [8, 128, DC * 128])
    kuw = p.inp("k_u_w", [8, 128, LC * 128]); quw = p.inp("q_u_w", [8, 128, LC * 128])
    vuw = p.inp("v_u_w", [4, 128, LC * 512]); ropeqw = p.inp("rope_q_w", [8, 128, LC * 128])
    cos_in = p.inp("cos_t", [128, TOK]); sin_in = p.inp("sin_t", [128, TOK])
    rt_in = p.inp("RT", [128, 128])
    ow_in = p.inp("o_w", [16, 128, DC * 128]); ln2_in = p.inp("ln2_w", [D])
    shg = p.inp("sh_gate", [8, 128, DC * 128]); shu = p.inp("sh_up", [8, 128, DC * 128])
    shd = p.inp("sh_down", [16, 128, 8 * 128])
    rg = p.inp("r_gate", [7, 8, 128, DC * 128]); ru = p.inp("r_up", [7, 8, 128, DC * 128])
    rd = p.inp("r_down", [7, 16, 128, 8 * 128])
    rw_in = p.inp("router_w", [1, 128, DC * 7]); rb_in = p.inp("router_b", [7])
    x5_out = p.out("x5_dm", [D, TOK])

    SH = dict(addr_space="Shared")
    q1 = p.dram("q1", [D, TOK], BF16); q1g = p.dram("q1g", [8 * D, TOK], BF16, **SH)
    k1 = p.dram("k1", [D, TOK], BF16); k1g = p.dram("k1g", [8 * D, TOK], BF16, **SH)
    v1 = p.dram("v1", [TOK, D], BF16); v1g = p.dram("v1g", [8 * TOK, D], BF16, **SH)
    ld1 = p.dram("ld1", [16, TOK]); ld1g = p.dram("ld1g", [8 * 16, TOK], **SH)
    o2 = p.dram("o2", [1024, S], BF16); o2g = p.dram("o2g", [8 * 1024, S], BF16, **SH)
    x2i = p.dram("x2i", [D, TOK])
    xn2i = p.dram("xn2i", [D, TOK], BF16); xn2g = p.dram("xn2g", [8 * D, TOK], BF16, **SH)
    dti = p.dram("dti", [D, TOK]); dtg = p.dram("dtg", [8 * D, TOK], **SH)
    bci = p.dram("bci", [32, TOK], BF16); bcg = p.dram("bcg", [8 * 32, TOK], BF16, **SH)
    y4i = p.dram("y4i", [4, 256, S], BF16); y4g = p.dram("y4g", [8 * 1024, S], BF16, **SH)
    x3i = p.dram("x3i", [D, TOK])
    xini = p.dram("xini", [D, TOK])
    q5i = p.dram("q5i", [D, TOK], BF16); q5g = p.dram("q5g", [8 * D, TOK], BF16, **SH)
    k5i = p.dram("k5i", [D, TOK], BF16); k5g = p.dram("k5g", [8 * D, TOK], BF16, **SH)
    v5i = p.dram("v5i", [TOK, D], BF16); v5g = p.dram("v5g", [8 * TOK, D], BF16, **SH)
    ao6i = p.dram("ao6i", [1024, S], BF16); ao6g = p.dram("ao6g", [8 * 1024, S], BF16, **SH)

    def AG(src, dst, wr_keys):
        sc.cc(lambda g, src=src, dst=dst: g.collective_compute(
            "AllGather", ALU.bypass, replica_groups=RG, ins=[src.opt()], outs=[dst.opt()]),
            reads=[], writes=wr_keys)

    p.push_scope("z1_")
    emit_p1(p, sc, dict(x_dm=x_dm, Wq=Wq, Wk=Wk, Wv=Wv, Wf=Wf,
                        q_dm=q1, k_dm=k1, v_tm=v1, ld_dm=ld1))
    sc.barrier()
    p.pop_scope()

    q2l = p.dram("q2l", [1024, S], BF16)
    k2l = p.dram("k2l", [1024, S], BF16)
    v2l = p.dram("v2l", [S, 1024], BF16)
    ld2l = p.dram("ld2l", [8, S])
    AG(q1, q1g, [Rr(q1g)])
    AG(k1, k1g, [Rr(k1g)])
    AG(v1, v1g, [Rr(v1g)])
    AG(ld1, ld1g, [Rr(ld1g)])
    sc.dma("sync", lambda e: e.dma_start(
        out=q2l.rearrange("d (r t) -> r d t", r=2),
        in_=q1g.rearrange("(r d) t -> r d t", d=D)[_ds(b2, 2), _ds(hs8 * 128, 1024), :]),
        reads=[Rr(q1g)], writes=[Rr(q2l)])
    sc.dma("sync", lambda e: e.dma_start(
        out=k2l.rearrange("d (r t) -> r d t", r=2),
        in_=k1g.rearrange("(r d) t -> r d t", d=D)[_ds(b2, 2), _ds(hs8 * 128, 1024), :]),
        reads=[Rr(k1g)], writes=[Rr(k2l)])
    sc.dma("sync", lambda e: e.dma_start(
        out=v2l.rearrange("(r s) e -> r s e", r=2),
        in_=v1g.rearrange("(r s) e -> r s e", s=TOK)[_ds(b2, 2), :, _ds(hs8 * 128, 1024)]),
        reads=[Rr(v1g)], writes=[Rr(v2l)])
    sc.dma("sync", lambda e: e.dma_start(
        out=ld2l.rearrange("x (r t) -> r x t", r=2),
        in_=ld1g.rearrange("(r x) t -> r x t", x=16)[_ds(b2, 2), _ds(hs8, 8), :]),
        reads=[Rr(ld1g)], writes=[Rr(ld2l)])

    def q_srcs2(h):
        return [(0, S, q2l[h * 128:(h + 1) * 128, :])]

    def k_srcs2(h):
        return [(0, S, k2l[h * 128:(h + 1) * 128, :])]

    def v_srcs2(h):
        return [(0, S // 128, v2l.rearrange("(m p) e -> p m e", p=128)[:, :, h * 128:(h + 1) * 128])]

    def ld_srcs2(h):
        return [(0, S, ld2l[h:h + 1, :])]

    for h in range(8):
        pass
    p.push_scope("z2_")
    io2 = dict(q_srcs=q_srcs2, k_srcs=k_srcs2, v_srcs=v_srcs2, ld_srcs=ld_srcs2,
               seg0=seg_in, o_dm=o2, deps=[Rr(q2l), Rr(k2l), Rr(v2l), Rr(ld2l)])
    emit_p2(p, sc, io2)
    sc.barrier()
    p.pop_scope()

    o3l = p.dram("o3l", [D, TOK], BF16)

    def o_ap3(c, tt):
        return o3l.rearrange("(c p) (t f) -> p c t f", p=128, f=512)[:, c, tt]

    AG(o2, o2g, [Rr(o2g)])
    sc.dma("sync", lambda e: e.dma_start(
        out=o3l.rearrange("(r d) t -> r d t", r=2),
        in_=o2g.rearrange("(r d) t -> r d t", d=1024)[_ds(b2, 2), :, _ds(hh * TOK, TOK)]),
        reads=[Rr(o2g)], writes=[Rr(o3l)] + [Rr(o_ap3, (c, tt)) for c in range(DC)
                                            for tt in range(NTT)])

    p.push_scope("z3_")
    emit_p3(p, sc, dict(o_ap=o_ap3, x_dm=x_dm, Wo=Wo, norm_w=normw,
                        wqgT=wqgT, wquT=wquT, wqdT=wqdT,
                        s_gate=sg_in, s_up=su_in, s_down=sd_in,
                        dtwT=dtwT, dtb=dtb_in, xpwT=xpwT, xpb=xpb_in,
                        x2_dm=x2i, xn2_dm=xn2i, dt_dm=dti, bc_dm=bci))
    sc.barrier()
    p.pop_scope()

    dt4l = p.dram("dt4l", [8, 256, TOK])
    xn4l = p.dram("xn4l", [8, 256, TOK], BF16)
    def dt_ap4(b, ds_, tc):
        return dt4l[2 * b + tc // 4, ds_ * 128:(ds_ + 1) * 128,
                    (tc % 4) * 512:(tc % 4 + 1) * 512]

    def xn_ap4(b, ds_, tc):
        return xn4l[2 * b + tc // 4, ds_ * 128:(ds_ + 1) * 128,
                    (tc % 4) * 512:(tc % 4 + 1) * 512]

    def bc_ap4(b, tc):
        return bcg[(2 * b + tc // 4) * 32:(2 * b + tc // 4) * 32 + 32,
                   (tc % 4) * 512:(tc % 4 + 1) * 512].unsqueeze(0)

    AG(dti, dtg, [Rr(dtg)])
    AG(xn2i, xn2g, [Rr(xn2g)])
    AG(bci, bcg, [Rr(bcg)] + [Rr(bc_ap4, (b, tc)) for b in range(4) for tc in range(8)])
    sc.dma("sync", lambda e: e.dma_start(
        out=dt4l[:],
        in_=dtg.rearrange("(k d) t -> k d t", d=D)[:, _ds(pid * 256, 256), :]),
        reads=[Rr(dtg)], writes=[Rr(dt4l)] + [Rr(dt_ap4, (b_, d_, tc))
                                              for b_ in range(4) for d_ in range(2)
                                              for tc in range(8)])
    sc.dma("sync", lambda e: e.dma_start(
        out=xn4l[:],
        in_=xn2g.rearrange("(k d) t -> k d t", d=D)[:, _ds(pid * 256, 256), :]),
        reads=[Rr(xn2g)], writes=[Rr(xn4l)] + [Rr(xn_ap4, (b_, d_, tc))
                                               for b_ in range(4) for d_ in range(2)
                                               for tc in range(8)])

    p.push_scope("z4_")
    emit_p4(p, sc, dict(dt_ap=dt_ap4, xn_ap=xn_ap4, bc_ap=bc_ap4, A_sl=al_in, y4=y4i))
    sc.barrier()
    p.pop_scope()

    y5l = p.dram("y5l", [D, TOK], BF16)

    def y_ap5(c):
        return y5l.rearrange("(c p) t -> p c t", p=128)[:, c]

    AG(y4i.rearrange("b r t -> (b r) t"), y4g, [Rr(y4g)])
    sc.dma("sync", lambda e: e.dma_start(
        out=y5l.rearrange("(k d) t -> k d t", d=256),
        in_=y4g.rearrange("(k d) t -> k d t", d=1024)[:, _ds((pid // 2) * 256, 256),
                                                      _ds(hh * TOK, TOK)]),
        reads=[Rr(y4g)], writes=[Rr(y5l)] + [Rr(y_ap5, c) for c in range(DC)])

    p.push_scope("z5_")
    emit_p5(p, sc, dict(y_ap=y_ap5, x2_dm=x2i, opwT=opwT, opb=opb_in, ln1_w=ln1_in,
                        kv_d_w=kvdw, q_d_w=qdw, rope_k_w=ropekw, k_u_w=kuw, q_u_w=quw,
                        v_u_w=vuw, rope_q_w=ropeqw, cos_t=cos_in, sin_t=sin_in, RT=rt_in,
                        x3_dm=x3i, xin_dm=xini, q5=q5i, k5=k5i, v5=v5i))
    sc.barrier()
    p.pop_scope()

    q6l = p.dram("q6l", [1024, S], BF16)
    k6l = p.dram("k6l", [1024, S], BF16)
    v6l = p.dram("v6l", [S, 1024], BF16)
    def q_srcs6(h):
        return [(0, S, q6l[h * 128:(h + 1) * 128, :])]

    def k_srcs6(h):
        return [(0, S, k6l[h * 128:(h + 1) * 128, :])]

    def v_srcs6(h):
        return [(0, S // 128, v6l.rearrange("(m p) e -> p m e", p=128)[:, :, h * 128:(h + 1) * 128])]

    AG(q5i, q5g, [Rr(q5g)])
    AG(k5i, k5g, [Rr(k5g)])
    AG(v5i, v5g, [Rr(v5g)])
    sc.dma("sync", lambda e: e.dma_start(
        out=q6l.rearrange("d (r t) -> r d t", r=2),
        in_=q5g.rearrange("(r d) t -> r d t", d=D)[_ds(b2, 2), _ds(hs8 * 128, 1024), :]),
        reads=[Rr(q5g)], writes=[Rr(q6l)] + [Rr(q_srcs6, h) for h in range(8)])
    sc.dma("sync", lambda e: e.dma_start(
        out=k6l.rearrange("d (r t) -> r d t", r=2),
        in_=k5g.rearrange("(r d) t -> r d t", d=D)[_ds(b2, 2), _ds(hs8 * 128, 1024), :]),
        reads=[Rr(k5g)], writes=[Rr(k6l)] + [Rr(k_srcs6, h) for h in range(8)])
    sc.dma("sync", lambda e: e.dma_start(
        out=v6l.rearrange("(r s) e -> r s e", r=2),
        in_=v5g.rearrange("(r s) e -> r s e", s=TOK)[_ds(b2, 2), :, _ds(hs8 * 128, 1024)]),
        reads=[Rr(v5g)], writes=[Rr(v6l)] + [Rr(v_srcs6, h) for h in range(8)])

    p.push_scope("z6_")
    emit_p6(p, sc, dict(q_srcs=q_srcs6, k_srcs=k_srcs6, v_srcs=v_srcs6, ao6=ao6i,
                        deps=[Rr(q6l), Rr(k6l), Rr(v6l)]))
    sc.barrier()
    p.pop_scope()

    ao7l = p.dram("ao7l", [D, TOK], BF16)

    def ao_ap7(c):
        return ao7l.rearrange("(c p) t -> p c t", p=128)[:, c]

    AG(ao6i, ao6g, [Rr(ao6g)])
    sc.dma("sync", lambda e: e.dma_start(
        out=ao7l.rearrange("(r d) t -> r d t", r=2),
        in_=ao6g.rearrange("(r d) t -> r d t", d=1024)[_ds(b2, 2), :, _ds(hh * TOK, TOK)]),
        reads=[Rr(ao6g)], writes=[Rr(ao7l)] + [Rr(ao_ap7, c) for c in range(DC)])

    p.push_scope("z7_")
    emit_p7(p, sc, dict(ao_ap=ao_ap7, xin_dm=xini, x3_dm=x3i, o_w=ow_in, ln2_w=ln2_in,
                        sh_gate=shg, sh_up=shu, sh_down=shd, r_gate=rg, r_up=ru, r_down=rd,
                        router_w=rw_in, router_b=rb_in, x5_dm=x5_out))
    sc.barrier()
    p.pop_scope()

    sc.emit()
    return p.done()


def host_fused_inputs(inputs, core):
    b, hh = core // 2, core % 2
    sl = slice(hh * TOK, (hh + 1) * TOK)
    seg = np.ones((1, S), np.float32)
    seg[0, ::128] = 0.0
    RD = 64
    inv_freq = 1.0 / 10000.0 ** (np.arange(0, RD, 2, dtype=np.float32) / RD)
    t_ = (np.arange(TOK, dtype=np.float32) + hh * TOK)
    ang = np.outer(t_, inv_freq)
    emb = np.concatenate([ang, ang], -1)
    cos64 = np.cos(emb).T.astype(np.float32)
    sin64 = np.sin(emb).T.astype(np.float32)
    R64 = np.zeros((64, 64), np.float32)
    for j in range(32):
        R64[j, j + 32] = -1.0
        R64[j + 32, j] = 1.0
    RT = np.zeros((128, 128), np.float32)
    RT[0:64, 0:64] = R64.T
    RT[64:128, 64:128] = R64.T
    dsl = slice(core * 256, (core + 1) * 256)
    return {
        "x_dm": _dm(inputs["L0_attn_x"][b, sl]),
        "Wq": _pk(inputs["L0_attn_Wq"]), "Wk": _pk(inputs["L0_attn_Wk"]),
        "Wv": _pk(inputs["L0_attn_Wv"], 512), "Wf": _pk(inputs["L0_attn_Wf"], 16),
        "seg0": seg,
        "Wo": _pk(inputs["L0_attn_Wo"]),
        "norm_w": np.asarray(inputs["L0_attn_norm_w"]),
        "wqgT": _pk(np.asarray(inputs["L0_ffn_wq_gate"]).T),
        "wquT": _pk(np.asarray(inputs["L0_ffn_wq_up"]).T),
        "wqdT": _pk(np.asarray(inputs["L0_ffn_wq_down"]).T),
        "s_gate": np.asarray(inputs["L0_ffn_s_gate"]),
        "s_up": np.asarray(inputs["L0_ffn_s_up"]),
        "s_down": np.asarray(inputs["L0_ffn_s_down"]),
        "dtwT": _pk(np.asarray(inputs["L1_attn_dt_proj_w"]).T),
        "dtb": np.asarray(inputs["L1_attn_dt_proj_b"]),
        "xpwT": _pk(np.asarray(inputs["L1_attn_x_proj_w"]).T, 32),
        "xpb": np.asarray(inputs["L1_attn_x_proj_b"]),
        "A_sl": np.ascontiguousarray(np.asarray(inputs["L1_attn_A_log"])[dsl].reshape(2, 128, 16)),
        "opwT": _pk(np.asarray(inputs["L1_attn_out_proj_w"]).T),
        "opb": np.asarray(inputs["L1_attn_out_proj_b"]),
        "ln1_w": np.asarray(inputs["L1_ffn_ln1_w"]),
        "kv_d_w": _pk(inputs["L1_ffn_kv_d_w"]),
        "q_d_w": _pk(inputs["L1_ffn_q_d_w"]),
        "rope_k_w": _pk(inputs["L1_ffn_rope_k_w"]),
        "k_u_w": _pk(inputs["L1_ffn_k_u_w"]),
        "q_u_w": _pk(inputs["L1_ffn_q_u_w"]),
        "v_u_w": _pk(inputs["L1_ffn_v_u_w"], 512),
        "rope_q_w": _pk(inputs["L1_ffn_rope_q_w"]),
        "cos_t": np.concatenate([cos64, cos64], 0), "sin_t": np.concatenate([sin64, sin64], 0),
        "RT": RT,
        "o_w": _pk(inputs["L1_ffn_o_w"]),
        "ln2_w": np.asarray(inputs["L1_ffn_ln2_w"]),
        "sh_gate": _pk(inputs["L1_ffn_sh_gate"]),
        "sh_up": _pk(inputs["L1_ffn_sh_up"]),
        "sh_down": _pk(inputs["L1_ffn_sh_down"]),
        "r_gate": np.stack([_pk(np.asarray(inputs["L1_ffn_r_gate"])[e]) for e in range(7)]),
        "r_up": np.stack([_pk(np.asarray(inputs["L1_ffn_r_up"])[e]) for e in range(7)]),
        "r_down": np.stack([_pk(np.asarray(inputs["L1_ffn_r_down"])[e]) for e in range(7)]),
        "router_w": _pk(inputs["L1_ffn_router_w"], 7),
        "router_b": np.asarray(inputs["L1_ffn_router_b"]),
    }


_BUILT = {}


def kernel(**inputs):
    if "fz" not in _BUILT:
        _BUILT["fz"] = build_fused()
    nc = _BUILT["fz"]
    in_maps = [host_fused_inputs(inputs, c) for c in range(NCORE)]
    res = run_bass_kernel_spmd(nc, in_maps, core_ids=list(range(NCORE))).results
    out = np.empty((B, S, D), np.float32)
    for c in range(NCORE):
        b, hh = c // 2, c % 2
        out[b, hh * TOK:(hh + 1) * TOK] = np.asarray(res[c]["x5_dm"]).T
    return out


# version 11
# speedup vs baseline: 1.1109x; 1.1109x over previous
import numpy as np
import ml_dtypes
from contextlib import ExitStack
import concourse.bass as bass
import concourse.mybir as mybir
from concourse.bass_utils import run_bass_kernel_spmd

F32 = mybir.dt.float32
BF16 = mybir.dt.bfloat16
I32 = mybir.dt.int32
AF = mybir.ActivationFunctionType
ALU = mybir.AluOpType
AX = mybir.AxisListType.X

B, S, D = 4, 4096, 2048
NCORE = 8
TOK = (B * S) // NCORE
DC = D // 128
NTT = TOK // 512
LNEPS = 1e-5


def _dm(x2d):
    return np.ascontiguousarray(np.asarray(x2d).T)


def _pk(w, ocw=128):
    w = np.asarray(w)
    din, dout = w.shape
    dc, noc = din // 128, dout // ocw
    return np.ascontiguousarray(
        w.reshape(dc, 128, noc, ocw).transpose(2, 1, 0, 3).reshape(noc, 128, dc * ocw))


class P:
    def __init__(self, name):
        self.nc = bass.Bass("TRN2", target_bir_lowering=False, debug=False)
        self.es = ExitStack()
        self.name = name
        self.tag = ""
        self.scopes = [self.es]
        self.keep = []

    def push_scope(self, tag=""):
        self.tag = tag
        self.scopes.append(ExitStack())

    def pop_scope(self):
        self.scopes.pop().close()
        self.tag = ""

    def inp(self, name, shape, dt=F32):
        return self.nc.dram_tensor(name, shape, dt, kind="ExternalInput").ap()

    def out(self, name, shape, dt=F32):
        return self.nc.dram_tensor(name, shape, dt, kind="ExternalOutput").ap()

    def dram(self, name, shape, dt=F32, **kw):
        return self.nc.dram_tensor(self.tag + name, shape, dt, kind="Internal", **kw).ap()

    def sb(self, name, shape, dt=F32):
        h = self.scopes[-1].enter_context(self.nc.sbuf_tensor(self.tag + name, shape, dt))
        self.keep.append(h)
        return h

    def ps(self, name, shape, dt=F32):
        h = self.scopes[-1].enter_context(self.nc.psum_tensor(self.tag + name, shape, dt))
        self.keep.append(h)
        return h

    def sem(self, name):
        return self.es.enter_context(self.nc.semaphore(self.tag + name))

    def done(self):
        while len(self.scopes) > 1:
            self.pop_scope()
        self.es.close()
        return self.nc


def emit_p1(p, sc, io):
    nc = p.nc
    x_dm = io["x_dm"]
    Wq = io["Wq"]; Wk = io["Wk"]; Wv = io["Wv"]; Wf = io["Wf"]
    q_dm = io["q_dm"]; k_dm = io["k_dm"]; v_tm = io["v_tm"]; ld_dm = io["ld_dm"]

    xt = p.sb("xt", [128, 2, DC, 512], F32)
    xsq = p.sb("xsq", [128, 2, 512], F32)
    xn = p.sb("xn", [128, DC, TOK], BF16)
    ones_c = p.sb("ones_c", [128, 1], F32)
    ones_r = p.sb("ones_r", [1, 128], F32)
    srow = p.sb("srow", [1, 6, 512], F32)
    wsl = p.sb("wsl", [128, 2, DC, 128], BF16)
    wvsl = p.sb("wvsl", [128, DC, 512], BF16)
    wfsl = p.sb("wfsl", [128, DC, 16], BF16)
    ostg = p.sb("ostg", [128, 2, TOK], BF16)
    vstg = p.sb("vstg", [128, 2, 512], BF16)
    ldsb = p.sb("ldsb", [16, TOK], F32)
    lds1 = p.sb("lds1", [16, NTT, 512], F32)
    lds2 = p.sb("lds2", [16, NTT, 512], F32)

    p_s = p.ps("p_s", [1, 512], F32)
    p_q = p.ps("p_q", [1, 512], F32)
    p_r = p.ps("p_r", [128, 512], F32)
    p_m = p.ps("p_m", [128, 512], F32)
    p_o = p.ps("p_o", [128, 2, 512], F32)
    p_l = p.ps("p_l", [16, 512], F32)

    s_x = [p.sem("s_x0"), p.sem("s_x1")]
    s_w = [p.sem("s_w0"), p.sem("s_w1")]
    s_wv = p.sem("s_wv")
    s_wf = p.sem("s_wf")
    s_sq = p.sem("s_sq")
    s_stat = p.sem("s_stat")
    s_fva = p.sem("s_fva")
    s_fsc = p.sem("s_fsc")
    s_fin = p.sem("s_fin")
    s_bc = p.sem("s_bc")
    s_xn = p.sem("s_xn")
    s_pj = p.sem("s_pj")
    s_evS = p.sem("s_evS")
    s_evV = p.sem("s_evV")
    s_stq = [p.sem("s_stq0"), p.sem("s_stq1")]
    s_stl = p.sem("s_stl")
    s_stv = [p.sem("s_stv0"), p.sem("s_stv1")]
    s_one = p.sem("s_one")

    NQK = 2 * 16 * NTT

    if True:
        def _f_sync(sync):
            for tt in range(NTT):
                if tt >= 2:
                    sync.wait_ge(s_xn, tt - 1)
                sync.dma_start(
                    out=xt[:, tt % 2],
                    in_=x_dm.rearrange("(c p) (t f) -> p c t f", p=128, f=512)[:, :, tt],
                ).then_inc(s_x[tt % 2], 16)
            for wi, od in enumerate([q_dm, k_dm]):
                for oc in range(16):
                    idx = wi * 16 + oc
                    sync.wait_ge(s_evS, (idx + 1) * NTT)
                    sync.dma_start(out=od[oc * 128:(oc + 1) * 128, :], in_=ostg[:, idx % 2])\
                        .then_inc(s_stq[idx % 2], 16)
            sync.wait_ge(s_evV, NTT)
            sync.dma_start(out=ld_dm[:], in_=ldsb[:]).then_inc(s_stl, 16)
            for vc in range(4):
                for to in range(16):
                    vi = vc * 16 + to
                    sync.wait_ge(s_evV, NTT + vi + 1)
                    sync.dma_start(
                        out=v_tm[to * 128:(to + 1) * 128, vc * 512:(vc + 1) * 512],
                        in_=vstg[:, vi % 2],
                    ).then_inc(s_stv[vi % 2], 16)

            return sync.nop()

        def _f_gpsimd(gpsimd):
            gpsimd.dma_start(out=wfsl[:], in_=Wf[0].rearrange("p (k f) -> p k f", f=16)).then_inc(s_wf, 16)
            for wi, W in enumerate([Wq, Wk]):
                for oc in range(16):
                    idx = wi * 16 + oc
                    if idx >= 2:
                        gpsimd.wait_ge(s_pj, (idx - 1) * NTT)
                    gpsimd.dma_start(
                        out=wsl[:, idx % 2],
                        in_=W[oc].rearrange("p (k f) -> p k f", f=128),
                    ).then_inc(s_w[idx % 2], 16)
            for vc in range(4):
                if vc > 0:
                    gpsimd.wait_ge(s_pj, NQK + NTT + vc * 16)
                gpsimd.dma_start(
                    out=wvsl[:],
                    in_=Wv[vc].rearrange("p (k f) -> p k f", f=512),
                ).then_inc(s_wv, 16)
            gpsimd.wait_ge(s_stq[0], 16 * 16)
            gpsimd.wait_ge(s_stq[1], 16 * 16)
            gpsimd.wait_ge(s_stl, 16)
            gpsimd.wait_ge(s_stv[0], 32 * 16)
            gpsimd.wait_ge(s_stv[1], 32 * 16)

            return gpsimd.nop()

        def _f_scalar(scalar):
            for tt in range(NTT):
                scalar.wait_ge(s_x[tt % 2], 16 * (tt // 2 + 1))
                for c in range(DC):
                    n = tt * DC + c
                    if n >= 2:
                        scalar.wait_ge(s_stat, n - 1)
                    scalar.activation(out=xsq[:, n % 2], in_=xt[:, tt % 2, c], func=AF.Square).then_inc(s_sq, 1)
                scalar.wait_ge(s_fva, tt + 1)
                scalar.activation(out=srow[:, 3], in_=srow[:, 2], func=AF.Ln)
                scalar.drain()
                scalar.activation(out=srow[:, 3], in_=srow[:, 3], func=AF.Exp, scale=-0.5).then_inc(s_fsc, 1)
            for wi in range(2):
                for oc in range(16):
                    idx = wi * 16 + oc
                    for tt in range(NTT):
                        i = idx * NTT + tt
                        scalar.wait_ge(s_pj, i + 1)
                        if idx >= 2 and tt == 0:
                            scalar.wait_ge(s_stq[idx % 2], 16 * (idx // 2))
                        scalar.activation(
                            out=ostg[:, idx % 2, tt * 512:(tt + 1) * 512], in_=p_o[:, i % 2], func=AF.Silu,
                        ).then_inc(s_evS, 1)
            for tt in range(NTT):
                i = NQK + tt
                scalar.wait_ge(s_pj, i + 1)
                scalar.activation(out=lds1[:, tt], in_=p_l[:], func=AF.Relu, scale=-1.0)
                scalar.activation(out=lds2[:, tt], in_=p_l[:], func=AF.Abs)
                scalar.drain()
                scalar.activation(out=lds2[:, tt], in_=lds2[:, tt], func=AF.Exp, scale=-1.0)
                scalar.drain()
                scalar.activation(out=lds2[:, tt], in_=lds2[:, tt], func=AF.Ln, bias=1.0).then_inc(s_evS, 1)

            return scalar.nop()

        def _f_tensor(tensor):
            tensor.wait_ge(s_one, 1)
            for tt in range(NTT):
                if tt > 0:
                    tensor.wait_ge(s_fin, tt)
                for c in range(DC):
                    n = tt * DC + c
                    tensor.wait_ge(s_sq, n + 1)
                    tensor.matmul(p_s[:], ones_c[:], xt[:, tt % 2, c], start=(c == 0), stop=(c == DC - 1))
                    tensor.matmul(p_q[:], ones_c[:], xsq[:, n % 2], start=(c == 0), stop=(c == DC - 1)).then_inc(s_stat, 1)
                tensor.wait_ge(s_fin, tt + 1)
                if tt > 0:
                    tensor.wait_ge(s_xn, tt)
                tensor.matmul(p_r[:], ones_r[:], srow[:, 3], start=True, stop=True)
                tensor.matmul(p_m[:], ones_r[:], srow[:, 4], start=True, stop=True).then_inc(s_bc, 1)
            tensor.wait_ge(s_xn, NTT)
            for wi in range(2):
                for oc in range(16):
                    idx = wi * 16 + oc
                    tensor.wait_ge(s_w[idx % 2], 16 * (idx // 2 + 1))
                    for tt in range(NTT):
                        i = idx * NTT + tt
                        if i >= 2:
                            tensor.wait_ge(s_evS, i - 1)
                        for c in range(DC):
                            mi = tensor.matmul(p_o[:, i % 2], wsl[:, idx % 2, c],
                                               xn[:, c, tt * 512:(tt + 1) * 512],
                                               start=(c == 0), stop=(c == DC - 1))
                        mi.then_inc(s_pj, 1)
            tensor.wait_ge(s_wf, 16)
            for tt in range(NTT):
                if tt >= 1:
                    tensor.wait_ge(s_evS, NQK + tt)
                for c in range(DC):
                    mi = tensor.matmul(p_l[:], wfsl[:, c], xn[:, c, tt * 512:(tt + 1) * 512],
                                  start=(c == 0), stop=(c == DC - 1))
                mi.then_inc(s_pj, 1)
            for vc in range(4):
                tensor.wait_ge(s_wv, 16 * (vc + 1))
                for to in range(16):
                    vi = vc * 16 + to
                    i = NQK + NTT + vi
                    if vi == 0:
                        tensor.wait_ge(s_evS, NQK - 1)
                    elif vi == 1:
                        tensor.wait_ge(s_evS, NQK)
                    else:
                        tensor.wait_ge(s_evV, NTT + vi - 1)
                    for c in range(DC):
                        mi = tensor.matmul(p_o[:, i % 2], xn[:, c, to * 128:(to + 1) * 128], wvsl[:, c],
                                      start=(c == 0), stop=(c == DC - 1))
                    mi.then_inc(s_pj, 1)

            return tensor.nop()

        def _f_vector(vector):
            vector.memset(ones_c[:], 1.0)
            vector.memset(ones_r[:], 1.0).then_inc(s_one, 1)
            for tt in range(NTT):
                vector.wait_ge(s_stat, (tt + 1) * DC)
                vector.tensor_scalar(out=srow[:, 0], in0=p_s[:], scalar1=1.0 / D, scalar2=None, op0=ALU.mult)
                vector.tensor_scalar(out=srow[:, 1], in0=p_q[:], scalar1=1.0 / D, scalar2=None, op0=ALU.mult)
                vector.drain()
                vector.tensor_tensor(out=srow[:, 2], in0=srow[:, 0], in1=srow[:, 0], op=ALU.mult)
                vector.drain()
                vector.tensor_tensor(out=srow[:, 2], in0=srow[:, 1], in1=srow[:, 2], op=ALU.subtract)
                vector.drain()
                vector.tensor_scalar(out=srow[:, 2], in0=srow[:, 2], scalar1=LNEPS, scalar2=None, op0=ALU.add)\
                    .then_inc(s_fva, 1)
                vector.wait_ge(s_fsc, tt + 1)
                vector.tensor_scalar(out=srow[:, 4], in0=srow[:, 0], scalar1=-1.0, scalar2=None,
                                     op0=ALU.mult).then_inc(s_fin, 1)
                vector.wait_ge(s_bc, tt + 1)
                for c in range(DC):
                    vector.tensor_tensor(out=xt[:, tt % 2, c], in0=xt[:, tt % 2, c], in1=p_m[:], op=ALU.add)
                    vector.drain()
                    ti = vector.tensor_tensor(out=xn[:, c, tt * 512:(tt + 1) * 512], in0=xt[:, tt % 2, c],
                                              in1=p_r[:], op=ALU.mult)
                    if c == DC - 1:
                        ti.then_inc(s_xn, 1)
            for tt in range(NTT):
                vector.wait_ge(s_evS, NQK + tt + 1)
                vector.scalar_tensor_tensor(out=ldsb[:, tt * 512:(tt + 1) * 512], in0=lds1[:, tt], scalar=-1.0,
                                            in1=lds2[:, tt], op0=ALU.mult, op1=ALU.subtract).then_inc(s_evV, 1)
            for vc in range(4):
                for to in range(16):
                    vi = vc * 16 + to
                    i = NQK + NTT + vi
                    vector.wait_ge(s_pj, i + 1)
                    if vi >= 2:
                        vector.wait_ge(s_stv[vi % 2], 16 * (vi // 2))
                    vector.tensor_copy(out=vstg[:, vi % 2], in_=p_o[:, i % 2]).then_inc(s_evV, 1)
            return vector.nop()

        deps = io.get("deps", [])
        for nm, f in [("sync", _f_sync), ("gpsimd", _f_gpsimd), ("scalar", _f_scalar),
                      ("tensor", _f_tensor), ("vector", _f_vector)]:
            sc.op(nm, f, reads=deps)


def build_p1():
    p = P("p1")
    sc = Sched(p)
    io = dict(
        x_dm=p.inp("x_dm", [D, TOK]),
        Wq=p.inp("Wq", [16, 128, DC * 128]), Wk=p.inp("Wk", [16, 128, DC * 128]),
        Wv=p.inp("Wv", [4, 128, DC * 512]),
        Wf=p.inp("Wf", [1, 128, DC * 16]),
        q_dm=p.out("q_dm", [D, TOK], BF16),
        k_dm=p.out("k_dm", [D, TOK], BF16),
        v_tm=p.out("v_tm", [TOK, D], BF16),
        ld_dm=p.out("ld_dm", [16, TOK]),
    )
    emit_p1(p, sc, io)
    sc.emit()
    return p.done()


def host_p1_inputs(inputs, core):
    b, hh = core // 2, core % 2
    xs = inputs["L0_attn_x"][b, hh * TOK:(hh + 1) * TOK]
    return {
        "x_dm": _dm(xs),
        "Wq": _pk(inputs["L0_attn_Wq"]), "Wk": _pk(inputs["L0_attn_Wk"]),
        "Wv": _pk(inputs["L0_attn_Wv"], 512), "Wf": _pk(inputs["L0_attn_Wf"], 16),
    }


def emit_p2(p, sc, io):
    NCH = S // 128
    nc = p.nc
    q_srcs = io["q_srcs"]; k_srcs = io["k_srcs"]; v_srcs = io["v_srcs"]; ld_srcs = io["ld_srcs"]
    seg_in = io["seg0"]
    o_out = io["o_dm"]

    qh = p.sb("qh", [128, 2, S], BF16)
    kh = p.sb("kh", [128, 2, S], BF16)
    vh = p.sb("vh", [128, 2, NCH, 128], BF16)
    oh = p.sb("oh", [128, 2, S], F32)
    ldrow = p.sb("ldrow", [1, 2, S], F32)
    seg0 = p.sb("seg0_sb", [1, S], F32)
    relc = p.sb("relc", [1, 2, S], F32)
    expq = p.sb("expq", [1, 2, S], F32)
    rowK = p.sb("rowK", [1, 2, 128], F32)
    tiny = p.sb("tiny", [1, 2, 4], F32)
    relj_sb = p.sb("relj_sb", [128, 2, 1], F32)
    argm_sb = p.sb("argm_sb", [128, 2, 128], F32)
    D_sb = p.sb("D_sb", [128, 2, 128], F32)
    qd_sb = p.sb("qd_sb", [128, 2, 128], BF16)
    kd_sb = p.sb("kd_sb", [128, 2, 128], BF16)
    kdT_sb = p.sb("kdT_sb", [128, 2, 128], BF16)
    probs_sb = p.sb("probs_sb", [128, 2, 128], BF16)
    etc_sb = p.sb("etc_sb", [128, 2, 1], F32)
    S_sb = p.sb("S_sb", [128, 128], F32)
    S_bf = p.sb("S_bf", [128, 128], BF16)
    ident = p.sb("ident", [128, 128], BF16)
    identf = p.sb("identf", [1, 1], F32)
    ones_r = p.sb("ones_r", [1, 128], F32)

    p_sc = [p.ps("p_sc0", [128, 128], F32), p.ps("p_sc1", [128, 128], F32)]
    p_msc = [p.ps("p_msc0", [128, 4, 128], F32), p.ps("p_msc1", [128, 4, 128], F32)]
    p_o = [p.ps("p_o0", [128, 128], F32), p.ps("p_o1", [128, 128], F32)]
    p_T = p.ps("p_T", [128, 128], F32)
    p_kdT = p.ps("p_kdT", [128, 128], BF16)

    s_ld = [p.sem("s_ld0"), p.sem("s_ld1")]
    s_ldr = [p.sem("s_ldr0"), p.sem("s_ldr1")]
    s_seg = p.sem("s_seg")
    s_one = p.sem("s_one")
    s_rel = p.sem("s_rel")
    s_expq = p.sem("s_expq")
    s_row = p.sem("s_row")
    s_msc = p.sem("s_msc")
    s_arg = p.sem("s_arg")
    s_dsb = p.sem("s_dsb")
    s_dm = p.sem("s_dm")
    s_sc = p.sem("s_sc")
    s_pb = p.sem("s_pb")
    s_qd = p.sem("s_qd")
    s_kd = p.sem("s_kd")
    s_etc = p.sem("s_etc")
    s_kdT = p.sem("s_kdT")
    s_kdte = p.sem("s_kdte")
    s_o = p.sem("s_o")
    s_T = p.sem("s_T")
    s_S = p.sem("s_S")
    s_oe = p.sem("s_oe")
    s_id = p.sem("s_id")
    s_st = [p.sem("s_st0"), p.sem("s_st1")]

    def IT(h, m):
        return h * NCH + m

    NLD = {}
    if True:
        def _f_sync(sync):
            sync.dma_start(out=seg0[:], in_=seg_in[:]).then_inc(s_seg, 16)
            for h in range(8):
                if h >= 2:
                    sync.wait_ge(s_oe, NCH * (h - 1))
                    sync.wait_ge(s_T, NCH * (h - 1))
                    sync.wait_ge(s_sc, NCH * (h - 1))
                    sync.wait_ge(s_qd, NCH * (h - 1))
                    sync.wait_ge(s_kd, NCH * (h - 1))
                    sync.wait_ge(s_rel, h - 1)
                n = 0
                for col0, w, src in ld_srcs(h):
                    sync.dma_start(out=ldrow[0:1, h % 2, col0:col0 + w], in_=src).then_inc(s_ldr[h % 2], 16)
                    n += 0
                for col0, w, src in q_srcs(h):
                    sync.dma_start(out=qh[:, h % 2, col0:col0 + w], in_=src).then_inc(s_ld[h % 2], 16)
                    n += 1
                for col0, w, src in k_srcs(h):
                    sync.dma_start(out=kh[:, h % 2, col0:col0 + w], in_=src).then_inc(s_ld[h % 2], 16)
                    n += 1
                for m0, mc, src in v_srcs(h):
                    sync.dma_start(out=vh[:, h % 2, m0:m0 + mc], in_=src).then_inc(s_ld[h % 2], 16)
                    n += 1
                NLD.setdefault("n", n)
                NLD.setdefault("nldr", len(ld_srcs(h)))
            return sync.nop()

        def _f_gpsimd(gpsimd):
            gpsimd.memset(ident[:], 1.0)
            gpsimd.drain()
            gpsimd.affine_select(out=ident[:], in_=ident[:], pattern=[[-1, 128]],
                                 compare_op=ALU.is_equal, fill=0.0,
                                 base=0, channel_multiplier=1).then_inc(s_id, 1)
            for h in range(8):
                for m in range(NCH):
                    it = IT(h, m)
                    gpsimd.wait_ge(s_dsb, it + 1)
                    gpsimd.affine_select(out=D_sb[:, m % 2], in_=D_sb[:, m % 2], pattern=[[1, 128]],
                                         compare_op=ALU.is_ge, fill=0.0,
                                         base=0, channel_multiplier=-1).then_inc(s_dm, 1)
                gpsimd.wait_ge(s_oe, NCH * (h + 1))
                gpsimd.dma_start(out=o_out[h * 128:(h + 1) * 128, :], in_=oh[:, h % 2]).then_inc(s_st[h % 2], 16)
            gpsimd.wait_ge(s_st[0], 16 * 4)
            gpsimd.wait_ge(s_st[1], 16 * 4)
            return gpsimd.nop()

        def _f_vector(vector):
            vector.memset(ones_r[:], 1.0)
            vector.memset(identf[:], 1.0).then_inc(s_one, 1)
            vector.wait_ge(s_seg, 16)
            for h in range(8):
                vector.wait_ge(s_ldr[h % 2], 16 * NLD["nldr"] * (h // 2 + 1))
                if h >= 2:
                    vector.wait_ge(s_row, NCH * (h - 1))
                    vector.wait_ge(s_msc, NCH * (h - 1))
                vector.tensor_tensor_scan(
                    out=relc[:, h % 2], data0=seg0[:], data1=ldrow[:, h % 2], initial=0.0,
                    op0=ALU.mult, op1=ALU.add,
                ).then_inc(s_rel, 1)
                vector.drain()
                for m in range(NCH):
                    it = IT(h, m)
                    i0 = m * 128
                    vector.wait_ge(s_msc, it + 1)
                    if it >= 2:
                        vector.wait_ge(s_dsb, it - 1)
                    vector.tensor_copy(out=relj_sb[:, m % 2], in_=p_msc[m % 2][:, 3, 1:2])
                    vector.drain()
                    vector.tensor_scalar(out=argm_sb[:, m % 2], in0=p_msc[m % 2][:, 0],
                                         scalar1=relj_sb[:, m % 2], scalar2=0.0,
                                         op0=ALU.subtract, op1=ALU.min).then_inc(s_arg, 1)
                    if it >= 2:
                        vector.wait_ge(s_o, it - 1)
                    vector.tensor_tensor(out=qd_sb[:, m % 2], in0=qh[:, h % 2, i0:i0 + 128],
                                         in1=p_msc[m % 2][:, 1], op=ALU.mult).then_inc(s_qd, 1)
                    if it >= 2:
                        vector.wait_ge(s_kdT, it - 1)
                    vector.tensor_tensor(out=kd_sb[:, m % 2], in0=kh[:, h % 2, i0:i0 + 128],
                                         in1=p_msc[m % 2][:, 2], op=ALU.mult).then_inc(s_kd, 1)
                    vector.tensor_copy(out=etc_sb[:, m % 2], in_=p_msc[m % 2][:, 3, 0:1]).then_inc(s_etc, 1)
                    vector.wait_ge(s_sc, it + 1)
                    vector.wait_ge(s_dm, it + 1)
                    vector.tensor_tensor(out=probs_sb[:, m % 2], in0=p_sc[m % 2][:],
                                         in1=D_sb[:, m % 2], op=ALU.mult).then_inc(s_pb, 1)
                    vector.wait_ge(s_kdT, it + 1)
                    vector.tensor_copy(out=kdT_sb[:, m % 2], in_=p_kdT[:]).then_inc(s_kdte, 1)
                    vector.wait_ge(s_o, it + 1)
                    if m == 0 and h >= 2:
                        vector.wait_ge(s_st[h % 2], 16 * (h // 2))
                    vector.tensor_copy(out=oh[:, h % 2, i0:i0 + 128], in_=p_o[m % 2][:]).then_inc(s_oe, 1)
                    vector.wait_ge(s_T, it + 1)
                    if m == 0:
                        vector.tensor_copy(out=S_sb[:], in_=p_T[:])
                        vector.tensor_copy(out=S_bf[:], in_=p_T[:]).then_inc(s_S, 1)
                    else:
                        vector.drain()
                        vector.scalar_tensor_tensor(out=S_sb[:], in0=S_sb[:], scalar=etc_sb[:, m % 2],
                                                    in1=p_T[:], op0=ALU.mult, op1=ALU.add)
                        vector.drain()
                        vector.tensor_copy(out=S_bf[:], in_=S_sb[:]).then_inc(s_S, 1)

            return vector.nop()

        def _f_scalar(scalar):
            for h in range(8):
                scalar.wait_ge(s_rel, h + 1)
                if h >= 2:
                    scalar.wait_ge(s_msc, NCH * (h - 1))
                scalar.activation(out=expq[:, h % 2], in_=relc[:, h % 2], func=AF.Exp).then_inc(s_expq, 1)
                for m in range(NCH):
                    it = IT(h, m)
                    i0 = m * 128
                    if it >= 2:
                        scalar.wait_ge(s_msc, it - 1)
                    scalar.activation(out=rowK[:, m % 2], in_=relc[:, h % 2, i0:i0 + 128], func=AF.Exp,
                                      scale=-1.0, bias=relc[:, h % 2, i0 + 127:i0 + 128])
                    scalar.activation(out=tiny[:, m % 2, 2:3], in_=relc[:, h % 2, i0 + 127:i0 + 128],
                                      func=AF.Exp).then_inc(s_row, 1)
                    scalar.wait_ge(s_arg, it + 1)
                    if it >= 2:
                        scalar.wait_ge(s_dm, it - 1)
                    scalar.activation(out=D_sb[:, m % 2], in_=argm_sb[:, m % 2], func=AF.Exp).then_inc(s_dsb, 1)

            return scalar.nop()

        def _f_tensor(tensor):
            tensor.wait_ge(s_one, 1)
            tensor.wait_ge(s_id, 1)
            for h in range(8):
                tensor.wait_ge(s_ld[h % 2], 16 * NLD["n"] * (h // 2 + 1))
                tensor.wait_ge(s_expq, h + 1)
                for m in range(NCH):
                    it = IT(h, m)
                    i0 = m * 128
                    if it >= 2:
                        tensor.wait_ge(s_pb, it - 1)
                    tensor.matmul(p_sc[m % 2][:], kh[:, h % 2, i0:i0 + 128], qh[:, h % 2, i0:i0 + 128],
                                  start=True, stop=True).then_inc(s_sc, 1)
                    tensor.wait_ge(s_row, it + 1)
                    if it >= 2:
                        tensor.wait_ge(s_qd, it - 1)
                        tensor.wait_ge(s_kd, it - 1)
                        tensor.wait_ge(s_etc, it - 1)
                        tensor.wait_ge(s_arg, it - 1)
                    tensor.matmul(p_msc[m % 2][:, 0], ones_r[:], relc[:, h % 2, i0:i0 + 128],
                                  start=True, stop=True)
                    tensor.matmul(p_msc[m % 2][:, 1], ones_r[:], expq[:, h % 2, i0:i0 + 128],
                                  start=True, stop=True)
                    tensor.matmul(p_msc[m % 2][:, 2], ones_r[:], rowK[:, m % 2], start=True, stop=True)
                    tensor.matmul(p_msc[m % 2][:, 3, 0:1], ones_r[:], tiny[:, m % 2, 2:3],
                                  start=True, stop=True)
                    tensor.transpose(p_msc[m % 2][:, 3, 1:2], relc[:, h % 2, i0:i0 + 128],
                                     identf[:]).then_inc(s_msc, 1)
                    tensor.wait_ge(s_kd, it + 1)
                    if it >= 1:
                        tensor.wait_ge(s_kdte, it)
                    tensor.transpose(p_kdT[:], kd_sb[:, m % 2], ident[:]).then_inc(s_kdT, 1)
                    tensor.wait_ge(s_pb, it + 1)
                    if it >= 2:
                        tensor.wait_ge(s_oe, it - 1)
                    if m == 0:
                        tensor.matmul(p_o[m % 2][:], vh[:, h % 2, m], probs_sb[:, m % 2],
                                      start=True, stop=True).then_inc(s_o, 1)
                    else:
                        tensor.matmul(p_o[m % 2][:], vh[:, h % 2, m], probs_sb[:, m % 2],
                                      start=True, stop=False)
                        tensor.wait_ge(s_qd, it + 1)
                        tensor.wait_ge(s_S, it)
                        tensor.matmul(p_o[m % 2][:], S_bf[:], qd_sb[:, m % 2],
                                      start=False, stop=True).then_inc(s_o, 1)
                    tensor.wait_ge(s_kdte, it + 1)
                    tensor.wait_ge(s_S, it)
                    tensor.matmul(p_T[:], kdT_sb[:, m % 2], vh[:, h % 2, m],
                                  start=True, stop=True).then_inc(s_T, 1)
            return tensor.nop()

        deps = io.get("deps", [])
        for nm, f in [("sync", _f_sync), ("gpsimd", _f_gpsimd), ("scalar", _f_scalar),
                      ("tensor", _f_tensor), ("vector", _f_vector)]:
            sc.op(nm, f, reads=deps)


def build_p2():
    p = P("p2")
    sc = Sched(p)
    q_in = p.inp("q_dm", [1024, S], BF16)
    k_in = p.inp("k_dm", [1024, S], BF16)
    v_in = p.inp("v_tm", [S, 1024], BF16)
    ld_in = p.inp("ld8", [8, S])
    io = dict(
        q_srcs=lambda h: [(0, S, q_in[h * 128:(h + 1) * 128, :])],
        k_srcs=lambda h: [(0, S, k_in[h * 128:(h + 1) * 128, :])],
        v_srcs=lambda h: [(0, S // 128,
                           v_in.rearrange("(m p) e -> p m e", p=128)[:, :, h * 128:(h + 1) * 128])],
        ld_srcs=lambda h: [(0, S, ld_in[h:h + 1, :])],
        seg0=p.inp("seg0", [1, S]),
        o_dm=p.out("o_dm", [1024, S], BF16),
    )
    emit_p2(p, sc, io)
    sc.emit()
    return p.done()


def host_p2_inputs(p1res, inputs, core):
    b, hs = core // 2, (core % 2) * 8
    cA, cB = 2 * b, 2 * b + 1
    q = np.concatenate([p1res[cA]["q_dm"], p1res[cB]["q_dm"]], axis=1)
    k = np.concatenate([p1res[cA]["k_dm"], p1res[cB]["k_dm"]], axis=1)
    v = np.concatenate([p1res[cA]["v_tm"], p1res[cB]["v_tm"]], axis=0)
    ld = np.concatenate([p1res[cA]["ld_dm"], p1res[cB]["ld_dm"]], axis=1)
    seg = np.ones((1, S), np.float32)
    seg[0, ::128] = 0.0
    return {
        "q_dm": np.ascontiguousarray(q[hs * 128:(hs + 8) * 128]),
        "k_dm": np.ascontiguousarray(k[hs * 128:(hs + 8) * 128]),
        "v_tm": np.ascontiguousarray(v[:, hs * 128:(hs + 8) * 128]),
        "ld8": np.ascontiguousarray(ld[hs:hs + 8]),
        "seg0": seg,
    }


class Rr:
    def __init__(self, t, key=None):
        self.k = (id(t), key)


class Sched:
    CENG = ("tensor", "vector", "scalar", "gpsimd", "sync")
    DRAIN_ENG = ("vector", "scalar", "gpsimd")

    def __init__(self, p):
        self.p = p
        self.ops = []
        self.excl_ids = set()

    def exclusive(self, *tensors):
        for t in tensors:
            self.excl_ids.add(id(t))

    def op(self, engine, fn, reads=(), writes=()):
        self.ops.append(dict(
            eng=engine, fn=fn, reads=[r.k for r in reads], writes=[w.k for w in writes],
            dma=False, cc=False, barrier=False))

    def dma(self, queue, fn, reads=(), writes=(), slot=None):
        rs = [r.k for r in reads]
        ws = [w.k for w in writes]
        sk = slot.k if slot is not None else (ws[0] if ws else rs[0])
        self.ops.append(dict(
            eng=queue, fn=fn, reads=rs, writes=ws, dma=True, cc=False, barrier=False,
            slot=(sk, queue)))

    def cc(self, fn, reads=(), writes=()):
        self.ops.append(dict(
            eng="gpsimd", fn=fn, reads=[r.k for r in reads], writes=[w.k for w in writes],
            dma=False, cc=True, barrier=False))

    def barrier(self):
        self.ops.append(dict(eng="gpsimd", fn=None, reads=[], writes=[], dma=False,
                             cc=False, barrier=True))

    def emit(self):
        import os
        frac = float(os.environ.get("MK_TRUNC", "1.0"))
        if frac > 1.0:
            self.ops = self.ops[:int(frac)]
        elif frac < 1.0:
            self.ops = self.ops[:int(len(self.ops) * frac)]
        ops = self.ops
        last_writer = {}
        readers_since = {}
        prev_dma = {}
        last_reader = {}
        last_eng_op = {}
        all_since_barrier = []
        for i, o in enumerate(ops):
            deps = set()
            if o["barrier"]:
                deps.update(all_since_barrier)
                o["deps"] = deps
                all_since_barrier = [i]
                last_eng_op[o["eng"]] = i
                continue
            if all_since_barrier and ops[all_since_barrier[0]]["barrier"]:
                deps.add(all_since_barrier[0])
            if o["dma"]:
                if o["slot"] in prev_dma:
                    deps.add(prev_dma[o["slot"]])
                prev_dma[o["slot"]] = i
            if o["cc"]:
                if "cc_prev" in last_eng_op:
                    deps.add(last_eng_op["cc_prev"])
                last_eng_op["cc_prev"] = i
            for r in o["reads"]:
                if r in last_writer:
                    deps.add(last_writer[r])
                if r[0] in self.excl_ids:
                    if r in last_reader:
                        deps.add(last_reader[r])
                    last_reader[r] = i
            for w in o["writes"]:
                if w in last_writer:
                    deps.add(last_writer[w])
                for rd in readers_since.get(w, ()):
                    deps.add(rd)
            o["deps"] = deps
            for r in o["reads"]:
                readers_since.setdefault(r, []).append(i)
            for w in o["writes"]:
                last_writer[w] = i
                readers_since[w] = []
            last_eng_op[o["eng"]] = i
            all_since_barrier.append(i)
        needed = set()
        for i, o in enumerate(ops):
            if o["barrier"]:
                needed.update(o["deps"])
                continue
            for d in o["deps"]:
                od = ops[d]
                if od["dma"] or od["cc"] or od["barrier"] or od["eng"] != o["eng"]:
                    needed.add(d)
                elif o["dma"]:
                    needed.add(d)
        eng_sems = {e: self.p.sem(f"sch_{e}") for e in self.CENG}
        eng_rank = {e: 0 for e in self.CENG}
        cc_sem = self.p.sem("sch_cc")
        cc_cnt = 0
        slot_sems = {}
        free_sems = {"sync": [], "gpsimd": []}
        n_slot_sems = 0
        token = {}
        for i, o in enumerate(ops):
            if o["barrier"]:
                for sk, (sem, cnt) in slot_sems.items():
                    free_sems[sk[1]].append((sem, cnt))
                slot_sems = {}
                e = o["eng"]
                eng_rank[e] += 1
                token[i] = (eng_sems[e], eng_rank[e], 1)
            elif o["dma"]:
                sk = o["slot"]
                if sk not in slot_sems:
                    if free_sems[sk[1]]:
                        slot_sems[sk] = free_sems[sk[1]].pop()
                    else:
                        n_slot_sems += 1
                        slot_sems[sk] = (self.p.sem(f"sdma_{n_slot_sems}"), 0)
                sem, cnt = slot_sems[sk]
                cnt += 1
                slot_sems[sk] = (sem, cnt)
                token[i] = (sem, 16 * cnt, 16)
            elif o["cc"]:
                cc_cnt += 1
                token[i] = (cc_sem, cc_cnt, 1)
            elif i in needed:
                e = o["eng"]
                eng_rank[e] += 1
                token[i] = (eng_sems[e], eng_rank[e], 1)
        comp = [None] * len(ops)
        for i, o in enumerate(ops):
            C = {}
            for d in o["deps"]:
                od = ops[d]
                waited = (od["dma"] or od["cc"] or od["barrier"]
                          or od["eng"] != o["eng"] or o["barrier"] or o["dma"])
                if not waited or d not in token:
                    continue
                for e2, v2 in comp[d].items():
                    if C.get(e2, -1) < v2:
                        C[e2] = v2
                if od["barrier"]:
                    for e2 in self.CENG:
                        if C.get(e2, -1) < d:
                            C[e2] = d
                elif not od["dma"] and not od["cc"]:
                    if C.get(od["eng"], -1) < d:
                        C[od["eng"]] = d
            comp[i] = C

        watermark = {}
        final_dma = {}
        for i, o in enumerate(ops):
            if o["dma"]:
                final_dma[token[i][0]] = token[i][1]

        import os as _os
        _stats = {"drain": {}, "ops": {}}

        def emit_for(eng_name, eng):
            floor = -1
            last_own = -1
            lastW = {}
            lastR = {}
            for i, o in enumerate(ops):
                if o["eng"] != eng_name and not o["barrier"]:
                    continue
                if o["barrier"] and eng_name != o["eng"]:
                    continue
                want = {}
                for d in sorted(o["deps"]):
                    od = ops[d]
                    if od["dma"] or od["cc"] or od["barrier"] or od["eng"] != eng_name \
                            or o["barrier"] or o["dma"]:
                        if d not in token:
                            raise RuntimeError(f"dep {d} of op {i} has no token")
                        sem, val, _ = token[d]
                        cur = want.get(id(sem))
                        if cur is None or cur[0] < val:
                            want[id(sem)] = (val, sem)
                for sid, (val, sem) in want.items():
                    key = (eng_name, sid)
                    if watermark.get(key, -1) < val:
                        eng.wait_ge(sem, val)
                        watermark[key] = val
                if o["barrier"]:
                    if i in token:
                        sem, val, inc = token[i]
                        eng.nop().then_inc(sem, inc)
                    floor = last_own
                    last_own = i
                    continue
                fc = comp[i].get(eng_name, -1)
                if fc > floor:
                    floor = fc
                if not o["dma"] and not o["cc"] and eng_name in self.DRAIN_ENG:
                    hz = -1
                    for r in o["reads"]:
                        v = lastW.get(r, -1)
                        if v > hz:
                            hz = v
                    for w in o["writes"]:
                        v = lastW.get(w, -1)
                        if v > hz:
                            hz = v
                        v = lastR.get(w, -1)
                        if v > hz:
                            hz = v
                    if hz > floor:
                        eng.drain()
                        _stats["drain"][eng_name] = _stats["drain"].get(eng_name, 0) + 1
                        floor = last_own
                _stats["ops"][eng_name] = _stats["ops"].get(eng_name, 0) + 1
                inst = o["fn"](eng)
                if i in token:
                    sem, val, inc = token[i]
                    assert inst is not None, f"op {i} on {eng_name} must return its instruction"
                    inst.then_inc(sem, inc)
                if not o["dma"] and not o["cc"]:
                    for w in o["writes"]:
                        lastW[w] = i
                    for r in o["reads"]:
                        lastR[r] = i
                last_own = i
            if eng_name == "gpsimd":
                for sem, val in final_dma.items():
                    eng.wait_ge(sem, val)

        nc = self.p.nc
        with nc.Block() as block:
            @block.sync
            def _(sync):
                emit_for("sync", sync)

            @block.gpsimd
            def _(gpsimd):
                emit_for("gpsimd", gpsimd)

            @block.tensor
            def _(tensor):
                emit_for("tensor", tensor)

            @block.vector
            def _(vector):
                emit_for("vector", vector)

            @block.scalar
            def _(scalar):
                emit_for("scalar", scalar)
        if _os.environ.get("MK_STATS"):
            print("SCHED STATS", _stats)



HID = 5632
HC = HID // 128


def emit_p3(p, sc, io):
    nc = p.nc
    o_ap = io["o_ap"]
    x_in = io["x_dm"]
    Wo = io["Wo"]; normw = io["norm_w"]
    wqgT = io["wqgT"]; wquT = io["wquT"]; wqdT = io["wqdT"]
    sg_in = io["s_gate"]; su_in = io["s_up"]; sd_in = io["s_down"]
    dtwT = io["dtwT"]; dtb_in = io["dtb"]; xpwT = io["xpwT"]; xpb_in = io["xpb"]
    x2_out = io["x2_dm"]; xn2_out = io["xn2_dm"]; dt_out = io["dt_dm"]; bc_out = io["bc_dm"]

    ao_dm = p.dram("ao_dram", [D, TOK], BF16)
    x1_dm = p.dram("x1_dram", [D, TOK])
    gu_dm = p.dram("gu_dram", [HID, TOK], BF16)

    A = p.sb("A", [128, DC, TOK], BF16)
    xsm = p.sb("xsm", [128, DC, 512], BF16)
    gut = p.sb("gut", [128, HC, 512], BF16)
    xc = p.sb("xc", [128, 2, 512], F32)
    xcb = p.sb("xcb", [128, 2, 512], BF16)
    osq = p.sb("osq", [128, 2, 512], F32)
    t1b = p.sb("t1b", [128, 2, 512], F32)
    wdl = p.sb("wdl", [128, 2, HC, 128], BF16)
    wsl = wdl[:, :, 0:DC]
    wg2 = wdl[:, :, 0:DC]
    wu2 = wdl[:, :, DC:2 * DC]
    wxp = wdl[:, 0, DC:DC + 4].rearrange("p a b -> p (a b)").rearrange(
        "p (c e) -> p c e", e=32)
    stg = p.sb("stg", [128, 3, 512], BF16)
    ustg = p.sb("ustg", [128, 2, 512], BF16)
    asq = p.sb("asq", [128, 2, 512], F32)
    srow = p.sb("srow", [1, 5, 512], F32)
    accS = p.sb("accS", [1, NTT, 512], F32)
    accQ = p.sb("accQ", [1, NTT, 512], F32)
    gsum = p.sb("gsum", [1, 512], BF16)
    rbc = p.sb("rbc", [128, 2, 512], F32)
    mrbc = p.sb("mrbc", [128, 2, 512], F32)
    vrow = p.sb("vrow", [1, 512], F32)
    nw_c = p.sb("nw_c", [128, DC], F32)
    sg_c = p.sb("sg_c", [128, HC], F32)
    su_c = p.sb("su_c", [128, HC], F32)
    sd_c = p.sb("sd_c", [128, DC], F32)
    db_c = p.sb("db_c", [128, DC], F32)
    xpb_c = p.sb("xpb_c", [32, 1], F32)
    ones_c = p.sb("ones_c", [128, 1], F32)
    ones_cb = p.sb("ones_cb", [128, 1], BF16)
    ones_r = p.sb("ones_r", [1, 128], F32)
    neg128 = p.sb("neg128", [1, 128], BF16)
    identf = p.sb("identf", [1, 1], F32)

    p_mm = [p.ps("p_mm%d" % k, [128, 512], F32) for k in range(4)]
    p_sq = p.ps("p_sq", [128, 512], F32)
    p_r = p.ps("p_r", [128, 512], F32)
    p_m = p.ps("p_m", [128, 512], F32)
    p_t = p.ps("p_t", [128, 512], F32)

    ps_s = p_sq[0:1, 0:512]
    ps_q = p_sq[64:65, 0:512]
    rs_s = Rr(p_sq, "s")
    rs_q = Rr(p_sq, "q")

    sc.exclusive(*p_mm, p_sq, p_r, p_m, p_t)
    AF_, AL = AF, ALU

    sc.op("vector", lambda v: v.memset(ones_c[:], 1.0), writes=[Rr(ones_c)])
    sc.op("vector", lambda v: v.memset(ones_cb[:], 1.0), writes=[Rr(ones_cb)])
    sc.op("vector", lambda v: v.memset(ones_r[:], 1.0), writes=[Rr(ones_r)])
    sc.op("vector", lambda v: v.memset(neg128[:], -128.0), writes=[Rr(neg128)])
    sc.op("vector", lambda v: v.memset(identf[:], 1.0), writes=[Rr(identf)])
    sc.op("vector", lambda v: v.memset(accS[:], 0.0), writes=[Rr(accS)])
    sc.op("vector", lambda v: v.memset(accQ[:], 0.0), writes=[Rr(accQ)])

    def vec_cols(name, src, n, dst):
        for seg in range(0, n, 512):
            w = min(512, n - seg)
            sc.dma("sync", lambda s, src=src, seg=seg, w=w: s.dma_start(
                out=vrow[0:1, 0:w], in_=src.unsqueeze(0)[:, seg:seg + w]),
                reads=[], writes=[Rr(vrow)])
            for cc in range(w // 128):
                c = seg // 128 + cc
                sc.op("tensor", lambda t, cc=cc: t.transpose(p_t[:, 0:1],
                                                             vrow[0:1, cc * 128:(cc + 1) * 128], identf[:]),
                      reads=[Rr(vrow), Rr(identf)], writes=[Rr(p_t)])
                sc.op("vector", lambda v, c=c, dst=dst: v.tensor_copy(out=dst[:, c:c + 1],
                                                                      in_=p_t[:, 0:1]),
                      reads=[Rr(p_t)], writes=[Rr(dst)])

    vec_cols("nw", normw, D, nw_c)
    vec_cols("sg", sg_in, HID, sg_c)
    vec_cols("su", su_in, HID, su_c)
    vec_cols("sd", sd_in, D, sd_c)
    vec_cols("db", dtb_in, D, db_c)
    sc.dma("sync", lambda s: s.dma_start(out=vrow[0:1, 0:32], in_=xpb_in.unsqueeze(0)),
           reads=[], writes=[Rr(vrow)])
    sc.op("tensor", lambda t: t.transpose(p_t[0:32, 0:1], vrow[0:1, 0:32], identf[:]),
          reads=[Rr(vrow), Rr(identf)], writes=[Rr(p_t)])
    sc.op("vector", lambda v: v.tensor_copy(out=xpb_c[:], in_=p_t[0:32, 0:1]),
          reads=[Rr(p_t)], writes=[Rr(xpb_c)])

    fzc = [0]

    def finalize(mean, aps_s, aps_q, eps, denom=D):
        sl = fzc[0] % 2
        fzc[0] += 1
        if mean:
            sc.op("vector", lambda v: v.tensor_scalar(out=srow[:, 0], in0=aps_s[0], scalar1=1.0 / denom,
                                                      scalar2=None, op0=AL.mult),
                  reads=[aps_s[1]], writes=[Rr(srow, 0)])
            sc.op("vector", lambda v: v.tensor_scalar(out=srow[:, 1], in0=aps_q[0], scalar1=1.0 / denom,
                                                      scalar2=None, op0=AL.mult),
                  reads=[aps_q[1]], writes=[Rr(srow, 1)])
            sc.op("vector", lambda v: v.tensor_tensor(out=srow[:, 2], in0=srow[:, 0], in1=srow[:, 0],
                                                      op=AL.mult),
                  reads=[Rr(srow, 0)], writes=[Rr(srow, 2)])
            sc.op("vector", lambda v: v.tensor_tensor(out=srow[:, 2], in0=srow[:, 1], in1=srow[:, 2],
                                                      op=AL.subtract),
                  reads=[Rr(srow, 1), Rr(srow, 2)], writes=[Rr(srow, 2)])
            sc.op("vector", lambda v: v.tensor_scalar(out=srow[:, 2], in0=srow[:, 2], scalar1=eps,
                                                      scalar2=None, op0=AL.add),
                  reads=[Rr(srow, 2)], writes=[Rr(srow, 2)])
        else:
            sc.op("vector", lambda v: v.tensor_scalar(out=srow[:, 2], in0=aps_q[0], scalar1=1.0 / denom,
                                                      scalar2=eps, op0=AL.mult, op1=AL.add),
                  reads=[aps_q[1]], writes=[Rr(srow, 2)])
        sc.op("scalar", lambda s: s.activation(out=srow[:, 3], in_=srow[:, 2], func=AF_.Ln),
              reads=[Rr(srow, 2)], writes=[Rr(srow, 3)])
        sc.op("scalar", lambda s: s.activation(out=srow[:, 3], in_=srow[:, 3], func=AF_.Exp, scale=-0.5),
              reads=[Rr(srow, 3)], writes=[Rr(srow, 3)])
        sc.op("tensor", lambda t: t.matmul(p_r[:], ones_r[:], srow[:, 3], start=True, stop=True),
              reads=[Rr(srow, 3), Rr(ones_r)], writes=[Rr(p_r)])
        sc.op("scalar", lambda s, sl=sl: s.activation(out=rbc[:, sl], in_=p_r[:], func=AF_.Copy),
              reads=[Rr(p_r)], writes=[Rr(rbc, sl)])
        if mean:
            sc.op("vector", lambda v: v.scalar_tensor_tensor(out=srow[:, 4], in0=srow[:, 0],
                                                             scalar=-1.0, in1=srow[:, 3],
                                                             op0=AL.mult, op1=AL.mult),
                  reads=[Rr(srow, 0), Rr(srow, 3)], writes=[Rr(srow, 4)])
            sc.op("tensor", lambda t: t.matmul(p_m[:], ones_r[:], srow[:, 4], start=True, stop=True),
                  reads=[Rr(srow, 4), Rr(ones_r)], writes=[Rr(p_m)])
            sc.op("scalar", lambda s, sl=sl: s.activation(out=mrbc[:, sl], in_=p_m[:], func=AF_.Copy),
                  reads=[Rr(p_m)], writes=[Rr(mrbc, sl)])
        return sl

    for tt in range(NTT):
        for c in range(DC):
            i = tt * DC + c
            sc.dma("sync", lambda s, tt=tt, c=c, i=i: s.dma_start(
                out=xcb[:, i % 2], in_=o_ap(c, tt)),
                reads=[Rr(o_ap, (c, tt))], writes=[Rr(xcb, i % 2)])
            sc.op("scalar", lambda s, i=i: s.activation(out=osq[:, i % 2], in_=xcb[:, i % 2],
                                                        func=AF_.Square),
                  reads=[Rr(xcb, i % 2)], writes=[Rr(osq, i % 2)])
            sc.op("tensor", lambda t, c=c, i=i: t.matmul(ps_q, ones_c[:], osq[:, i % 2],
                                                         start=(c == 0), stop=(c == DC - 1)),
                  reads=[Rr(osq, i % 2), Rr(ones_c)], writes=[rs_q])
        sl = finalize(False, None, (ps_q, rs_q), 1e-6)
        for c in range(DC):
            i = tt * DC + c
            sc.dma("sync", lambda s, tt=tt, c=c, i=i: s.dma_start(
                out=xcb[:, i % 2], in_=o_ap(c, tt)),
                reads=[Rr(o_ap, (c, tt))], writes=[Rr(xcb, i % 2)])
            sc.op("vector", lambda v, tt=tt, c=c, i=i, sl=sl: v.scalar_tensor_tensor(
                out=A[:, c, tt * 512:(tt + 1) * 512], in0=xcb[:, i % 2],
                scalar=nw_c[:, c:c + 1], in1=rbc[:, sl], op0=AL.mult, op1=AL.mult),
                reads=[Rr(xcb, i % 2), Rr(nw_c), Rr(rbc, sl)], writes=[Rr(A, (c, tt))])

    for oc in range(16):
        sc.dma("gpsimd", lambda g, oc=oc: g.dma_start(
            out=wsl[:, oc % 2],
            in_=Wo[oc].rearrange("p (k f) -> p k f", f=128)),
            writes=[Rr(wdl, ("g", oc % 2))])
        for tt in range(NTT):
            i = oc * NTT + tt
            pm = p_mm[i % 4]
            for c in range(DC):
                sc.op("tensor", lambda t, oc=oc, tt=tt, c=c, pm=pm: t.matmul(
                    pm[:], wsl[:, oc % 2, c], A[:, c, tt * 512:(tt + 1) * 512],
                    start=(c == 0), stop=(c == DC - 1)),
                    reads=[Rr(wdl, ("g", oc % 2)), Rr(A, (c, tt))], writes=[Rr(pm)])
            sc.op("scalar", lambda s, i=i, pm=pm: s.activation(out=stg[:, i % 3], in_=pm[:],
                                                               func=AF_.Copy),
                  reads=[Rr(pm)], writes=[Rr(stg, i % 3)])
            sc.op("scalar", lambda s, i=i, pm=pm: s.activation(out=osq[:, i % 2], in_=pm[:],
                                                               func=AF_.Square),
                  reads=[Rr(pm)], writes=[Rr(osq, i % 2)])
            sc.op("tensor", lambda t, i=i: t.matmul(ps_s, ones_cb[:], stg[:, i % 3],
                                                    start=True, stop=True),
                  reads=[Rr(stg, i % 3), Rr(ones_cb)], writes=[rs_s])
            sc.op("tensor", lambda t, i=i: t.matmul(ps_q, ones_c[:], osq[:, i % 2],
                                                    start=True, stop=True),
                  reads=[Rr(osq, i % 2), Rr(ones_c)], writes=[rs_q])
            sc.op("vector", lambda v, tt=tt: v.tensor_tensor(out=accS[:, tt], in0=accS[:, tt],
                                                             in1=ps_s, op=AL.add),
                  reads=[rs_s, Rr(accS)], writes=[Rr(accS)])
            sc.op("vector", lambda v, tt=tt: v.tensor_tensor(out=accQ[:, tt], in0=accQ[:, tt],
                                                             in1=ps_q, op=AL.add),
                  reads=[rs_q, Rr(accQ)], writes=[Rr(accQ)])
            sc.dma("sync", lambda s, oc=oc, tt=tt, i=i: s.dma_start(
                out=ao_dm.rearrange("(c p) (t f) -> p c t f", p=128, f=512)[:, oc, tt],
                in_=stg[:, i % 3]),
                reads=[Rr(stg, i % 3)], writes=[Rr(ao_dm, (oc, tt))], slot=Rr(stg, i % 3))
    for tt in range(NTT):
        sl1 = finalize(True, (accS[:, tt], Rr(accS)), (accQ[:, tt], Rr(accQ)), LNEPS)
        for c in range(DC):
            i = tt * DC + c
            sc.dma("sync", lambda s, tt=tt, c=c, i=i: s.dma_start(
                out=xc[:, i % 2],
                in_=x_in.rearrange("(c p) (t f) -> p c t f", p=128, f=512)[:, c, tt]),
                writes=[Rr(xc, i % 2)])
            sc.dma("sync", lambda s, tt=tt, c=c, i=i: s.dma_start(
                out=xcb[:, i % 2],
                in_=ao_dm.rearrange("(c p) (t f) -> p c t f", p=128, f=512)[:, c, tt]),
                reads=[Rr(ao_dm, (c, tt))], writes=[Rr(xcb, i % 2)])
            sc.op("vector", lambda v, i=i, sl1=sl1: v.tensor_tensor(
                out=asq[:, i % 2], in0=xcb[:, i % 2], in1=rbc[:, sl1], op=AL.mult),
                reads=[Rr(xcb, i % 2), Rr(rbc, sl1)], writes=[Rr(asq, i % 2)])
            sc.op("gpsimd", lambda g, i=i, sl1=sl1: g.tensor_tensor(
                out=t1b[:, i % 2], in0=asq[:, i % 2], in1=mrbc[:, sl1], op=AL.add),
                reads=[Rr(asq, i % 2), Rr(mrbc, sl1)], writes=[Rr(t1b, i % 2)])
            sc.op("vector", lambda v, i=i: v.tensor_tensor(out=xc[:, i % 2], in0=xc[:, i % 2],
                                                           in1=t1b[:, i % 2], op=AL.add),
                  reads=[Rr(xc, i % 2), Rr(t1b, i % 2)], writes=[Rr(xc, i % 2)])
            sc.dma("sync", lambda s, tt=tt, c=c, i=i: s.dma_start(
                out=x1_dm.rearrange("(c p) (t f) -> p c t f", p=128, f=512)[:, c, tt],
                in_=xc[:, i % 2]),
                reads=[Rr(xc, i % 2)], writes=[Rr(x1_dm, (c, tt))], slot=Rr(xc, i % 2))
            sc.op("scalar", lambda s, i=i: s.activation(out=osq[:, i % 2], in_=xc[:, i % 2],
                                                        func=AF_.Square),
                  reads=[Rr(xc, i % 2)], writes=[Rr(osq, i % 2)])
            sc.op("tensor", lambda t, c=c, i=i: t.matmul(ps_s, ones_c[:], xc[:, i % 2],
                                                         start=(c == 0), stop=(c == DC - 1)),
                  reads=[Rr(xc, i % 2), Rr(ones_c)], writes=[rs_s])
            sc.op("tensor", lambda t, c=c, i=i: t.matmul(ps_q, ones_c[:], osq[:, i % 2],
                                                         start=(c == 0), stop=(c == DC - 1)),
                  reads=[Rr(osq, i % 2), Rr(ones_c)], writes=[rs_q])
        sl2 = finalize(True, (ps_s, rs_s), (ps_q, rs_q), LNEPS)
        for c in range(DC):
            i = tt * DC + c
            sc.dma("sync", lambda s, tt=tt, c=c, i=i: s.dma_start(
                out=xc[:, i % 2],
                in_=x1_dm.rearrange("(c p) (t f) -> p c t f", p=128, f=512)[:, c, tt]),
                reads=[Rr(x1_dm, (c, tt))], writes=[Rr(xc, i % 2)])
            sc.op("gpsimd", lambda g, i=i, sl2=sl2: g.tensor_tensor(
                out=t1b[:, i % 2], in0=xc[:, i % 2], in1=rbc[:, sl2], op=AL.mult),
                reads=[Rr(xc, i % 2), Rr(rbc, sl2)], writes=[Rr(t1b, i % 2)])
            sc.op("vector", lambda v, tt=tt, c=c, i=i, sl2=sl2: v.tensor_tensor(
                out=A[:, c, tt * 512:(tt + 1) * 512], in0=t1b[:, i % 2], in1=mrbc[:, sl2],
                op=AL.add),
                reads=[Rr(t1b, i % 2), Rr(mrbc, sl2)], writes=[Rr(A, (c, tt))])

    for hc in range(HC):
        sc.dma("gpsimd", lambda g, hc=hc: g.dma_start(
            out=wg2[:, hc % 2],
            in_=wqgT[hc].rearrange("p (k f) -> p k f", f=128)),
            writes=[Rr(wdl, ("g", hc % 2))])
        sc.dma("gpsimd", lambda g, hc=hc: g.dma_start(
            out=wu2[:, hc % 2],
            in_=wquT[hc].rearrange("p (k f) -> p k f", f=128)),
            writes=[Rr(wdl, ("u", hc % 2))])
        for tt in range(NTT):
            i = hc * NTT + tt
            pg = p_mm[(2 * i) % 4]
            pu = p_mm[(2 * i + 1) % 4]
            for c in range(DC):
                sc.op("tensor", lambda t, hc=hc, tt=tt, c=c, pg=pg: t.matmul(
                    pg[:], wg2[:, hc % 2, c], A[:, c, tt * 512:(tt + 1) * 512],
                    start=(c == 0), stop=(c == DC - 1)),
                    reads=[Rr(wdl, ("g", hc % 2)), Rr(A, (c, tt))], writes=[Rr(pg)])
            for c in range(DC):
                sc.op("tensor", lambda t, hc=hc, tt=tt, c=c, pu=pu: t.matmul(
                    pu[:], wu2[:, hc % 2, c], A[:, c, tt * 512:(tt + 1) * 512],
                    start=(c == 0), stop=(c == DC - 1)),
                    reads=[Rr(wdl, ("u", hc % 2)), Rr(A, (c, tt))], writes=[Rr(pu)])
            sc.op("scalar", lambda s, hc=hc, i=i, pg=pg: s.activation(
                out=stg[:, i % 3], in_=pg[:], func=AF_.Silu, scale=sg_c[:, hc:hc + 1]),
                reads=[Rr(pg), Rr(sg_c)], writes=[Rr(stg, i % 3)])
            sc.op("scalar", lambda s, hc=hc, i=i, pu=pu: s.activation(
                out=ustg[:, i % 2], in_=pu[:], func=AF_.Copy, scale=su_c[:, hc:hc + 1]),
                reads=[Rr(pu), Rr(su_c)], writes=[Rr(ustg, i % 2)])
            sc.op("vector", lambda v, i=i: v.tensor_tensor(out=ustg[:, i % 2], in0=stg[:, i % 3],
                                                           in1=ustg[:, i % 2], op=AL.mult),
                  reads=[Rr(stg, i % 3), Rr(ustg, i % 2)], writes=[Rr(ustg, i % 2)])
            sc.dma("sync", lambda s, hc=hc, tt=tt, i=i: s.dma_start(
                out=gu_dm.rearrange("(c p) (t f) -> p c t f", p=128, f=512)[:, hc, tt],
                in_=ustg[:, i % 2]),
                reads=[Rr(ustg, i % 2)], writes=[Rr(gu_dm, (hc, tt))], slot=Rr(ustg, i % 2))

    for tt in range(NTT):
        sc.dma("sync", lambda s, tt=tt: s.dma_start(
            out=gut[:], in_=gu_dm.rearrange("(h p) (t f) -> p h t f", p=128, f=512)[:, :, tt]),
            reads=[Rr(gu_dm, (hc, tt)) for hc in range(HC)], writes=[Rr(gut)])
        for hc in range(HC):
            sc.op("tensor", lambda t, hc=hc: t.matmul(ps_s, ones_cb[:], gut[:, hc],
                                                      start=(hc == 0), stop=(hc == HC - 1)),
                  reads=[Rr(gut), Rr(ones_cb)], writes=[rs_s])
        sc.op("scalar", lambda s, tt=tt: s.activation(out=gsum[:], in_=ps_s,
                                                      func=AF_.Copy),
              reads=[rs_s], writes=[Rr(gsum)])
        for oc in range(16):
            i = tt * 16 + oc
            sc.dma("gpsimd", lambda g, oc=oc, i=i: g.dma_start(
                out=wdl[:, i % 2],
                in_=wqdT[oc].rearrange("p (k f) -> p k f", f=128)),
                writes=[Rr(wdl, ("g", i % 2)), Rr(wdl, ("u", i % 2))])
            pm = p_mm[i % 4]
            for hc in range(HC):
                sc.op("tensor", lambda t, hc=hc, i=i, pm=pm: t.matmul(
                    pm[:], wdl[:, i % 2, hc], gut[:, hc],
                    start=(hc == 0), stop=False),
                    reads=[Rr(wdl, ("g", i % 2)), Rr(wdl, ("u", i % 2)), Rr(gut)],
                    writes=[Rr(pm)])
            sc.op("tensor", lambda t, tt=tt, pm=pm: t.matmul(
                pm[:], neg128[:], gsum[:], start=False, stop=True),
                reads=[Rr(neg128), Rr(gsum)], writes=[Rr(pm)])
            sc.op("scalar", lambda s, oc=oc, pm=pm: s.activation(
                out=xsm[:, oc], in_=pm[:], func=AF_.Copy, scale=sd_c[:, oc:oc + 1]),
                reads=[Rr(pm), Rr(sd_c)], writes=[Rr(xsm, oc)])
            sc.op("scalar", lambda s, oc=oc, i=i, pm=pm: s.activation(
                out=osq[:, i % 2], in_=pm[:], func=AF_.Square, scale=sd_c[:, oc:oc + 1]),
                reads=[Rr(pm), Rr(sd_c)], writes=[Rr(osq, i % 2)])
            sc.op("tensor", lambda t, oc=oc: t.matmul(ps_s, ones_cb[:], xsm[:, oc],
                                                      start=(oc == 0), stop=(oc == 15)),
                  reads=[Rr(xsm, oc), Rr(ones_cb)], writes=[rs_s])
            sc.op("tensor", lambda t, oc=oc, i=i: t.matmul(ps_q, ones_c[:], osq[:, i % 2],
                                                           start=(oc == 0), stop=(oc == 15)),
                  reads=[Rr(osq, i % 2), Rr(ones_c)], writes=[rs_q])
        sl1 = finalize(True, (ps_s, rs_s), (ps_q, rs_q), LNEPS)
        for c in range(DC):
            i = tt * DC + c
            sc.dma("sync", lambda s, tt=tt, c=c, i=i: s.dma_start(
                out=xc[:, i % 2],
                in_=x1_dm.rearrange("(c p) (t f) -> p c t f", p=128, f=512)[:, c, tt]),
                reads=[Rr(x1_dm, (c, tt))], writes=[Rr(xc, i % 2)])
            sc.op("vector", lambda v, c=c, i=i, sl1=sl1: v.tensor_tensor(
                out=asq[:, i % 2], in0=xsm[:, c], in1=rbc[:, sl1], op=AL.mult),
                reads=[Rr(xsm, c), Rr(rbc, sl1)], writes=[Rr(asq, i % 2)])
            sc.op("gpsimd", lambda g, i=i, sl1=sl1: g.tensor_tensor(
                out=t1b[:, i % 2], in0=asq[:, i % 2], in1=mrbc[:, sl1], op=AL.add),
                reads=[Rr(asq, i % 2), Rr(mrbc, sl1)], writes=[Rr(t1b, i % 2)])
            sc.op("vector", lambda v, i=i: v.tensor_tensor(out=xc[:, i % 2], in0=xc[:, i % 2],
                                                           in1=t1b[:, i % 2], op=AL.add),
                  reads=[Rr(xc, i % 2), Rr(t1b, i % 2)], writes=[Rr(xc, i % 2)])
            sc.dma("sync", lambda s, tt=tt, c=c, i=i: s.dma_start(
                out=x2_out.rearrange("(c p) (t f) -> p c t f", p=128, f=512)[:, c, tt],
                in_=xc[:, i % 2]),
                reads=[Rr(xc, i % 2)], writes=[Rr(x2_out, (c, tt))], slot=Rr(xc, i % 2))
            sc.op("scalar", lambda s, i=i: s.activation(out=osq[:, i % 2], in_=xc[:, i % 2],
                                                        func=AF_.Square),
                  reads=[Rr(xc, i % 2)], writes=[Rr(osq, i % 2)])
            sc.op("tensor", lambda t, c=c, i=i: t.matmul(ps_s, ones_c[:], xc[:, i % 2],
                                                         start=(c == 0), stop=(c == DC - 1)),
                  reads=[Rr(xc, i % 2), Rr(ones_c)], writes=[rs_s])
            sc.op("tensor", lambda t, c=c, i=i: t.matmul(ps_q, ones_c[:], osq[:, i % 2],
                                                         start=(c == 0), stop=(c == DC - 1)),
                  reads=[Rr(osq, i % 2), Rr(ones_c)], writes=[rs_q])
        sl2 = finalize(True, (ps_s, rs_s), (ps_q, rs_q), LNEPS)
        for c in range(DC):
            i = tt * DC + c
            sc.dma("sync", lambda s, tt=tt, c=c, i=i: s.dma_start(
                out=xc[:, i % 2],
                in_=x2_out.rearrange("(c p) (t f) -> p c t f", p=128, f=512)[:, c, tt]),
                reads=[Rr(x2_out, (c, tt))], writes=[Rr(xc, i % 2)])
            sc.op("gpsimd", lambda g, i=i, sl2=sl2: g.tensor_tensor(
                out=t1b[:, i % 2], in0=xc[:, i % 2], in1=rbc[:, sl2], op=AL.mult),
                reads=[Rr(xc, i % 2), Rr(rbc, sl2)], writes=[Rr(t1b, i % 2)])
            sc.op("vector", lambda v, tt=tt, c=c, i=i, sl2=sl2: v.tensor_tensor(
                out=A[:, c, tt * 512:(tt + 1) * 512], in0=t1b[:, i % 2], in1=mrbc[:, sl2],
                op=AL.add),
                reads=[Rr(t1b, i % 2), Rr(mrbc, sl2)], writes=[Rr(A, (c, tt))])
            sc.dma("sync", lambda s, tt=tt, c=c: s.dma_start(
                out=xn2_out.rearrange("(c p) (t f) -> p c t f", p=128, f=512)[:, c, tt],
                in_=A[:, c, tt * 512:(tt + 1) * 512]),
                reads=[Rr(A, (c, tt))], writes=[Rr(xn2_out, (c, tt))], slot=Rr(A, "st"))

    for oc in range(16):
        sc.dma("gpsimd", lambda g, oc=oc: g.dma_start(
            out=wsl[:, oc % 2],
            in_=dtwT[oc].rearrange("p (k f) -> p k f", f=128)),
            writes=[Rr(wdl, ("g", oc % 2))])
        for tt in range(NTT):
            i = oc * NTT + tt
            pm = p_mm[i % 4]
            for c in range(DC):
                sc.op("tensor", lambda t, oc=oc, tt=tt, c=c, pm=pm: t.matmul(
                    pm[:], wsl[:, oc % 2, c], A[:, c, tt * 512:(tt + 1) * 512],
                    start=(c == 0), stop=(c == DC - 1)),
                    reads=[Rr(wdl, ("g", oc % 2)), Rr(A, (c, tt))], writes=[Rr(pm)])
            sc.op("scalar", lambda s, oc=oc, i=i, pm=pm: s.activation(
                out=osq[:, i % 2], in_=pm[:], func=AF_.Relu, bias=db_c[:, oc:oc + 1]),
                reads=[Rr(pm), Rr(db_c)], writes=[Rr(osq, i % 2)])
            sc.op("scalar", lambda s, oc=oc, i=i, pm=pm: s.activation(
                out=asq[:, i % 2], in_=pm[:], func=AF_.Abs, bias=db_c[:, oc:oc + 1]),
                reads=[Rr(pm), Rr(db_c)], writes=[Rr(asq, i % 2)])
            sc.op("scalar", lambda s, i=i: s.activation(out=asq[:, i % 2], in_=asq[:, i % 2],
                                                        func=AF_.Exp, scale=-1.0),
                  reads=[Rr(asq, i % 2)], writes=[Rr(asq, i % 2)])
            sc.op("scalar", lambda s, i=i: s.activation(out=asq[:, i % 2], in_=asq[:, i % 2],
                                                        func=AF_.Ln, bias=1.0),
                  reads=[Rr(asq, i % 2)], writes=[Rr(asq, i % 2)])
            sc.op("gpsimd", lambda g, i=i: g.tensor_tensor(out=asq[:, i % 2], in0=osq[:, i % 2],
                                                           in1=asq[:, i % 2], op=AL.add),
                  reads=[Rr(osq, i % 2), Rr(asq, i % 2)], writes=[Rr(asq, i % 2)])
            sc.dma("sync", lambda s, oc=oc, tt=tt, i=i: s.dma_start(
                out=dt_out.rearrange("(c p) (t f) -> p c t f", p=128, f=512)[:, oc, tt],
                in_=asq[:, i % 2]),
                reads=[Rr(asq, i % 2)], writes=[Rr(dt_out, (oc, tt))], slot=Rr(asq, i % 2))
    sc.dma("gpsimd", lambda g: g.dma_start(
        out=wxp[:], in_=xpwT[0].rearrange("p (k f) -> p k f", f=32)),
        writes=[Rr(wdl, ("u", 0))])
    for tt in range(NTT):
        for c in range(DC):
            sc.op("tensor", lambda t, tt=tt, c=c: t.matmul(
                p_t[0:32, 0:512], wxp[:, c], A[:, c, tt * 512:(tt + 1) * 512],
                start=(c == 0), stop=(c == DC - 1)),
                reads=[Rr(wdl, ("u", 0)), Rr(A, (c, tt))], writes=[Rr(p_t)])
        sc.op("vector", lambda v, tt=tt: v.tensor_scalar(
            out=ustg[0:32, 1], in0=p_t[0:32, 0:512], scalar1=xpb_c[:, 0:1], scalar2=None,
            op0=AL.add),
            reads=[Rr(p_t), Rr(xpb_c)], writes=[Rr(ustg, 1)])
        sc.dma("sync", lambda s, tt=tt: s.dma_start(out=bc_out[:, tt * 512:(tt + 1) * 512],
                                                    in_=ustg[0:32, 1]),
               reads=[Rr(ustg, 1)], writes=[Rr(bc_out, tt)], slot=Rr(ustg, (1, "bst")))


def build_p3():
    p = P("p3")
    o_in = p.inp("o_dm", [D, TOK], BF16)
    sc = Sched(p)
    io = dict(
        o_ap=lambda c, tt: o_in.rearrange("(c p) (t f) -> p c t f", p=128, f=512)[:, c, tt],
        x_dm=p.inp("x_dm", [D, TOK]),
        Wo=p.inp("Wo", [16, 128, DC * 128]),
        norm_w=p.inp("norm_w", [D]),
        wqgT=p.inp("wqgT", [HC, 128, DC * 128], I32),
        wquT=p.inp("wquT", [HC, 128, DC * 128], I32),
        wqdT=p.inp("wqdT", [16, 128, HC * 128], I32),
        s_gate=p.inp("s_gate", [HID]), s_up=p.inp("s_up", [HID]), s_down=p.inp("s_down", [D]),
        dtwT=p.inp("dtwT", [16, 128, DC * 128]), dtb=p.inp("dtb", [D]),
        xpwT=p.inp("xpwT", [1, 128, DC * 32]), xpb=p.inp("xpb", [32]),
        x2_dm=p.out("x2_dm", [D, TOK]),
        xn2_dm=p.out("xn2_dm", [D, TOK], BF16),
        dt_dm=p.out("dt_dm", [D, TOK]),
        bc_dm=p.out("bc_dm", [32, TOK]),
    )
    emit_p3(p, sc, io)
    sc.emit()
    return p.done()


def host_p3_inputs(p2res, inputs, core):
    b, hh = core // 2, core % 2
    sl = slice(hh * TOK, (hh + 1) * TOK)
    o_full = np.concatenate([p2res[2 * b]["o_dm"], p2res[2 * b + 1]["o_dm"]], axis=0)
    return {
        "o_dm": np.ascontiguousarray(o_full[:, sl]),
        "x_dm": _dm(inputs["L0_attn_x"][b, sl]),
        "Wo": _pk(inputs["L0_attn_Wo"]),
        "norm_w": np.asarray(inputs["L0_attn_norm_w"]),
        "wqgT": _pk(np.asarray(inputs["L0_ffn_wq_gate"]).T),
        "wquT": _pk(np.asarray(inputs["L0_ffn_wq_up"]).T),
        "wqdT": _pk(np.asarray(inputs["L0_ffn_wq_down"]).T),
        "s_gate": np.asarray(inputs["L0_ffn_s_gate"]),
        "s_up": np.asarray(inputs["L0_ffn_s_up"]),
        "s_down": np.asarray(inputs["L0_ffn_s_down"]),
        "dtwT": _pk(np.asarray(inputs["L1_attn_dt_proj_w"]).T),
        "dtb": np.asarray(inputs["L1_attn_dt_proj_b"]),
        "xpwT": _pk(np.asarray(inputs["L1_attn_x_proj_w"]).T, 32),
        "xpb": np.asarray(inputs["L1_attn_x_proj_b"]),
    }


def emit_p4(p, sc, io):
    nc = p.nc
    dt_ap = io["dt_ap"]; xn_ap = io["xn_ap"]; bc_ap = io["bc_ap"]
    al_in = io["A_sl"]
    y_out = io["y4"]

    dt_t = p.sb("dt_t", [128, 2, 2, 512], F32)
    x_t = p.sb("x_t", [128, 2, 2, 512], BF16)
    bcr = p.sb("bcr", [1, 2, 32, 512], BF16)
    dtx = p.sb("dtx", [128, 2, 2, 512], F32)
    dA = p.sb("dA", [128, 4, 512], F32)
    u_sb = p.sb("u_sb", [128, 4, 512], F32)
    h_sb = p.sb("h_sb", [128, 4, 512], F32)
    c_sb = p.sb("c_sb", [128, 4, 512], F32)
    tmp = p.sb("tmp", [128, 2, 4, 512], F32)
    y_st = p.sb("y_st", [128, 2, 512], BF16)
    hcol = p.sb("hcol", [128, 8, 16], F32)
    A_c = p.sb("A_c", [128, 2, 16], F32)
    ones_rb = p.sb("ones_rb", [1, 128], BF16)
    idf32 = p.sb("idf32", [128, 128], F32)

    p_b = [p.ps("p_b%d" % k, [128, 512], F32) for k in range(3)]
    p_c = [p.ps("p_c%d" % k, [128, 512], F32) for k in range(3)]
    p_y = [p.ps("p_y%d" % k, [128, 512], F32) for k in range(2)]

    sc.exclusive(*p_b, *p_c, *p_y)
    AF_, AL = AF, ALU

    sc.op("vector", lambda v: v.memset(ones_rb[:], 1.0), writes=[Rr(ones_rb)])
    sc.op("vector", lambda v: v.memset(hcol[:], 0.0), writes=[Rr(hcol)])
    sc.op("gpsimd", lambda g: g.memset(idf32[:], 1.0), writes=[Rr(idf32, "ms")])
    sc.op("gpsimd", lambda g: g.affine_select(out=idf32[:], in_=idf32[:], pattern=[[-1, 128]],
                                              compare_op=ALU.is_equal, fill=0.0,
                                              base=0, channel_multiplier=1),
          reads=[Rr(idf32, "ms")], writes=[Rr(idf32)])
    sc.dma("sync", lambda s: s.dma_start(out=A_c[:], in_=al_in.rearrange("s p n -> p s n")),
           writes=[Rr(A_c)])
    sc.op("scalar", lambda s: s.activation(out=A_c.rearrange("p s n -> p (s n)"),
                                           in_=A_c.rearrange("p s n -> p (s n)"), func=AF_.Exp),
          reads=[Rr(A_c)], writes=[Rr(A_c)])
    sc.op("vector", lambda v: v.tensor_scalar(out=A_c.rearrange("p s n -> p (s n)"),
                                              in0=A_c.rearrange("p s n -> p (s n)"),
                                              scalar1=-1.0, scalar2=None, op0=AL.mult),
          reads=[Rr(A_c)], writes=[Rr(A_c)])

    for b in range(4):
        for tc in range(8):
            it = b * 8 + tc
            sl = slice(tc * 512, (tc + 1) * 512)
            for ds in range(2):
                sc.dma("sync", lambda s, b=b, ds=ds, tc=tc, it=it: s.dma_start(
                    out=dt_t[:, ds, it % 2], in_=dt_ap(b, ds, tc)),
                    reads=[Rr(dt_ap, (b, ds, tc))], writes=[Rr(dt_t, (ds, it % 2))])
                sc.dma("sync", lambda s, b=b, ds=ds, tc=tc, it=it: s.dma_start(
                    out=x_t[:, ds, it % 2], in_=xn_ap(b, ds, tc)),
                    reads=[Rr(xn_ap, (b, ds, tc))], writes=[Rr(x_t, (ds, it % 2))])
            sc.dma("sync", lambda s, b=b, tc=tc, it=it: s.dma_start(
                out=bcr[:, it % 2], in_=bc_ap(b, tc)),
                reads=[Rr(bc_ap, (b, tc))], writes=[Rr(bcr, it % 2)])
            for ds in range(2):
                sc.op("vector", lambda v, ds=ds, it=it: v.tensor_tensor(
                    out=dtx[:, ds, it % 2], in0=dt_t[:, ds, it % 2],
                    in1=x_t[:, ds, it % 2], op=AL.mult),
                    reads=[Rr(dt_t, (ds, it % 2)), Rr(x_t, (ds, it % 2))],
                    writes=[Rr(dtx, (ds, it % 2))])

            def emit_bc(n, it=it):
                pb = p_b[n % 3]
                pc = p_c[n % 3]
                sc.op("tensor", lambda t, n=n, it=it, pb=pb: t.matmul(
                    pb[:], ones_rb[:], bcr[0:1, it % 2, n], start=True, stop=True),
                    reads=[Rr(bcr, it % 2), Rr(ones_rb)], writes=[Rr(pb)])
                sc.op("tensor", lambda t, n=n, it=it, pc=pc: t.matmul(
                    pc[:], ones_rb[:], bcr[0:1, it % 2, 16 + n], start=True, stop=True),
                    reads=[Rr(bcr, it % 2), Rr(ones_rb)], writes=[Rr(pc)])
                sc.op("scalar", lambda s, n=n, pc=pc: s.activation(
                    out=c_sb[:, n % 4], in_=pc[:], func=AF_.Copy),
                    reads=[Rr(pc)], writes=[Rr(c_sb, n % 4)])

            emit_bc(0)
            emit_bc(1)
            for n in range(16):
                j = it * 16 + n
                if n + 2 < 16:
                    emit_bc(n + 2)
                pb = p_b[n % 3]
                for ds in range(2):
                    g = b * 2 + ds
                    jj = j * 2 + ds
                    sc.op("scalar", lambda s, ds=ds, n=n, it=it, jj=jj: s.activation(
                        out=dA[:, jj % 4], in_=dt_t[:, ds, it % 2], func=AF_.Exp,
                        scale=A_c[:, ds, n:n + 1]),
                        reads=[Rr(dt_t, (ds, it % 2)), Rr(A_c)], writes=[Rr(dA, jj % 4)])
                    sc.op("vector", lambda v, ds=ds, it=it, jj=jj, pb=pb: v.tensor_tensor(
                        out=u_sb[:, jj % 4], in0=dtx[:, ds, it % 2], in1=pb[:], op=AL.mult),
                        reads=[Rr(dtx, (ds, it % 2)), Rr(pb)], writes=[Rr(u_sb, jj % 4)])
                    sc.op("vector", lambda v, g=g, n=n, jj=jj: v.tensor_tensor_scan(
                        out=h_sb[:, jj % 4], data0=dA[:, jj % 4], data1=u_sb[:, jj % 4],
                        initial=hcol[:, g, n:n + 1], op0=AL.mult, op1=AL.add),
                        reads=[Rr(dA, jj % 4), Rr(u_sb, jj % 4), Rr(hcol, (g, n))],
                        writes=[Rr(h_sb, jj % 4)])
                    sc.op("scalar", lambda s, g=g, n=n, jj=jj: s.activation(
                        out=hcol[:, g, n:n + 1], in_=h_sb[:, jj % 4, 511:512], func=AF_.Copy),
                        reads=[Rr(h_sb, jj % 4)], writes=[Rr(hcol, (g, n))])
                    sc.op("gpsimd", lambda gp, ds=ds, n=n, jj=jj: gp.tensor_tensor(
                        out=tmp[:, ds, n % 4], in0=h_sb[:, jj % 4], in1=c_sb[:, n % 4],
                        op=AL.mult),
                        reads=[Rr(h_sb, jj % 4), Rr(c_sb, n % 4)], writes=[Rr(tmp, (ds, n % 4))])
                    py = p_y[ds]
                    sc.op("tensor", lambda t, ds=ds, n=n, py=py: t.matmul(
                        py[:], idf32[:], tmp[:, ds, n % 4], start=(n == 0), stop=(n == 15)),
                        reads=[Rr(idf32), Rr(tmp, (ds, n % 4))], writes=[Rr(py)])
            for ds in range(2):
                py = p_y[ds]
                sc.op("scalar", lambda s, ds=ds, py=py: s.activation(
                    out=y_st[:, ds], in_=py[:], func=AF_.Copy),
                    reads=[Rr(py)], writes=[Rr(y_st, ds)])
                sc.dma("gpsimd", lambda gq, b=b, ds=ds, sl=sl, it=it: gq.dma_start(
                    out=y_out[b, ds * 128:(ds + 1) * 128, sl], in_=y_st[:, ds]),
                    reads=[Rr(y_st, ds)], writes=[Rr(y_out, (b, ds, it))], slot=Rr(y_st, (ds, "st")))


def build_p4():
    p = P("p4")
    dt_in = p.inp("dt4", [4, 256, S])
    x_in = p.inp("xn4", [4, 256, S], BF16)
    bc_in = p.inp("bc4", [4, 32, S])
    sc = Sched(p)
    io = dict(
        dt_ap=lambda b, ds, tc: dt_in[b, ds * 128:(ds + 1) * 128, tc * 512:(tc + 1) * 512],
        xn_ap=lambda b, ds, tc: x_in[b, ds * 128:(ds + 1) * 128, tc * 512:(tc + 1) * 512],
        bc_ap=lambda b, tc: bc_in[b:b + 1, :, tc * 512:(tc + 1) * 512],
        A_sl=p.inp("A_sl", [2, 128, 16]),
        y4=p.out("y4", [4, 256, S], BF16),
    )
    emit_p4(p, sc, io)
    sc.emit()
    return p.done()


def host_p4_inputs(p3res, inputs, core):
    dsl = slice(core * 256, (core + 1) * 256)
    dt = np.stack([np.concatenate([p3res[2 * b]["dt_dm"], p3res[2 * b + 1]["dt_dm"]], axis=1)[dsl]
                   for b in range(4)])
    xn = np.stack([np.concatenate([p3res[2 * b]["xn2_dm"], p3res[2 * b + 1]["xn2_dm"]], axis=1)[dsl]
                   for b in range(4)])
    bc = np.stack([np.concatenate([p3res[2 * b]["bc_dm"], p3res[2 * b + 1]["bc_dm"]], axis=1)
                   for b in range(4)])
    al = np.asarray(inputs["L1_attn_A_log"])[dsl].reshape(2, 128, 16)
    return {"dt4": np.ascontiguousarray(dt), "xn4": np.ascontiguousarray(xn),
            "bc4": np.ascontiguousarray(bc), "A_sl": np.ascontiguousarray(al)}


LAT = 512
LC = LAT // 128


def emit_p5(p, sc, io):
    nc = p.nc
    y_ap = io["y_ap"]
    x2_in = io["x2_dm"]
    opwT = io["opwT"]; opb_in = io["opb"]; ln1_in = io["ln1_w"]
    kvdw = io["kv_d_w"]; qdw = io["q_d_w"]; ropekw = io["rope_k_w"]
    kuw = io["k_u_w"]; quw = io["q_u_w"]; vuw = io["v_u_w"]; ropeqw = io["rope_q_w"]
    cos_in = io["cos_t"]; sin_in = io["sin_t"]; rt_in = io["RT"]
    x3_out = io["x3_dm"]; xin_out = io["xin_dm"]
    q5_out = io["q5"]; k5_out = io["k5"]; v5_out = io["v5"]

    o5_dm = p.dram("o5_dram", [D, TOK], BF16)

    A5 = p.sb("A5", [128, DC, TOK], BF16)
    kvd = p.sb("kvd", [128, LC, TOK], BF16)
    qdb = p.sb("qdb", [128, LC, TOK], BF16)
    xc = p.sb("xc", [128, 2, 512], F32)
    osq = p.sb("osq", [128, 2, 512], F32)
    asq = p.sb("asq", [128, 2, 512], F32)
    stg = p.sb("stg", [128, 2, 512], BF16)
    stg2 = p.sb("stg2", [128, 2, 512], BF16)
    t1s = p.sb("t1s", [128, 2, 512], F32)
    t2s = p.sb("t2s", [128, 2, 512], F32)
    wsl = p.sb("wsl", [128, 2, DC, 128], BF16)
    wvs = p.sb("wvs", [128, 2, LC, 512], BF16)
    cosb = p.sb("cosb", [128, TOK], F32)
    sinb = p.sb("sinb", [128, TOK], F32)
    rtb = p.sb("rtb", [128, 128], BF16)
    srow = p.sb("srow", [1, 6, 512], F32)
    accS = p.sb("accS", [1, NTT, 512], F32)
    accQ = p.sb("accQ", [1, NTT, 512], F32)
    vrow = p.sb("vrow", [1, 1024], F32)
    opb_c = p.sb("opb_c", [128, DC], F32)
    ln1_c = p.sb("ln1_c", [128, DC], F32)
    ones_c = p.sb("ones_c", [128, 1], F32)
    ones_cb = p.sb("ones_cb", [128, 1], BF16)
    ones_r = p.sb("ones_r", [1, 128], F32)
    identf = p.sb("identf", [1, 1], F32)

    p_mm = [p.ps("p_mm0", [128, 512], F32), p.ps("p_mm1", [128, 512], F32)]
    p_rot = p.ps("p_rot", [128, 512], F32)
    p_s = p.ps("p_s", [1, 512], F32)
    p_q = p.ps("p_q", [1, 512], F32)
    p_r = p.ps("p_r", [128, 512], F32)
    p_m = p.ps("p_m", [128, 512], F32)
    p_t = p.ps("p_t", [128, 512], F32)

    sc.exclusive(p_mm[0], p_mm[1], p_rot, p_s, p_q, p_r, p_m, p_t)
    AF_, AL = AF, ALU

    for nm, buf, val in [("oc", ones_c, 1.0), ("ocb", ones_cb, 1.0), ("or", ones_r, 1.0),
                         ("idf", identf, 1.0), ("aS", accS, 0.0), ("aQ", accQ, 0.0)]:
        sc.op("vector", lambda v, buf=buf, val=val: v.memset(buf[:], val), writes=[Rr(buf)])
    sc.dma("sync", lambda s: s.dma_start(out=cosb[:], in_=cos_in[:]), writes=[Rr(cosb)])
    sc.dma("sync", lambda s: s.dma_start(out=sinb[:], in_=sin_in[:]), writes=[Rr(sinb)])
    sc.dma("gpsimd", lambda g: g.dma_start(out=rtb[:], in_=rt_in[:]), writes=[Rr(rtb)])
    for c in range(DC):
        sc.dma("gpsimd", lambda g, c=c: g.dma_start(
            out=A5[:, c], in_=y_ap(c)),
            reads=[Rr(y_ap, c)], writes=[Rr(A5, c)], slot=Rr(A5, "ld"))

    def vec_cols(src, n, dst):
        for seg in range(0, n, 1024):
            w = min(1024, n - seg)
            sc.dma("sync", lambda s, src=src, seg=seg, w=w: s.dma_start(
                out=vrow[0:1, 0:w], in_=src.unsqueeze(0)[:, seg:seg + w]),
                reads=[], writes=[Rr(vrow)])
            for cc in range(w // 128):
                c = seg // 128 + cc
                sc.op("tensor", lambda t, cc=cc: t.transpose(p_t[:, 0:1],
                                                             vrow[0:1, cc * 128:(cc + 1) * 128], identf[:]),
                      reads=[Rr(vrow), Rr(identf)], writes=[Rr(p_t)])
                sc.op("vector", lambda v, c=c, dst=dst: v.tensor_copy(out=dst[:, c:c + 1], in_=p_t[:, 0:1]),
                      reads=[Rr(p_t)], writes=[Rr(dst)])

    vec_cols(opb_in, D, opb_c)
    vec_cols(ln1_in, D, ln1_c)

    def finalize(mean, aps_s, aps_q, eps):
        if mean:
            sc.op("vector", lambda v: v.tensor_scalar(out=srow[:, 0], in0=aps_s[0], scalar1=1.0 / D,
                                                      scalar2=None, op0=AL.mult),
                  reads=[aps_s[1]], writes=[Rr(srow, 0)])
            sc.op("vector", lambda v: v.tensor_scalar(out=srow[:, 1], in0=aps_q[0], scalar1=1.0 / D,
                                                      scalar2=None, op0=AL.mult),
                  reads=[aps_q[1]], writes=[Rr(srow, 1)])
            sc.op("vector", lambda v: v.tensor_tensor(out=srow[:, 2], in0=srow[:, 0], in1=srow[:, 0],
                                                      op=AL.mult),
                  reads=[Rr(srow, 0)], writes=[Rr(srow, 2)])
            sc.op("vector", lambda v: v.tensor_tensor(out=srow[:, 2], in0=srow[:, 1], in1=srow[:, 2],
                                                      op=AL.subtract),
                  reads=[Rr(srow, 1), Rr(srow, 2)], writes=[Rr(srow, 2)])
            sc.op("vector", lambda v: v.tensor_scalar(out=srow[:, 2], in0=srow[:, 2], scalar1=eps,
                                                      scalar2=None, op0=AL.add),
                  reads=[Rr(srow, 2)], writes=[Rr(srow, 2)])
        sc.op("scalar", lambda s: s.activation(out=srow[:, 3], in_=srow[:, 2], func=AF_.Ln),
              reads=[Rr(srow, 2)], writes=[Rr(srow, 3)])
        sc.op("scalar", lambda s: s.activation(out=srow[:, 3], in_=srow[:, 3], func=AF_.Exp, scale=-0.5),
              reads=[Rr(srow, 3)], writes=[Rr(srow, 3)])
        if mean:
            sc.op("vector", lambda v: v.tensor_scalar(out=srow[:, 4], in0=srow[:, 0], scalar1=-1.0,
                                                      scalar2=None, op0=AL.mult),
                  reads=[Rr(srow, 0)], writes=[Rr(srow, 4)])
        sc.op("tensor", lambda t: t.matmul(p_r[:], ones_r[:], srow[:, 3], start=True, stop=True),
              reads=[Rr(srow, 3), Rr(ones_r)], writes=[Rr(p_r)])
        if mean:
            sc.op("tensor", lambda t: t.matmul(p_m[:], ones_r[:], srow[:, 4], start=True, stop=True),
                  reads=[Rr(srow, 4), Rr(ones_r)], writes=[Rr(p_m)])

    for oc in range(16):
        sc.dma("gpsimd", lambda g, oc=oc: g.dma_start(
            out=wsl[:, oc % 2],
            in_=opwT[oc].rearrange("p (k f) -> p k f", f=128)),
            writes=[Rr(wsl, oc % 2)])
        for tt in range(NTT):
            i = oc * NTT + tt
            for c in range(DC):
                sc.op("tensor", lambda t, oc=oc, tt=tt, c=c, i=i: t.matmul(
                    p_mm[i % 2][:], wsl[:, oc % 2, c], A5[:, c, tt * 512:(tt + 1) * 512],
                    start=(c == 0), stop=(c == DC - 1)),
                    reads=[Rr(wsl, oc % 2), Rr(A5, c)], writes=[Rr(p_mm[i % 2])])
            sc.op("vector", lambda v, oc=oc, i=i: v.tensor_scalar(
                out=stg[:, i % 2], in0=p_mm[i % 2][:], scalar1=opb_c[:, oc:oc + 1], scalar2=None,
                op0=AL.add),
                reads=[Rr(p_mm[i % 2]), Rr(opb_c)], writes=[Rr(stg, i % 2)])
            sc.op("scalar", lambda s, oc=oc, i=i: s.activation(
                out=osq[:, i % 2], in_=p_mm[i % 2][:], func=AF_.Square, bias=opb_c[:, oc:oc + 1]),
                reads=[Rr(p_mm[i % 2]), Rr(opb_c)], writes=[Rr(osq, i % 2)])
            sc.op("tensor", lambda t, i=i: t.matmul(p_s[:], ones_cb[:], stg[:, i % 2], start=True, stop=True),
                  reads=[Rr(stg, i % 2), Rr(ones_cb)], writes=[Rr(p_s)])
            sc.op("tensor", lambda t, i=i: t.matmul(p_q[:], ones_c[:], osq[:, i % 2], start=True, stop=True),
                  reads=[Rr(osq, i % 2), Rr(ones_c)], writes=[Rr(p_q)])
            sc.op("vector", lambda v, tt=tt: v.tensor_tensor(out=accS[:, tt], in0=accS[:, tt],
                                                             in1=p_s[:], op=AL.add),
                  reads=[Rr(p_s), Rr(accS)], writes=[Rr(accS)])
            sc.op("vector", lambda v, tt=tt: v.tensor_tensor(out=accQ[:, tt], in0=accQ[:, tt],
                                                             in1=p_q[:], op=AL.add),
                  reads=[Rr(p_q), Rr(accQ)], writes=[Rr(accQ)])
            sc.dma("sync", lambda s, oc=oc, tt=tt, i=i: s.dma_start(
                out=o5_dm.rearrange("(c p) (t f) -> p c t f", p=128, f=512)[:, oc, tt], in_=stg[:, i % 2]),
                reads=[Rr(stg, i % 2)], writes=[Rr(o5_dm, (oc, tt))], slot=Rr(stg, i % 2))

    for tt in range(NTT):
        finalize(True, (accS[:, tt], Rr(accS)), (accQ[:, tt], Rr(accQ)), LNEPS)
        for c in range(DC):
            i = tt * DC + c
            sc.dma("sync", lambda s, tt=tt, c=c, i=i: s.dma_start(
                out=xc[:, i % 2],
                in_=x2_in.rearrange("(c p) (t f) -> p c t f", p=128, f=512)[:, c, tt]),
                writes=[Rr(xc, i % 2)])
            sc.dma("gpsimd", lambda g, tt=tt, c=c, i=i: g.dma_start(
                out=stg[:, i % 2],
                in_=o5_dm.rearrange("(c p) (t f) -> p c t f", p=128, f=512)[:, c, tt]),
                reads=[Rr(o5_dm, (c, tt))], writes=[Rr(stg, i % 2)])
            sc.op("vector", lambda v, i=i: v.tensor_tensor(out=osq[:, i % 2], in0=stg[:, i % 2],
                                                           in1=p_m[:], op=AL.add),
                  reads=[Rr(stg, i % 2), Rr(p_m)], writes=[Rr(osq, i % 2)])
            sc.op("vector", lambda v, i=i: v.tensor_tensor(out=osq[:, i % 2], in0=osq[:, i % 2],
                                                           in1=p_r[:], op=AL.mult),
                  reads=[Rr(osq, i % 2), Rr(p_r)], writes=[Rr(osq, i % 2)])
            sc.op("vector", lambda v, i=i: v.tensor_tensor(out=xc[:, i % 2], in0=xc[:, i % 2],
                                                           in1=osq[:, i % 2], op=AL.add),
                  reads=[Rr(xc, i % 2), Rr(osq, i % 2)], writes=[Rr(xc, i % 2)])
            sc.dma("sync", lambda s, tt=tt, c=c, i=i: s.dma_start(
                out=x3_out.rearrange("(c p) (t f) -> p c t f", p=128, f=512)[:, c, tt], in_=xc[:, i % 2]),
                reads=[Rr(xc, i % 2)], writes=[Rr(x3_out, (c, tt))], slot=Rr(xc, i % 2))
            sc.op("scalar", lambda s, i=i: s.activation(out=asq[:, i % 2], in_=xc[:, i % 2], func=AF_.Square),
                  reads=[Rr(xc, i % 2)], writes=[Rr(asq, i % 2)])
            sc.op("tensor", lambda t, c=c, i=i: t.matmul(p_s[:], ones_c[:], xc[:, i % 2],
                                                         start=(c == 0), stop=(c == DC - 1)),
                  reads=[Rr(xc, i % 2), Rr(ones_c)], writes=[Rr(p_s)])
            sc.op("tensor", lambda t, c=c, i=i: t.matmul(p_q[:], ones_c[:], asq[:, i % 2],
                                                         start=(c == 0), stop=(c == DC - 1)),
                  reads=[Rr(asq, i % 2), Rr(ones_c)], writes=[Rr(p_q)])
        finalize(True, (p_s[:], Rr(p_s)), (p_q[:], Rr(p_q)), LNEPS)
        sc.op("vector", lambda v: v.reciprocal(out=srow[:, 5], in_=srow[:, 2]),
              reads=[Rr(srow, 2)], writes=[Rr(srow, 5)])
        sc.op("vector", lambda v: v.tensor_scalar(out=srow[:, 5], in0=srow[:, 5], scalar1=-LNEPS,
                                                  scalar2=1.0 + 1e-5, op0=AL.mult, op1=AL.add),
              reads=[Rr(srow, 5)], writes=[Rr(srow, 5)])
        sc.op("scalar", lambda s: s.activation(out=srow[:, 5], in_=srow[:, 5], func=AF_.Ln),
              reads=[Rr(srow, 5)], writes=[Rr(srow, 5)])
        sc.op("scalar", lambda s: s.activation(out=srow[:, 5], in_=srow[:, 5], func=AF_.Exp, scale=-0.5),
              reads=[Rr(srow, 5)], writes=[Rr(srow, 5)])
        sc.op("vector", lambda v: v.tensor_tensor(out=srow[:, 5], in0=srow[:, 5], in1=srow[:, 3],
                                                  op=AL.mult),
              reads=[Rr(srow, 5), Rr(srow, 3)], writes=[Rr(srow, 5)])
        sc.op("tensor", lambda t: t.matmul(p_t[:], ones_r[:], srow[:, 5], start=True, stop=True),
              reads=[Rr(srow, 5), Rr(ones_r)], writes=[Rr(p_t)])
        for c in range(DC):
            i = tt * DC + c
            sc.dma("sync", lambda s, tt=tt, c=c, i=i: s.dma_start(
                out=xc[:, i % 2],
                in_=x3_out.rearrange("(c p) (t f) -> p c t f", p=128, f=512)[:, c, tt]),
                reads=[Rr(x3_out, (c, tt))], writes=[Rr(xc, i % 2)])
            sc.op("vector", lambda v, i=i: v.tensor_tensor(out=osq[:, i % 2], in0=xc[:, i % 2],
                                                           in1=p_m[:], op=AL.add),
                  reads=[Rr(xc, i % 2), Rr(p_m)], writes=[Rr(osq, i % 2)])
            sc.op("vector", lambda v, i=i: v.tensor_tensor(out=t1s[:, i % 2], in0=osq[:, i % 2],
                                                           in1=p_r[:], op=AL.mult),
                  reads=[Rr(osq, i % 2), Rr(p_r)], writes=[Rr(t1s, i % 2)])
            sc.dma("sync", lambda s, tt=tt, c=c, i=i: s.dma_start(
                out=xin_out.rearrange("(c p) (t f) -> p c t f", p=128, f=512)[:, c, tt], in_=t1s[:, i % 2]),
                reads=[Rr(t1s, i % 2)], writes=[Rr(xin_out, (c, tt))], slot=Rr(t1s, i % 2))
            sc.op("vector", lambda v, i=i: v.tensor_tensor(out=t2s[:, i % 2], in0=osq[:, i % 2],
                                                           in1=p_t[:], op=AL.mult),
                  reads=[Rr(osq, i % 2), Rr(p_t)], writes=[Rr(t2s, i % 2)])
            sc.op("vector", lambda v, tt=tt, c=c, i=i: v.tensor_scalar(
                out=A5[:, c, tt * 512:(tt + 1) * 512], in0=t2s[:, i % 2],
                scalar1=ln1_c[:, c:c + 1], scalar2=None, op0=AL.mult),
                reads=[Rr(t2s, i % 2), Rr(ln1_c)], writes=[Rr(A5, (c, tt, "xa")), Rr(A5, c)])

    def A5r(c, tt):
        return Rr(A5, (c, tt, "xa"))

    for wi, (W, dst) in enumerate([(kvdw, kvd), (qdw, qdb)]):
        for ol in range(LC):
            oc = wi * LC + ol
            sc.dma("gpsimd", lambda g, W=W, ol=ol, oc=oc: g.dma_start(
                out=wsl[:, oc % 2],
                in_=W[ol].rearrange("p (k f) -> p k f", f=128)),
                writes=[Rr(wsl, oc % 2)])
            for tt in range(NTT):
                i = oc * NTT + tt
                for c in range(DC):
                    sc.op("tensor", lambda t, oc=oc, tt=tt, c=c, i=i: t.matmul(
                        p_mm[i % 2][:], wsl[:, oc % 2, c], A5[:, c, tt * 512:(tt + 1) * 512],
                        start=(c == 0), stop=(c == DC - 1)),
                        reads=[Rr(wsl, oc % 2), A5r(c, tt)], writes=[Rr(p_mm[i % 2])])
                sc.op("vector", lambda v, dst=dst, ol=ol, tt=tt, i=i: v.tensor_copy(
                    out=dst[:, ol, tt * 512:(tt + 1) * 512], in_=p_mm[i % 2][:]),
                    reads=[Rr(p_mm[i % 2])], writes=[Rr(dst, (ol, tt))])

    def rope_proj(Wap, nch, src_getter, src_res, out_dram, tag):
        for oc in range(8):
            sc.dma("gpsimd", lambda g, Wap=Wap, oc=oc, nch=nch, tag=tag: g.dma_start(
                out=wsl[:, oc % 2, 0:nch],
                in_=Wap[oc].rearrange("p (k f) -> p k f", f=128)),
                writes=[Rr(wsl, oc % 2)])
            for tt in range(NTT):
                i = oc * NTT + tt
                for c in range(nch):
                    sc.op("tensor", lambda t, oc=oc, tt=tt, c=c, i=i, nch=nch: t.matmul(
                        p_mm[i % 2][:], wsl[:, oc % 2, c], src_getter(c, tt),
                        start=(c == 0), stop=(c == nch - 1)),
                        reads=[Rr(wsl, oc % 2), src_res(c, tt)], writes=[Rr(p_mm[i % 2])])
                sc.op("vector", lambda v, i=i: v.tensor_copy(out=stg[:, i % 2], in_=p_mm[i % 2][:]),
                      reads=[Rr(p_mm[i % 2])], writes=[Rr(stg, i % 2)])
                sc.op("tensor", lambda t, i=i: t.matmul(p_rot[:], rtb[:], stg[:, i % 2],
                                                        start=True, stop=True),
                      reads=[Rr(rtb), Rr(stg, i % 2)], writes=[Rr(p_rot)])
                sc.op("vector", lambda v, tt=tt, i=i: v.tensor_tensor(
                    out=t1s[:, i % 2], in0=stg[:, i % 2], in1=cosb[:, tt * 512:(tt + 1) * 512],
                    op=AL.mult),
                    reads=[Rr(stg, i % 2), Rr(cosb)], writes=[Rr(t1s, i % 2)])
                sc.op("vector", lambda v, tt=tt, i=i: v.tensor_tensor(
                    out=t2s[:, i % 2], in0=p_rot[:], in1=sinb[:, tt * 512:(tt + 1) * 512],
                    op=AL.mult),
                    reads=[Rr(p_rot), Rr(sinb)], writes=[Rr(t2s, i % 2)])
                sc.op("vector", lambda v, i=i: v.tensor_tensor(
                    out=stg2[:, i % 2], in0=t1s[:, i % 2], in1=t2s[:, i % 2], op=AL.add),
                    reads=[Rr(t1s, i % 2), Rr(t2s, i % 2)], writes=[Rr(stg2, i % 2)])
                for half in range(2):
                    h = oc * 2 + half
                    sc.dma("sync", lambda s, h=h, half=half, tt=tt, i=i, out_dram=out_dram: s.dma_start(
                        out=out_dram[h * 128 + 64:(h * 128) + 128, tt * 512:(tt + 1) * 512],
                        in_=stg2[half * 64:(half + 1) * 64, i % 2]),
                        reads=[Rr(stg2, i % 2)], writes=[Rr(out_dram, (h, tt, "r"))],
                        slot=Rr(stg2, (i % 2, "st")))

    rope_proj(ropekw, DC, lambda c, tt: A5[:, c, tt * 512:(tt + 1) * 512], A5r, k5_out, "rk")
    rope_proj(ropeqw, LC, lambda c, tt: qdb[:, c, tt * 512:(tt + 1) * 512],
              lambda c, tt: Rr(qdb, (c, tt)), q5_out, "rq")

    def nope_proj(Wap, src, src_res, out_dram):
        for oc in range(8):
            sc.dma("gpsimd", lambda g, Wap=Wap, oc=oc: g.dma_start(
                out=wsl[:, oc % 2, 0:LC],
                in_=Wap[oc].rearrange("p (k f) -> p k f", f=128)),
                writes=[Rr(wsl, oc % 2)])
            for tt in range(NTT):
                i = oc * NTT + tt
                for c in range(LC):
                    sc.op("tensor", lambda t, oc=oc, tt=tt, c=c, i=i, src=src: t.matmul(
                        p_mm[i % 2][:], wsl[:, oc % 2, c], src[:, c, tt * 512:(tt + 1) * 512],
                        start=(c == 0), stop=(c == LC - 1)),
                        reads=[Rr(wsl, oc % 2), src_res(c, tt)], writes=[Rr(p_mm[i % 2])])
                sc.op("vector", lambda v, i=i: v.tensor_copy(out=stg2[:, i % 2], in_=p_mm[i % 2][:]),
                      reads=[Rr(p_mm[i % 2])], writes=[Rr(stg2, i % 2)])
                for half in range(2):
                    h = oc * 2 + half
                    sc.dma("sync", lambda s, h=h, half=half, tt=tt, i=i, out_dram=out_dram: s.dma_start(
                        out=out_dram[h * 128:h * 128 + 64, tt * 512:(tt + 1) * 512],
                        in_=stg2[half * 64:(half + 1) * 64, i % 2]),
                        reads=[Rr(stg2, i % 2)], writes=[Rr(out_dram, (h, tt, "n"))],
                        slot=Rr(stg2, (i % 2, "st")))

    nope_proj(kuw, kvd, lambda c, tt: Rr(kvd, (c, tt)), k5_out)
    nope_proj(quw, qdb, lambda c, tt: Rr(qdb, (c, tt)), q5_out)

    for vc in range(4):
        sc.dma("gpsimd", lambda g, vc=vc: g.dma_start(
            out=wvs[:, vc % 2],
            in_=vuw[vc].rearrange("p (k f) -> p k f", f=512)),
            writes=[Rr(wvs, vc % 2)])
        for to in range(16):
            i = vc * 16 + to
            for c in range(LC):
                sc.op("tensor", lambda t, vc=vc, to=to, c=c, i=i: t.matmul(
                    p_mm[i % 2][:], kvd[:, c, to * 128:(to + 1) * 128], wvs[:, vc % 2, c],
                    start=(c == 0), stop=(c == LC - 1)),
                    reads=[Rr(wvs, vc % 2), Rr(kvd, (c, to // 4))], writes=[Rr(p_mm[i % 2])])
            sc.op("vector", lambda v, i=i: v.tensor_copy(out=stg[:, i % 2], in_=p_mm[i % 2][:]),
                  reads=[Rr(p_mm[i % 2])], writes=[Rr(stg, i % 2)])
            sc.dma("sync", lambda s, vc=vc, to=to, i=i: s.dma_start(
                out=v5_out[to * 128:(to + 1) * 128, vc * 512:(vc + 1) * 512], in_=stg[:, i % 2]),
                reads=[Rr(stg, i % 2)], writes=[Rr(v5_out, (vc, to))], slot=Rr(stg, i % 2))



def build_p5():
    p = P("p5")
    y_in = p.inp("y_dm", [D, TOK], BF16)
    sc = Sched(p)
    io = dict(
        y_ap=lambda c: y_in.rearrange("(c p) t -> p c t", p=128)[:, c],
        x2_dm=p.inp("x2_dm", [D, TOK]),
        opwT=p.inp("opwT", [16, 128, DC * 128]), opb=p.inp("opb", [D]), ln1_w=p.inp("ln1_w", [D]),
        kv_d_w=p.inp("kv_d_w", [4, 128, DC * 128]), q_d_w=p.inp("q_d_w", [4, 128, DC * 128]),
        rope_k_w=p.inp("rope_k_w", [8, 128, DC * 128]),
        k_u_w=p.inp("k_u_w", [8, 128, LC * 128]), q_u_w=p.inp("q_u_w", [8, 128, LC * 128]),
        v_u_w=p.inp("v_u_w", [4, 128, LC * 512]), rope_q_w=p.inp("rope_q_w", [8, 128, LC * 128]),
        cos_t=p.inp("cos_t", [128, TOK]), sin_t=p.inp("sin_t", [128, TOK]),
        RT=p.inp("RT", [128, 128]),
        x3_dm=p.out("x3_dm", [D, TOK]),
        xin_dm=p.out("xin_dm", [D, TOK]),
        q5=p.out("q5", [D, TOK], BF16),
        k5=p.out("k5", [D, TOK], BF16),
        v5=p.out("v5", [TOK, D], BF16),
    )
    emit_p5(p, sc, io)
    sc.emit()
    return p.done()


def host_p5_inputs(p4res, p3res, inputs, core):
    b, hh = core // 2, core % 2
    sl = slice(hh * TOK, (hh + 1) * TOK)
    yfull = np.concatenate([np.asarray(p4res[cc]["y4"]).reshape(4, 256, S)[b] for cc in range(8)], axis=0)
    RD = 64
    inv_freq = 1.0 / 10000.0 ** (np.arange(0, RD, 2, dtype=np.float32) / RD)
    t_ = (np.arange(TOK, dtype=np.float32) + hh * TOK)
    ang = np.outer(t_, inv_freq)
    emb = np.concatenate([ang, ang], -1)
    cos64 = np.cos(emb).T.astype(np.float32)
    sin64 = np.sin(emb).T.astype(np.float32)
    cos_t = np.concatenate([cos64, cos64], 0)
    sin_t = np.concatenate([sin64, sin64], 0)
    R64 = np.zeros((64, 64), np.float32)
    for j in range(32):
        R64[j, j + 32] = -1.0
        R64[j + 32, j] = 1.0
    RT64 = R64.T
    RT = np.zeros((128, 128), np.float32)
    RT[0:64, 0:64] = RT64
    RT[64:128, 64:128] = RT64
    return {
        "y_dm": np.ascontiguousarray(yfull[:, sl]),
        "x2_dm": np.asarray(p3res[core]["x2_dm"]),
        "opwT": _pk(np.asarray(inputs["L1_attn_out_proj_w"]).T),
        "opb": np.asarray(inputs["L1_attn_out_proj_b"]),
        "ln1_w": np.asarray(inputs["L1_ffn_ln1_w"]),
        "kv_d_w": _pk(inputs["L1_ffn_kv_d_w"]),
        "q_d_w": _pk(inputs["L1_ffn_q_d_w"]),
        "rope_k_w": _pk(inputs["L1_ffn_rope_k_w"]),
        "k_u_w": _pk(inputs["L1_ffn_k_u_w"]),
        "q_u_w": _pk(inputs["L1_ffn_q_u_w"]),
        "v_u_w": _pk(inputs["L1_ffn_v_u_w"], 512),
        "rope_q_w": _pk(inputs["L1_ffn_rope_q_w"]),
        "cos_t": cos_t, "sin_t": sin_t, "RT": RT,
    }


def emit_p6(p, sc, io):
    nc = p.nc
    q_srcs = io["q_srcs"]; k_srcs = io["k_srcs"]; v_srcs = io["v_srcs"]
    ao_out = io["ao6"]
    nm_dm = p.dram("negm_dram", [8, 32, 128])

    qh = p.sb("qh", [128, 2, S], BF16)
    kh = p.sb("kh", [128, 2, S], BF16)
    vh = p.sb("vh", [128, 2, S // 128, 128], BF16)
    mx = p.sb("mx", [128, 32], F32)
    mt = p.sb("mt", [128, 2, 1], F32)
    mts = p.sb("mts", [128, 2, 8], F32)
    nmS = p.sb("nmS", [128, 2, 512], F32)
    sst = p.sb("sst", [128, 3, 512], F32)
    mte = p.sb("mte", [32, 128], F32)
    nmb = p.sb("nmb", [1, 2, S], BF16)
    probs = p.sb("probs", [128, 3, 512], BF16)
    rb = p.sb("rb", [128, 512], F32)
    rrow = p.sb("rrow", [1, 512], F32)
    aos = p.sb("aos", [128, 2, 512], BF16)
    ident = p.sb("ident", [128, 128], F32)
    ones_r = p.sb("ones_r", [1, 128], F32)
    ones_kb = p.sb("ones_kb", [1, 128], BF16)
    ones_cb = p.sb("ones_cb", [128, 1], BF16)

    p_s = [p.ps("p_s%d" % k, [128, 512], F32) for k in range(3)]
    p_sum = p.ps("p_sum", [1, 512], F32)
    p_pv = p.ps("p_pv", [128, 512], F32)
    p_bc = p.ps("p_bc", [128, 512], F32)
    p_mt = p.ps("p_mt", [32, 128], F32)
    p_nm = p.ps("p_nm", [128, 512], F32)

    sc.exclusive(*p_s, p_sum, p_pv, p_bc, p_mt, p_nm)
    AF_, AL = AF, ALU
    SCALE = 1.0 / np.sqrt(128.0)

    sc.op("vector", lambda v: v.memset(ones_r[:], 1.0), writes=[Rr(ones_r)])
    sc.op("vector", lambda v: v.memset(ones_kb[:], 1.0), writes=[Rr(ones_kb)])
    sc.op("vector", lambda v: v.memset(ones_cb[:], 1.0), writes=[Rr(ones_cb)])
    sc.op("gpsimd", lambda g: g.memset(ident[:], 1.0), writes=[Rr(ident)])
    sc.op("gpsimd", lambda g: g.affine_select(out=ident[:], in_=ident[:], pattern=[[-1, 128]],
                                              compare_op=AL.is_equal, fill=0.0,
                                              base=0, channel_multiplier=1),
          reads=[Rr(ident)], writes=[Rr(ident)])

    for h in range(8):
        for col0, w, src in q_srcs(h):
            sc.dma("sync", lambda s, h=h, col0=col0, w=w, src=src: s.dma_start(
                out=qh[:, h % 2, col0:col0 + w], in_=src),
                reads=[Rr(q_srcs, h)], writes=[Rr(qh, h % 2)])
        for col0, w, src in k_srcs(h):
            sc.dma("sync", lambda s, h=h, col0=col0, w=w, src=src: s.dma_start(
                out=kh[:, h % 2, col0:col0 + w], in_=src),
                reads=[Rr(k_srcs, h)], writes=[Rr(kh, h % 2)])
        for m0, mc, src in v_srcs(h):
            sc.dma("sync", lambda s, h=h, m0=m0, mc=mc, src=src: s.dma_start(
                out=vh[:, h % 2, m0:m0 + mc], in_=src),
                reads=[Rr(v_srcs, h)], writes=[Rr(vh, h % 2)])
        for qt in range(32):
            kcn = qt // 4 + 1
            for kc in range(kcn):
                i = qt * 8 + kc
                sc.op("tensor", lambda t, h=h, qt=qt, kc=kc, i=i: t.matmul(
                    p_s[i % 3][:], qh[:, h % 2, qt * 128:(qt + 1) * 128],
                    kh[:, h % 2, kc * 512:(kc + 1) * 512], start=True, stop=True),
                    reads=[Rr(qh, h % 2), Rr(kh, h % 2)], writes=[Rr(p_s[i % 3])])
                sc.op("vector", lambda v, qt=qt, i=i, kc=kc: v.reduce_max(
                    out=mts[:, qt % 2, kc:kc + 1], in_=p_s[i % 3][:], axis=AX),
                    reads=[Rr(p_s[i % 3])], writes=[Rr(mts, (qt % 2, kc))])
            sc.op("vector", lambda v, qt=qt, kcn=kcn: v.reduce_max(
                out=mx[:, qt:qt + 1], in_=mts[:, qt % 2, 0:kcn], axis=AX),
                reads=[Rr(mts, (qt % 2, kc)) for kc in range(kcn)], writes=[Rr(mx, qt)])
        sc.op("vector", lambda v: v.tensor_scalar(out=mx[:], in0=mx[:], scalar1=-SCALE * 0 - 1.0,
                                                  scalar2=None, op0=AL.mult),
              reads=[Rr(mx, qt) for qt in range(32)], writes=[Rr(mx, "neg")])
        sc.op("tensor", lambda t: t.transpose(p_mt[:], mx[:], ident[:]),
              reads=[Rr(mx, "neg"), Rr(ident)], writes=[Rr(p_mt)])
        sc.op("vector", lambda v: v.tensor_copy(out=mte[:], in_=p_mt[:]),
              reads=[Rr(p_mt)], writes=[Rr(mte)])
        sc.dma("sync", lambda s, h=h: s.dma_start(out=nm_dm[h], in_=mte[:]),
               reads=[Rr(mte)], writes=[Rr(nm_dm, h)], slot=Rr(mte, "st"))
        sc.dma("gpsimd", lambda g, h=h: g.dma_start(
            out=nmb[:, h % 2], in_=nm_dm[h].rearrange("a b -> (a b)").unsqueeze(0)),
            reads=[Rr(nm_dm, h)], writes=[Rr(nmb, h % 2)])
        for qc in range(8):
            lastkt = qc * 4 + 3
            sc.op("tensor", lambda t, h=h, qc=qc: t.matmul(
                p_nm[:], ones_kb[:], nmb[0:1, h % 2, qc * 512:(qc + 1) * 512],
                start=True, stop=True),
                reads=[Rr(ones_kb), Rr(nmb, h % 2)], writes=[Rr(p_nm)])
            sc.op("vector", lambda v, qc=qc: v.tensor_copy(out=nmS[:, qc % 2], in_=p_nm[:]),
                  reads=[Rr(p_nm)], writes=[Rr(nmS, qc % 2)])

            def emit_score(kt, h=h, qc=qc):
                sc.op("tensor", lambda t, h=h, qc=qc, kt=kt: t.matmul(
                    p_s[kt % 3][:], kh[:, h % 2, kt * 128:(kt + 1) * 128],
                    qh[:, h % 2, qc * 512:(qc + 1) * 512], start=True, stop=True),
                    reads=[Rr(kh, h % 2), Rr(qh, h % 2)], writes=[Rr(p_s[kt % 3])])

            emit_score(0)
            emit_score(1)
            for kt in range(lastkt + 1):
                i = kt
                if kt + 2 <= lastkt:
                    emit_score(kt + 2)
                sc.op("vector", lambda v, qc=qc, i=i: v.tensor_tensor(
                    out=sst[:, i % 3], in0=p_s[i % 3][:], in1=nmS[:, qc % 2], op=AL.add),
                    reads=[Rr(p_s[i % 3]), Rr(nmS, qc % 2)], writes=[Rr(sst, i % 3)])
                sc.op("scalar", lambda s, i=i: s.activation(
                    out=probs[:, i % 3], in_=sst[:, i % 3], func=AF_.Exp, scale=SCALE),
                    reads=[Rr(sst, i % 3)], writes=[Rr(probs, i % 3)])
                if kt >= qc * 4:
                    sc.op("gpsimd", lambda g, qc=qc, kt=kt, i=i: g.affine_select(
                        out=probs[:, i % 3], in_=probs[:, i % 3], pattern=[[1, 512]],
                        compare_op=AL.is_ge, fill=0.0,
                        base=qc * 512 - kt * 128, channel_multiplier=-1),
                        reads=[Rr(probs, i % 3)], writes=[Rr(probs, i % 3)])
                sc.op("tensor", lambda t, kt=kt, lastkt=lastkt, i=i: t.matmul(
                    p_sum[:], ones_cb[:], probs[:, i % 3], start=(kt == 0), stop=(kt == lastkt)),
                    reads=[Rr(probs, i % 3), Rr(ones_cb)], writes=[Rr(p_sum)])
                sc.op("tensor", lambda t, h=h, kt=kt, lastkt=lastkt, i=i: t.matmul(
                    p_pv[:], vh[:, h % 2, kt], probs[:, i % 3], start=(kt == 0), stop=(kt == lastkt)),
                    reads=[Rr(vh, h % 2), Rr(probs, i % 3)], writes=[Rr(p_pv)])
            sc.op("vector", lambda v: v.reciprocal(out=rrow[:], in_=p_sum[:]),
                  reads=[Rr(p_sum)], writes=[Rr(rrow)])
            sc.op("tensor", lambda t: t.matmul(p_bc[:], ones_r[:], rrow[:], start=True, stop=True),
                  reads=[Rr(rrow), Rr(ones_r)], writes=[Rr(p_bc)])
            sc.op("vector", lambda v: v.tensor_copy(out=rb[:], in_=p_bc[:]),
                  reads=[Rr(p_bc)], writes=[Rr(rb)])
            sc.op("vector", lambda v, qc=qc: v.tensor_tensor(out=aos[:, qc % 2], in0=p_pv[:],
                                                             in1=rb[:], op=AL.mult),
                  reads=[Rr(p_pv), Rr(rb)], writes=[Rr(aos, qc % 2)])
            sc.dma("sync", lambda s, h=h, qc=qc: s.dma_start(
                out=ao_out[h * 128:(h + 1) * 128, qc * 512:(qc + 1) * 512], in_=aos[:, qc % 2]),
                reads=[Rr(aos, qc % 2)], writes=[Rr(ao_out, (h, qc))], slot=Rr(aos, qc % 2))



def build_p6():
    p = P("p6")
    q_in = p.inp("q6", [1024, S], BF16)
    k_in = p.inp("k6", [1024, S], BF16)
    v_in = p.inp("v6", [S, 1024], BF16)
    sc = Sched(p)
    io = dict(
        q_srcs=lambda h: [(0, S, q_in[h * 128:(h + 1) * 128, :])],
        k_srcs=lambda h: [(0, S, k_in[h * 128:(h + 1) * 128, :])],
        v_srcs=lambda h: [(0, S // 128,
                           v_in.rearrange("(m p) e -> p m e", p=128)[:, :, h * 128:(h + 1) * 128])],
        ao6=p.out("ao6", [1024, S], BF16),
    )
    emit_p6(p, sc, io)
    sc.emit()
    return p.done()


def host_p6_inputs(p5res, inputs, core):
    b, hs = core // 2, (core % 2) * 8
    q = np.concatenate([p5res[2 * b]["q5"], p5res[2 * b + 1]["q5"]], axis=1)
    k = np.concatenate([p5res[2 * b]["k5"], p5res[2 * b + 1]["k5"]], axis=1)
    v = np.concatenate([p5res[2 * b]["v5"], p5res[2 * b + 1]["v5"]], axis=0)
    return {
        "q6": np.ascontiguousarray(q[hs * 128:(hs + 8) * 128]),
        "k6": np.ascontiguousarray(k[hs * 128:(hs + 8) * 128]),
        "v6": np.ascontiguousarray(v[:, hs * 128:(hs + 8) * 128]),
    }


def emit_p7(p, sc, io):
    nc = p.nc
    ao_ap = io["ao_ap"]
    xin_in = io["xin_dm"]; x3_in = io["x3_dm"]
    ow_in = io["o_w"]; ln2_in = io["ln2_w"]
    shg = io["sh_gate"]; shu = io["sh_up"]; shd = io["sh_down"]
    rg = io["r_gate"]; ru = io["r_up"]; rd = io["r_down"]
    rw_in = io["router_w"]; rb_in = io["router_b"]
    x5_out = io["x5_dm"]

    h_dm = p.dram("h_dram", [D, TOK])
    o7_dm = p.dram("o7_dram", [D, TOK], BF16)
    gu_dm = p.dram("gu_dram", [8, 1024, TOK], BF16)
    wr_dm = p.dram("wr_dram", [7, TOK])

    A7 = p.sb("A7", [128, DC, TOK], BF16)
    guh = p.sb("guh", [128, 8, 8, 512], BF16)
    wsl = p.sb("wsl", [128, 2, DC, 128], BF16)
    wu2 = p.sb("wu2", [128, 2, DC, 128], BF16)
    wdl = p.sb("wdl", [128, 2, 8, 128], BF16)
    wrb = p.sb("wrb", [128, DC, 7], BF16)
    rbr = p.sb("rbr", [1, 7], BF16)
    xc = p.sb("xc", [128, 2, 512], F32)
    osq = p.sb("osq", [128, 2, 512], F32)
    asq = p.sb("asq", [128, 2, 512], F32)
    t1b = p.sb("t1b", [128, 2, 512], F32)
    stg = p.sb("stg", [128, 2, 512], BF16)
    gstg = p.sb("gstg", [128, 2, 512], BF16)
    fstg = p.sb("fstg", [128, 2, 512], F32)
    wbc = p.sb("wbc", [128, NTT, 512], BF16)
    rbc = p.sb("rbc", [128, 2, 512], F32)
    mrbc = p.sb("mrbc", [128, 2, 512], F32)
    prr = p.sb("prr", [128, 2, 7], F32)
    m1 = p.sb("m1", [128, 2, 1], F32)
    m2 = p.sb("m2", [128, 2, 1], F32)
    eqm = p.sb("eqm", [128, 2, 7], F32)
    wr_sb = p.sb("wr_sb", [7, 128], F32)
    wrow = p.sb("wrow", [1, TOK], BF16)
    srow = p.sb("srow", [1, 5, 512], F32)
    vrow = p.sb("vrow", [1, 512], F32)
    ln2_c = p.sb("ln2_c", [128, DC], F32)
    ones_c = p.sb("ones_c", [128, 1], F32)
    ones_cb = p.sb("ones_cb", [128, 1], BF16)
    ones_r = p.sb("ones_r", [1, 128], F32)
    ones_rb = p.sb("ones_rb", [1, 128], BF16)
    identf = p.sb("identf", [1, 1], F32)
    ident = p.sb("ident", [128, 128], F32)

    p_mm = [p.ps("p_mm%d" % k, [128, 512], F32) for k in range(4)]
    p_sq = [p.ps("p_sqA", [128, 512], F32), p.ps("p_sqB", [128, 512], F32)]
    p_r = p.ps("p_r", [128, 512], F32)
    p_m = p.ps("p_m", [128, 512], F32)

    def ps_s(k):
        return p_sq[k][0:1, 0:512]

    def ps_q(k):
        return p_sq[k][64:65, 0:512]

    def rs_s(k):
        return Rr(p_sq[k], "s")

    def rs_q(k):
        return Rr(p_sq[k], "q")

    arow = [p_sq[0][0:1, 0:512], p_sq[0][32:33, 0:512], p_sq[0][64:65, 0:512],
            p_sq[1][0:1, 0:512]]
    arkey = [Rr(p_sq[0], "s"), Rr(p_sq[0], "m32"), Rr(p_sq[0], "q"), Rr(p_sq[1], "s")]

    sc.exclusive(*p_mm, p_sq[0], p_sq[1], p_r, p_m)
    AF_, AL = AF, ALU

    for buf, val in [(ones_c, 1.0), (ones_cb, 1.0), (ones_r, 1.0), (ones_rb, 1.0),
                     (identf, 1.0)]:
        sc.op("vector", lambda v, buf=buf, val=val: v.memset(buf[:], val), writes=[Rr(buf)])
    sc.op("gpsimd", lambda g: g.memset(ident[:], 1.0), writes=[Rr(ident)])
    sc.op("gpsimd", lambda g: g.affine_select(out=ident[:], in_=ident[:], pattern=[[-1, 128]],
                                              compare_op=AL.is_equal, fill=0.0,
                                              base=0, channel_multiplier=1),
          reads=[Rr(ident)], writes=[Rr(ident)])
    sc.dma("gpsimd", lambda g: g.dma_start(out=wrb[:], in_=rw_in[0].rearrange("p (k f) -> p k f", f=7)),
           writes=[Rr(wrb)])
    sc.dma("gpsimd", lambda g: g.dma_start(out=rbr[:], in_=rb_in.unsqueeze(0)), writes=[Rr(rbr)])
    for c in range(DC):
        sc.dma("sync", lambda s, c=c: s.dma_start(
            out=A7[:, c], in_=ao_ap(c)),
            reads=[Rr(ao_ap, c)], writes=[Rr(A7, (c, "ao"))], slot=Rr(A7, "ld"))

    def vec_cols(src, n, dst):
        for seg in range(0, n, 512):
            w = min(512, n - seg)
            sc.dma("sync", lambda s, src=src, seg=seg, w=w: s.dma_start(
                out=vrow[0:1, 0:w], in_=src.unsqueeze(0)[:, seg:seg + w]),
                reads=[], writes=[Rr(vrow)])
            for cc in range(w // 128):
                c = seg // 128 + cc
                sc.op("tensor", lambda t, cc=cc: t.transpose(p_m[:, 0:1],
                                                             vrow[0:1, cc * 128:(cc + 1) * 128], identf[:]),
                      reads=[Rr(vrow), Rr(identf)], writes=[Rr(p_m)])
                sc.op("vector", lambda v, c=c, dst=dst: v.tensor_copy(out=dst[:, c:c + 1], in_=p_m[:, 0:1]),
                      reads=[Rr(p_m)], writes=[Rr(dst)])

    vec_cols(ln2_in, D, ln2_c)

    fzc = [0]

    def finalize(mean, aps_s, aps_q, eps):
        sl = fzc[0] % 2
        fzc[0] += 1
        if mean:
            sc.op("vector", lambda v: v.tensor_scalar(out=srow[:, 0], in0=aps_s[0], scalar1=1.0 / D,
                                                      scalar2=None, op0=AL.mult),
                  reads=[aps_s[1]], writes=[Rr(srow, 0)])
            sc.op("vector", lambda v: v.tensor_scalar(out=srow[:, 1], in0=aps_q[0], scalar1=1.0 / D,
                                                      scalar2=None, op0=AL.mult),
                  reads=[aps_q[1]], writes=[Rr(srow, 1)])
            sc.op("vector", lambda v: v.tensor_tensor(out=srow[:, 2], in0=srow[:, 0], in1=srow[:, 0],
                                                      op=AL.mult),
                  reads=[Rr(srow, 0)], writes=[Rr(srow, 2)])
            sc.op("vector", lambda v: v.tensor_tensor(out=srow[:, 2], in0=srow[:, 1], in1=srow[:, 2],
                                                      op=AL.subtract),
                  reads=[Rr(srow, 1), Rr(srow, 2)], writes=[Rr(srow, 2)])
            sc.op("vector", lambda v: v.tensor_scalar(out=srow[:, 2], in0=srow[:, 2], scalar1=eps,
                                                      scalar2=None, op0=AL.add),
                  reads=[Rr(srow, 2)], writes=[Rr(srow, 2)])
        else:
            sc.op("vector", lambda v: v.tensor_scalar(out=srow[:, 2], in0=aps_q[0], scalar1=1.0 / D,
                                                      scalar2=eps, op0=AL.mult, op1=AL.add),
                  reads=[aps_q[1]], writes=[Rr(srow, 2)])
        sc.op("scalar", lambda s: s.activation(out=srow[:, 3], in_=srow[:, 2], func=AF_.Ln),
              reads=[Rr(srow, 2)], writes=[Rr(srow, 3)])
        sc.op("scalar", lambda s: s.activation(out=srow[:, 3], in_=srow[:, 3], func=AF_.Exp, scale=-0.5),
              reads=[Rr(srow, 3)], writes=[Rr(srow, 3)])
        sc.op("tensor", lambda t: t.matmul(p_r[:], ones_r[:], srow[:, 3], start=True, stop=True),
              reads=[Rr(srow, 3), Rr(ones_r)], writes=[Rr(p_r)])
        sc.op("scalar", lambda s, sl=sl: s.activation(out=rbc[:, sl], in_=p_r[:], func=AF_.Copy),
              reads=[Rr(p_r)], writes=[Rr(rbc, sl)])
        if mean:
            sc.op("vector", lambda v: v.scalar_tensor_tensor(out=srow[:, 4], in0=srow[:, 0],
                                                             scalar=-1.0, in1=srow[:, 3],
                                                             op0=AL.mult, op1=AL.mult),
                  reads=[Rr(srow, 0), Rr(srow, 3)], writes=[Rr(srow, 4)])
            sc.op("tensor", lambda t: t.matmul(p_m[:], ones_r[:], srow[:, 4], start=True, stop=True),
                  reads=[Rr(srow, 4), Rr(ones_r)], writes=[Rr(p_m)])
            sc.op("scalar", lambda s, sl=sl: s.activation(out=mrbc[:, sl], in_=p_m[:], func=AF_.Copy),
                  reads=[Rr(p_m)], writes=[Rr(mrbc, sl)])
        return sl

    for oc in range(16):
        sc.dma("gpsimd", lambda g, oc=oc: g.dma_start(
            out=wsl[:, oc % 2],
            in_=ow_in[oc].rearrange("p (k f) -> p k f", f=128)),
            writes=[Rr(wsl, oc % 2)])
        for tt in range(NTT):
            i = oc * NTT + tt
            pm = p_mm[i % 4]
            for c in range(DC):
                sc.op("tensor", lambda t, oc=oc, tt=tt, c=c, pm=pm: t.matmul(
                    pm[:], wsl[:, oc % 2, c], A7[:, c, tt * 512:(tt + 1) * 512],
                    start=(c == 0), stop=(c == DC - 1)),
                    reads=[Rr(wsl, oc % 2), Rr(A7, (c, "ao"))], writes=[Rr(pm)])
            sc.dma("sync", lambda s, oc=oc, tt=tt, i=i: s.dma_start(
                out=xc[:, i % 2],
                in_=xin_in.rearrange("(c p) (t f) -> p c t f", p=128, f=512)[:, oc, tt]),
                writes=[Rr(xc, i % 2)])
            sc.op("vector", lambda v, i=i, pm=pm: v.tensor_tensor(out=fstg[:, i % 2], in0=xc[:, i % 2],
                                                                  in1=pm[:], op=AL.add),
                  reads=[Rr(xc, i % 2), Rr(pm)], writes=[Rr(fstg, i % 2)])
            sc.dma("sync", lambda s, oc=oc, tt=tt, i=i: s.dma_start(
                out=h_dm.rearrange("(c p) (t f) -> p c t f", p=128, f=512)[:, oc, tt], in_=fstg[:, i % 2]),
                reads=[Rr(fstg, i % 2)], writes=[Rr(h_dm, (oc, tt))], slot=Rr(fstg, i % 2))
            sc.op("scalar", lambda s, i=i: s.activation(out=osq[:, i % 2], in_=fstg[:, i % 2],
                                                        func=AF_.Square),
                  reads=[Rr(fstg, i % 2)], writes=[Rr(osq, i % 2)])
            sc.op("tensor", lambda t, oc=oc, tt=tt, i=i: t.matmul(
                arow[tt], ones_c[:], osq[:, i % 2], start=(oc == 0), stop=(oc == 15)),
                reads=[Rr(osq, i % 2), Rr(ones_c)], writes=[arkey[tt]])

    for tt in range(NTT):
        sl = finalize(False, None, (arow[tt], arkey[tt]), 1e-5)
        for c in range(DC):
            i = tt * DC + c
            sc.dma("sync", lambda s, tt=tt, c=c, i=i: s.dma_start(
                out=xc[:, i % 2],
                in_=h_dm.rearrange("(c p) (t f) -> p c t f", p=128, f=512)[:, c, tt]),
                reads=[Rr(h_dm, (c, tt))], writes=[Rr(xc, i % 2)])
            sc.op("gpsimd", lambda g, i=i, sl=sl: g.tensor_tensor(
                out=t1b[:, i % 2], in0=xc[:, i % 2], in1=rbc[:, sl], op=AL.mult),
                reads=[Rr(xc, i % 2), Rr(rbc, sl)], writes=[Rr(t1b, i % 2)])
            sc.op("vector", lambda v, tt=tt, c=c, i=i: v.tensor_scalar(
                out=A7[:, c, tt * 512:(tt + 1) * 512], in0=t1b[:, i % 2],
                scalar1=ln2_c[:, c:c + 1], scalar2=None, op0=AL.mult),
                reads=[Rr(t1b, i % 2), Rr(ln2_c)], writes=[Rr(A7, (c, tt)), Rr(A7, (c, "ao"))])

    def A7r(c, tt):
        return Rr(A7, (c, tt))

    for t1 in range(16):
        i = t1
        tt = t1 // 4
        pm = p_mm[t1 % 4]
        for c in range(DC):
            sc.op("tensor", lambda t, t1=t1, tt=tt, c=c, pm=pm: t.matmul(
                pm[:, 0:7], A7[:, c, t1 * 128:(t1 + 1) * 128], wrb[:, c],
                start=(c == 0), stop=False),
                reads=[A7r(c, tt), Rr(wrb)], writes=[Rr(pm)])
        sc.op("tensor", lambda t, pm=pm: t.matmul(pm[:, 0:7], ones_rb[:], rbr[:],
                                                  start=False, stop=True),
              reads=[Rr(ones_rb), Rr(rbr)], writes=[Rr(pm)])
        sc.op("scalar", lambda s, i=i, pm=pm: s.activation(out=prr[:, i % 2], in_=pm[:, 0:7],
                                                           func=AF_.Sigmoid),
              reads=[Rr(pm)], writes=[Rr(prr, i % 2)])
        sc.op("vector", lambda v, i=i: v.reduce_max(out=m1[:, i % 2], in_=prr[:, i % 2], axis=AX),
              reads=[Rr(prr, i % 2)], writes=[Rr(m1, i % 2)])
        sc.op("vector", lambda v, i=i: v.tensor_scalar(out=eqm[:, i % 2], in0=prr[:, i % 2],
                                                       scalar1=m1[:, i % 2, 0:1], scalar2=None,
                                                       op0=AL.is_equal),
              reads=[Rr(prr, i % 2), Rr(m1, i % 2)], writes=[Rr(eqm, i % 2)])
        sc.op("vector", lambda v, i=i: v.scalar_tensor_tensor(out=eqm[:, i % 2], in0=eqm[:, i % 2],
                                                              scalar=-1e30, in1=prr[:, i % 2],
                                                              op0=AL.mult, op1=AL.add),
              reads=[Rr(eqm, i % 2), Rr(prr, i % 2)], writes=[Rr(eqm, i % 2)])
        sc.op("vector", lambda v, i=i: v.reduce_max(out=m2[:, i % 2], in_=eqm[:, i % 2], axis=AX),
              reads=[Rr(eqm, i % 2)], writes=[Rr(m2, i % 2)])
        sc.op("vector", lambda v, i=i: v.tensor_scalar(out=eqm[:, i % 2], in0=prr[:, i % 2],
                                                       scalar1=m2[:, i % 2, 0:1], scalar2=None,
                                                       op0=AL.is_ge),
              reads=[Rr(prr, i % 2), Rr(m2, i % 2)], writes=[Rr(eqm, i % 2)])
        sc.op("vector", lambda v, i=i: v.tensor_tensor(out=prr[:, i % 2], in0=prr[:, i % 2],
                                                       in1=eqm[:, i % 2], op=AL.mult),
              reads=[Rr(prr, i % 2), Rr(eqm, i % 2)], writes=[Rr(prr, i % 2)])
        sc.op("vector", lambda v, i=i: v.tensor_tensor(out=m1[:, i % 2], in0=m1[:, i % 2],
                                                       in1=m2[:, i % 2], op=AL.add),
              reads=[Rr(m1, i % 2), Rr(m2, i % 2)], writes=[Rr(m1, i % 2)])
        sc.op("vector", lambda v, i=i: v.reciprocal(out=m1[:, i % 2], in_=m1[:, i % 2]),
              reads=[Rr(m1, i % 2)], writes=[Rr(m1, i % 2)])
        sc.op("vector", lambda v, i=i: v.tensor_scalar(out=prr[:, i % 2], in0=prr[:, i % 2],
                                                       scalar1=m1[:, i % 2, 0:1], scalar2=0.125,
                                                       op0=AL.mult, op1=AL.mult),
              reads=[Rr(prr, i % 2), Rr(m1, i % 2)], writes=[Rr(prr, i % 2)])
        sc.op("tensor", lambda t, i=i: t.transpose(p_m[0:7, 0:128], prr[:, i % 2], ident[:]),
              reads=[Rr(prr, i % 2), Rr(ident)], writes=[Rr(p_m)])
        sc.op("vector", lambda v: v.tensor_copy(out=wr_sb[:], in_=p_m[0:7, 0:128]),
              reads=[Rr(p_m)], writes=[Rr(wr_sb)])
        sc.dma("sync", lambda s, t1=t1: s.dma_start(out=wr_dm[:, t1 * 128:(t1 + 1) * 128], in_=wr_sb[:]),
               reads=[Rr(wr_sb)], writes=[Rr(wr_dm, t1)], slot=Rr(wr_sb, "st"))

    ents_g = [shg] + [rg[e] for e in range(7)]
    ents_u = [shu] + [ru[e] for e in range(7)]
    for e in range(8):
        if e > 0:
            sc.dma("gpsimd", lambda g, e=e: g.dma_start(out=wrow[:], in_=wr_dm[e - 1:e, :]),
                   reads=[Rr(wr_dm, t1) for t1 in range(16)], writes=[Rr(wrow)])
            for tt in range(NTT):
                sc.op("tensor", lambda t, tt=tt: t.matmul(
                    p_r[:], ones_rb[:], wrow[0:1, tt * 512:(tt + 1) * 512],
                    start=True, stop=True),
                    reads=[Rr(wrow), Rr(ones_rb)], writes=[Rr(p_r)])
                sc.op("scalar", lambda s, tt=tt: s.activation(out=wbc[:, tt], in_=p_r[:],
                                                              func=AF_.Copy),
                      reads=[Rr(p_r)], writes=[Rr(wbc, tt)])
        for uc in range(8):
            j = e * 8 + uc
            sc.dma("gpsimd", lambda g, e=e, uc=uc, j=j: g.dma_start(
                out=wsl[:, j % 2],
                in_=ents_g[e][uc].rearrange("p (k f) -> p k f", f=128)),
                writes=[Rr(wsl, j % 2)])
            sc.dma("gpsimd", lambda g, e=e, uc=uc, j=j: g.dma_start(
                out=wu2[:, j % 2],
                in_=ents_u[e][uc].rearrange("p (k f) -> p k f", f=128)),
                writes=[Rr(wu2, j % 2)])
            for tt in range(NTT):
                i = j * NTT + tt
                pg = p_mm[(2 * i) % 4]
                pu = p_mm[(2 * i + 1) % 4]
                for c in range(DC):
                    sc.op("tensor", lambda t, j=j, tt=tt, c=c, pg=pg: t.matmul(
                        pg[:], wsl[:, j % 2, c], A7[:, c, tt * 512:(tt + 1) * 512],
                        start=(c == 0), stop=(c == DC - 1)),
                        reads=[Rr(wsl, j % 2), A7r(c, tt)], writes=[Rr(pg)])
                for c in range(DC):
                    sc.op("tensor", lambda t, j=j, tt=tt, c=c, pu=pu: t.matmul(
                        pu[:], wu2[:, j % 2, c], A7[:, c, tt * 512:(tt + 1) * 512],
                        start=(c == 0), stop=(c == DC - 1)),
                        reads=[Rr(wu2, j % 2), A7r(c, tt)], writes=[Rr(pu)])
                sc.op("scalar", lambda s, i=i, pg=pg: s.activation(out=stg[:, i % 2], in_=pg[:],
                                                                   func=AF_.Silu),
                      reads=[Rr(pg)], writes=[Rr(stg, i % 2)])
                upscale = 0.125 if e == 0 else 1.0
                sc.op("scalar", lambda s, i=i, pu=pu, upscale=upscale: s.activation(
                    out=gstg[:, i % 2], in_=pu[:], func=AF_.Copy, scale=upscale),
                    reads=[Rr(pu)], writes=[Rr(gstg, i % 2)])
                sc.op("vector", lambda v, i=i: v.tensor_tensor(out=gstg[:, i % 2], in0=stg[:, i % 2],
                                                               in1=gstg[:, i % 2], op=AL.mult),
                      reads=[Rr(stg, i % 2), Rr(gstg, i % 2)], writes=[Rr(gstg, i % 2)])
                if e > 0:
                    sc.op("gpsimd", lambda g, i=i, tt=tt: g.tensor_tensor(
                        out=gstg[:, i % 2], in0=gstg[:, i % 2], in1=wbc[:, tt], op=AL.mult),
                        reads=[Rr(gstg, i % 2), Rr(wbc, tt)], writes=[Rr(gstg, i % 2)])
                sc.dma("sync", lambda s, e=e, uc=uc, tt=tt, i=i: s.dma_start(
                    out=gu_dm[e, uc * 128:(uc + 1) * 128, tt * 512:(tt + 1) * 512].rearrange(
                        "r t -> r t"), in_=gstg[:, i % 2]),
                    reads=[Rr(gstg, i % 2)], writes=[Rr(gu_dm, (e, uc, tt))], slot=Rr(gstg, i % 2))

    ents_d = [shd] + [rd[e] for e in range(7)]
    for tg in range(NTT):
        tt = tg
        sc.dma("sync", lambda s, tg=tg: s.dma_start(
            out=guh[:], in_=gu_dm.rearrange("e (u p) t -> p e u t", p=128)[:, :, :,
                                                                          tg * 512:(tg + 1) * 512]),
            reads=[Rr(gu_dm, (e, uc, tg)) for e in range(8) for uc in range(8)],
            writes=[Rr(guh)])
        for oc in range(16):
            i = tg * 16 + oc
            pm = p_mm[i % 4]
            for e in range(8):
                j2 = oc * 8 + e
                sc.dma("gpsimd", lambda g, e=e, oc=oc, j2=j2: g.dma_start(
                    out=wdl[:, j2 % 2],
                    in_=ents_d[e][oc].rearrange("p (k f) -> p k f", f=128)),
                    writes=[Rr(wdl, j2 % 2)])
                for uc in range(8):
                    sc.op("tensor", lambda t, e=e, uc=uc, j2=j2, pm=pm: t.matmul(
                        pm[:], wdl[:, j2 % 2, uc], guh[:, e, uc, :],
                        start=(e == 0 and uc == 0), stop=(e == 7 and uc == 7)),
                        reads=[Rr(wdl, j2 % 2), Rr(guh)], writes=[Rr(pm)])
            sc.dma("sync", lambda s, oc=oc, tt=tt, i=i: s.dma_start(
                out=xc[:, i % 2],
                in_=h_dm.rearrange("(c p) (t f) -> p c t f", p=128, f=512)[:, oc, tt]),
                reads=[Rr(h_dm, (oc, tt))], writes=[Rr(xc, i % 2)])
            sc.op("vector", lambda v, i=i, pm=pm: v.tensor_tensor(out=fstg[:, i % 2], in0=xc[:, i % 2],
                                                                  in1=pm[:], op=AL.add),
                  reads=[Rr(xc, i % 2), Rr(pm)], writes=[Rr(fstg, i % 2)])
            sc.dma("gpsimd", lambda g, oc=oc, tt=tt, i=i: g.dma_start(
                out=o7_dm.rearrange("(c p) (t f) -> p c t f", p=128, f=512)[:, oc, tt],
                in_=fstg[:, i % 2]),
                reads=[Rr(fstg, i % 2)], writes=[Rr(o7_dm, (oc, tt))], slot=Rr(fstg, (i % 2, "st")))
            sc.op("scalar", lambda s, i=i: s.activation(out=osq[:, i % 2], in_=fstg[:, i % 2],
                                                        func=AF_.Square),
                  reads=[Rr(fstg, i % 2)], writes=[Rr(osq, i % 2)])
            sc.op("tensor", lambda t, oc=oc, tg=tg, i=i: t.matmul(
                ps_s(tg % 2), ones_c[:], fstg[:, i % 2], start=(oc == 0), stop=(oc == 15)),
                reads=[Rr(fstg, i % 2), Rr(ones_c)], writes=[rs_s(tg % 2)])
            sc.op("tensor", lambda t, oc=oc, tg=tg, i=i: t.matmul(
                ps_q(tg % 2), ones_c[:], osq[:, i % 2], start=(oc == 0), stop=(oc == 15)),
                reads=[Rr(osq, i % 2), Rr(ones_c)], writes=[rs_q(tg % 2)])
        sl = finalize(True, (ps_s(tg % 2), rs_s(tg % 2)), (ps_q(tg % 2), rs_q(tg % 2)), LNEPS)
        for c in range(DC):
            i = tt * DC + c
            sc.dma("sync", lambda s, tt=tt, c=c, i=i: s.dma_start(
                out=xc[:, i % 2],
                in_=x3_in.rearrange("(c p) (t f) -> p c t f", p=128, f=512)[:, c, tt]),
                writes=[Rr(xc, i % 2)])
            sc.dma("sync", lambda s, tt=tt, c=c, i=i: s.dma_start(
                out=stg[:, i % 2],
                in_=o7_dm.rearrange("(c p) (t f) -> p c t f", p=128, f=512)[:, c, tt]),
                reads=[Rr(o7_dm, (c, tt))], writes=[Rr(stg, i % 2)])
            sc.op("vector", lambda v, i=i, sl=sl: v.tensor_tensor(
                out=asq[:, i % 2], in0=stg[:, i % 2], in1=rbc[:, sl], op=AL.mult),
                reads=[Rr(stg, i % 2), Rr(rbc, sl)], writes=[Rr(asq, i % 2)])
            sc.op("gpsimd", lambda g, i=i, sl=sl: g.tensor_tensor(
                out=t1b[:, i % 2], in0=asq[:, i % 2], in1=mrbc[:, sl], op=AL.add),
                reads=[Rr(asq, i % 2), Rr(mrbc, sl)], writes=[Rr(t1b, i % 2)])
            sc.op("vector", lambda v, i=i: v.tensor_tensor(out=xc[:, i % 2], in0=xc[:, i % 2],
                                                           in1=t1b[:, i % 2], op=AL.add),
                  reads=[Rr(xc, i % 2), Rr(t1b, i % 2)], writes=[Rr(xc, i % 2)])
            sc.dma("sync", lambda s, tt=tt, c=c, i=i: s.dma_start(
                out=x5_out.rearrange("(c p) (t f) -> p c t f", p=128, f=512)[:, c, tt],
                in_=xc[:, i % 2]),
                reads=[Rr(xc, i % 2)], writes=[Rr(x5_out, (c, tt))], slot=Rr(xc, i % 2))


def build_p7():
    p = P("p7")
    ao_in = p.inp("ao7", [D, TOK], BF16)
    sc = Sched(p)
    io = dict(
        ao_ap=lambda c: ao_in.rearrange("(c p) t -> p c t", p=128)[:, c],
        xin_dm=p.inp("xin_dm", [D, TOK]),
        x3_dm=p.inp("x3_dm", [D, TOK]),
        o_w=p.inp("o_w", [16, 128, DC * 128]),
        ln2_w=p.inp("ln2_w", [D]),
        sh_gate=p.inp("sh_gate", [8, 128, DC * 128]), sh_up=p.inp("sh_up", [8, 128, DC * 128]),
        sh_down=p.inp("sh_down", [16, 128, 8 * 128]),
        r_gate=p.inp("r_gate", [7, 8, 128, DC * 128]), r_up=p.inp("r_up", [7, 8, 128, DC * 128]),
        r_down=p.inp("r_down", [7, 16, 128, 8 * 128]),
        router_w=p.inp("router_w", [1, 128, DC * 7]),
        router_b=p.inp("router_b", [7]),
        x5_dm=p.out("x5_dm", [D, TOK]),
    )
    emit_p7(p, sc, io)
    sc.emit()
    return p.done()


def host_p7_inputs(p6res, p5res, inputs, core):
    b, hh = core // 2, core % 2
    sl = slice(hh * TOK, (hh + 1) * TOK)
    ao = np.concatenate([p6res[2 * b]["ao6"], p6res[2 * b + 1]["ao6"]], axis=0)
    return {
        "ao7": np.ascontiguousarray(ao[:, sl]),
        "xin_dm": np.asarray(p5res[core]["xin_dm"]),
        "x3_dm": np.asarray(p5res[core]["x3_dm"]),
        "o_w": _pk(inputs["L1_ffn_o_w"]),
        "ln2_w": np.asarray(inputs["L1_ffn_ln2_w"]),
        "sh_gate": _pk(inputs["L1_ffn_sh_gate"]),
        "sh_up": _pk(inputs["L1_ffn_sh_up"]),
        "sh_down": _pk(inputs["L1_ffn_sh_down"]),
        "r_gate": np.stack([_pk(np.asarray(inputs["L1_ffn_r_gate"])[e]) for e in range(7)]),
        "r_up": np.stack([_pk(np.asarray(inputs["L1_ffn_r_up"])[e]) for e in range(7)]),
        "r_down": np.stack([_pk(np.asarray(inputs["L1_ffn_r_down"])[e]) for e in range(7)]),
        "router_w": _pk(inputs["L1_ffn_router_w"], 7),
        "router_b": np.asarray(inputs["L1_ffn_router_b"]),
    }


def build_fused():
    from concourse.bass import ds as _ds
    p = P("fz")
    nc = p.nc
    nc.cache_partition_id()
    pid = nc.partition_id()
    b2 = (pid // 2) * 2
    hh = pid % 2
    hs8 = (pid % 2) * 8
    sc = Sched(p)
    RG = [list(range(8))]

    x_dm = p.inp("x_dm", [D, TOK])
    Wq = p.inp("Wq", [16, 128, DC * 128]); Wk = p.inp("Wk", [16, 128, DC * 128])
    Wv = p.inp("Wv", [4, 128, DC * 512])
    Wf = p.inp("Wf", [1, 128, DC * 16])
    seg_in = p.inp("seg0", [1, S])
    Wo = p.inp("Wo", [16, 128, DC * 128]); normw = p.inp("norm_w", [D])
    wqgT = p.inp("wqgT", [HC, 128, DC * 128], I32); wquT = p.inp("wquT", [HC, 128, DC * 128], I32)
    wqdT = p.inp("wqdT", [16, 128, HC * 128], I32)
    sg_in = p.inp("s_gate", [HID]); su_in = p.inp("s_up", [HID]); sd_in = p.inp("s_down", [D])
    dtwT = p.inp("dtwT", [16, 128, DC * 128]); dtb_in = p.inp("dtb", [D])
    xpwT = p.inp("xpwT", [1, 128, DC * 32]); xpb_in = p.inp("xpb", [32])
    al_in = p.inp("A_sl", [2, 128, 16])
    opwT = p.inp("opwT", [16, 128, DC * 128]); opb_in = p.inp("opb", [D]); ln1_in = p.inp("ln1_w", [D])
    kvdw = p.inp("kv_d_w", [4, 128, DC * 128]); qdw = p.inp("q_d_w", [4, 128, DC * 128])
    ropekw = p.inp("rope_k_w", [8, 128, DC * 128])
    kuw = p.inp("k_u_w", [8, 128, LC * 128]); quw = p.inp("q_u_w", [8, 128, LC * 128])
    vuw = p.inp("v_u_w", [4, 128, LC * 512]); ropeqw = p.inp("rope_q_w", [8, 128, LC * 128])
    cos_in = p.inp("cos_t", [128, TOK]); sin_in = p.inp("sin_t", [128, TOK])
    rt_in = p.inp("RT", [128, 128])
    ow_in = p.inp("o_w", [16, 128, DC * 128]); ln2_in = p.inp("ln2_w", [D])
    shg = p.inp("sh_gate", [8, 128, DC * 128]); shu = p.inp("sh_up", [8, 128, DC * 128])
    shd = p.inp("sh_down", [16, 128, 8 * 128])
    rg = p.inp("r_gate", [7, 8, 128, DC * 128]); ru = p.inp("r_up", [7, 8, 128, DC * 128])
    rd = p.inp("r_down", [7, 16, 128, 8 * 128])
    rw_in = p.inp("router_w", [1, 128, DC * 7]); rb_in = p.inp("router_b", [7])
    x5_out = p.out("x5_dm", [D, TOK])

    SH = dict(addr_space="Shared")
    q1 = p.dram("q1", [D, TOK], BF16); q1g = p.dram("q1g", [8 * D, TOK], BF16, **SH)
    k1 = p.dram("k1", [D, TOK], BF16); k1g = p.dram("k1g", [8 * D, TOK], BF16, **SH)
    v1 = p.dram("v1", [TOK, D], BF16); v1g = p.dram("v1g", [8 * TOK, D], BF16, **SH)
    ld1 = p.dram("ld1", [16, TOK]); ld1g = p.dram("ld1g", [8 * 16, TOK], **SH)
    o2 = p.dram("o2", [1024, S], BF16); o2g = p.dram("o2g", [8 * 1024, S], BF16, **SH)
    x2i = p.dram("x2i", [D, TOK])
    xn2i = p.dram("xn2i", [D, TOK], BF16); xn2g = p.dram("xn2g", [8 * D, TOK], BF16, **SH)
    dti = p.dram("dti", [D, TOK]); dtg = p.dram("dtg", [8 * D, TOK], **SH)
    bci = p.dram("bci", [32, TOK], BF16); bcg = p.dram("bcg", [8 * 32, TOK], BF16, **SH)
    y4i = p.dram("y4i", [4, 256, S], BF16); y4g = p.dram("y4g", [8 * 1024, S], BF16, **SH)
    x3i = p.dram("x3i", [D, TOK])
    xini = p.dram("xini", [D, TOK])
    q5i = p.dram("q5i", [D, TOK], BF16); q5g = p.dram("q5g", [8 * D, TOK], BF16, **SH)
    k5i = p.dram("k5i", [D, TOK], BF16); k5g = p.dram("k5g", [8 * D, TOK], BF16, **SH)
    v5i = p.dram("v5i", [TOK, D], BF16); v5g = p.dram("v5g", [8 * TOK, D], BF16, **SH)
    ao6i = p.dram("ao6i", [1024, S], BF16); ao6g = p.dram("ao6g", [8 * 1024, S], BF16, **SH)

    def AG(src, dst, wr_keys):
        sc.cc(lambda g, src=src, dst=dst: g.collective_compute(
            "AllGather", ALU.bypass, replica_groups=RG, ins=[src.opt()], outs=[dst.opt()]),
            reads=[], writes=wr_keys)

    p.push_scope("z1_")
    emit_p1(p, sc, dict(x_dm=x_dm, Wq=Wq, Wk=Wk, Wv=Wv, Wf=Wf,
                        q_dm=q1, k_dm=k1, v_tm=v1, ld_dm=ld1))
    sc.barrier()
    p.pop_scope()

    q2l = p.dram("q2l", [1024, S], BF16)
    k2l = p.dram("k2l", [1024, S], BF16)
    v2l = p.dram("v2l", [S, 1024], BF16)
    ld2l = p.dram("ld2l", [8, S])
    AG(q1, q1g, [Rr(q1g)])
    AG(k1, k1g, [Rr(k1g)])
    AG(v1, v1g, [Rr(v1g)])
    AG(ld1, ld1g, [Rr(ld1g)])
    sc.dma("sync", lambda e: e.dma_start(
        out=q2l.rearrange("d (r t) -> r d t", r=2),
        in_=q1g.rearrange("(r d) t -> r d t", d=D)[_ds(b2, 2), _ds(hs8 * 128, 1024), :]),
        reads=[Rr(q1g)], writes=[Rr(q2l)])
    sc.dma("sync", lambda e: e.dma_start(
        out=k2l.rearrange("d (r t) -> r d t", r=2),
        in_=k1g.rearrange("(r d) t -> r d t", d=D)[_ds(b2, 2), _ds(hs8 * 128, 1024), :]),
        reads=[Rr(k1g)], writes=[Rr(k2l)])
    sc.dma("sync", lambda e: e.dma_start(
        out=v2l.rearrange("(r s) e -> r s e", r=2),
        in_=v1g.rearrange("(r s) e -> r s e", s=TOK)[_ds(b2, 2), :, _ds(hs8 * 128, 1024)]),
        reads=[Rr(v1g)], writes=[Rr(v2l)])
    sc.dma("sync", lambda e: e.dma_start(
        out=ld2l.rearrange("x (r t) -> r x t", r=2),
        in_=ld1g.rearrange("(r x) t -> r x t", x=16)[_ds(b2, 2), _ds(hs8, 8), :]),
        reads=[Rr(ld1g)], writes=[Rr(ld2l)])

    def q_srcs2(h):
        return [(0, S, q2l[h * 128:(h + 1) * 128, :])]

    def k_srcs2(h):
        return [(0, S, k2l[h * 128:(h + 1) * 128, :])]

    def v_srcs2(h):
        return [(0, S // 128, v2l.rearrange("(m p) e -> p m e", p=128)[:, :, h * 128:(h + 1) * 128])]

    def ld_srcs2(h):
        return [(0, S, ld2l[h:h + 1, :])]

    for h in range(8):
        pass
    p.push_scope("z2_")
    io2 = dict(q_srcs=q_srcs2, k_srcs=k_srcs2, v_srcs=v_srcs2, ld_srcs=ld_srcs2,
               seg0=seg_in, o_dm=o2, deps=[Rr(q2l), Rr(k2l), Rr(v2l), Rr(ld2l)])
    emit_p2(p, sc, io2)
    sc.barrier()
    p.pop_scope()

    o3l = p.dram("o3l", [D, TOK], BF16)

    def o_ap3(c, tt):
        return o3l.rearrange("(c p) (t f) -> p c t f", p=128, f=512)[:, c, tt]

    AG(o2, o2g, [Rr(o2g)])
    sc.dma("sync", lambda e: e.dma_start(
        out=o3l.rearrange("(r d) t -> r d t", r=2),
        in_=o2g.rearrange("(r d) t -> r d t", d=1024)[_ds(b2, 2), :, _ds(hh * TOK, TOK)]),
        reads=[Rr(o2g)], writes=[Rr(o3l)] + [Rr(o_ap3, (c, tt)) for c in range(DC)
                                            for tt in range(NTT)])

    p.push_scope("z3_")
    emit_p3(p, sc, dict(o_ap=o_ap3, x_dm=x_dm, Wo=Wo, norm_w=normw,
                        wqgT=wqgT, wquT=wquT, wqdT=wqdT,
                        s_gate=sg_in, s_up=su_in, s_down=sd_in,
                        dtwT=dtwT, dtb=dtb_in, xpwT=xpwT, xpb=xpb_in,
                        x2_dm=x2i, xn2_dm=xn2i, dt_dm=dti, bc_dm=bci))
    sc.barrier()
    p.pop_scope()

    dt4l = p.dram("dt4l", [8, 256, TOK])
    xn4l = p.dram("xn4l", [8, 256, TOK], BF16)
    def dt_ap4(b, ds_, tc):
        return dt4l[2 * b + tc // 4, ds_ * 128:(ds_ + 1) * 128,
                    (tc % 4) * 512:(tc % 4 + 1) * 512]

    def xn_ap4(b, ds_, tc):
        return xn4l[2 * b + tc // 4, ds_ * 128:(ds_ + 1) * 128,
                    (tc % 4) * 512:(tc % 4 + 1) * 512]

    def bc_ap4(b, tc):
        return bcg[(2 * b + tc // 4) * 32:(2 * b + tc // 4) * 32 + 32,
                   (tc % 4) * 512:(tc % 4 + 1) * 512].unsqueeze(0)

    AG(dti, dtg, [Rr(dtg)])
    AG(xn2i, xn2g, [Rr(xn2g)])
    AG(bci, bcg, [Rr(bcg)] + [Rr(bc_ap4, (b, tc)) for b in range(4) for tc in range(8)])
    sc.dma("sync", lambda e: e.dma_start(
        out=dt4l[:],
        in_=dtg.rearrange("(k d) t -> k d t", d=D)[:, _ds(pid * 256, 256), :]),
        reads=[Rr(dtg)], writes=[Rr(dt4l)] + [Rr(dt_ap4, (b_, d_, tc))
                                              for b_ in range(4) for d_ in range(2)
                                              for tc in range(8)])
    sc.dma("sync", lambda e: e.dma_start(
        out=xn4l[:],
        in_=xn2g.rearrange("(k d) t -> k d t", d=D)[:, _ds(pid * 256, 256), :]),
        reads=[Rr(xn2g)], writes=[Rr(xn4l)] + [Rr(xn_ap4, (b_, d_, tc))
                                               for b_ in range(4) for d_ in range(2)
                                               for tc in range(8)])

    p.push_scope("z4_")
    emit_p4(p, sc, dict(dt_ap=dt_ap4, xn_ap=xn_ap4, bc_ap=bc_ap4, A_sl=al_in, y4=y4i))
    sc.barrier()
    p.pop_scope()

    y5l = p.dram("y5l", [D, TOK], BF16)

    def y_ap5(c):
        return y5l.rearrange("(c p) t -> p c t", p=128)[:, c]

    AG(y4i.rearrange("b r t -> (b r) t"), y4g, [Rr(y4g)])
    sc.dma("sync", lambda e: e.dma_start(
        out=y5l.rearrange("(k d) t -> k d t", d=256),
        in_=y4g.rearrange("(k d) t -> k d t", d=1024)[:, _ds((pid // 2) * 256, 256),
                                                      _ds(hh * TOK, TOK)]),
        reads=[Rr(y4g)], writes=[Rr(y5l)] + [Rr(y_ap5, c) for c in range(DC)])

    p.push_scope("z5_")
    emit_p5(p, sc, dict(y_ap=y_ap5, x2_dm=x2i, opwT=opwT, opb=opb_in, ln1_w=ln1_in,
                        kv_d_w=kvdw, q_d_w=qdw, rope_k_w=ropekw, k_u_w=kuw, q_u_w=quw,
                        v_u_w=vuw, rope_q_w=ropeqw, cos_t=cos_in, sin_t=sin_in, RT=rt_in,
                        x3_dm=x3i, xin_dm=xini, q5=q5i, k5=k5i, v5=v5i))
    sc.barrier()
    p.pop_scope()

    q6l = p.dram("q6l", [1024, S], BF16)
    k6l = p.dram("k6l", [1024, S], BF16)
    v6l = p.dram("v6l", [S, 1024], BF16)
    def q_srcs6(h):
        return [(0, S, q6l[h * 128:(h + 1) * 128, :])]

    def k_srcs6(h):
        return [(0, S, k6l[h * 128:(h + 1) * 128, :])]

    def v_srcs6(h):
        return [(0, S // 128, v6l.rearrange("(m p) e -> p m e", p=128)[:, :, h * 128:(h + 1) * 128])]

    AG(q5i, q5g, [Rr(q5g)])
    AG(k5i, k5g, [Rr(k5g)])
    AG(v5i, v5g, [Rr(v5g)])
    sc.dma("sync", lambda e: e.dma_start(
        out=q6l.rearrange("d (r t) -> r d t", r=2),
        in_=q5g.rearrange("(r d) t -> r d t", d=D)[_ds(b2, 2), _ds(hs8 * 128, 1024), :]),
        reads=[Rr(q5g)], writes=[Rr(q6l)] + [Rr(q_srcs6, h) for h in range(8)])
    sc.dma("sync", lambda e: e.dma_start(
        out=k6l.rearrange("d (r t) -> r d t", r=2),
        in_=k5g.rearrange("(r d) t -> r d t", d=D)[_ds(b2, 2), _ds(hs8 * 128, 1024), :]),
        reads=[Rr(k5g)], writes=[Rr(k6l)] + [Rr(k_srcs6, h) for h in range(8)])
    sc.dma("sync", lambda e: e.dma_start(
        out=v6l.rearrange("(r s) e -> r s e", r=2),
        in_=v5g.rearrange("(r s) e -> r s e", s=TOK)[_ds(b2, 2), :, _ds(hs8 * 128, 1024)]),
        reads=[Rr(v5g)], writes=[Rr(v6l)] + [Rr(v_srcs6, h) for h in range(8)])

    p.push_scope("z6_")
    emit_p6(p, sc, dict(q_srcs=q_srcs6, k_srcs=k_srcs6, v_srcs=v_srcs6, ao6=ao6i,
                        deps=[Rr(q6l), Rr(k6l), Rr(v6l)]))
    sc.barrier()
    p.pop_scope()

    ao7l = p.dram("ao7l", [D, TOK], BF16)

    def ao_ap7(c):
        return ao7l.rearrange("(c p) t -> p c t", p=128)[:, c]

    AG(ao6i, ao6g, [Rr(ao6g)])
    sc.dma("sync", lambda e: e.dma_start(
        out=ao7l.rearrange("(r d) t -> r d t", r=2),
        in_=ao6g.rearrange("(r d) t -> r d t", d=1024)[_ds(b2, 2), :, _ds(hh * TOK, TOK)]),
        reads=[Rr(ao6g)], writes=[Rr(ao7l)] + [Rr(ao_ap7, c) for c in range(DC)])

    p.push_scope("z7_")
    emit_p7(p, sc, dict(ao_ap=ao_ap7, xin_dm=xini, x3_dm=x3i, o_w=ow_in, ln2_w=ln2_in,
                        sh_gate=shg, sh_up=shu, sh_down=shd, r_gate=rg, r_up=ru, r_down=rd,
                        router_w=rw_in, router_b=rb_in, x5_dm=x5_out))
    sc.barrier()
    p.pop_scope()

    sc.emit()
    return p.done()


def host_fused_inputs(inputs, core):
    b, hh = core // 2, core % 2
    sl = slice(hh * TOK, (hh + 1) * TOK)
    seg = np.ones((1, S), np.float32)
    seg[0, ::128] = 0.0
    RD = 64
    inv_freq = 1.0 / 10000.0 ** (np.arange(0, RD, 2, dtype=np.float32) / RD)
    t_ = (np.arange(TOK, dtype=np.float32) + hh * TOK)
    ang = np.outer(t_, inv_freq)
    emb = np.concatenate([ang, ang], -1)
    cos64 = np.cos(emb).T.astype(np.float32)
    sin64 = np.sin(emb).T.astype(np.float32)
    R64 = np.zeros((64, 64), np.float32)
    for j in range(32):
        R64[j, j + 32] = -1.0
        R64[j + 32, j] = 1.0
    RT = np.zeros((128, 128), np.float32)
    RT[0:64, 0:64] = R64.T
    RT[64:128, 64:128] = R64.T
    dsl = slice(core * 256, (core + 1) * 256)
    return {
        "x_dm": _dm(inputs["L0_attn_x"][b, sl]),
        "Wq": _pk(inputs["L0_attn_Wq"]), "Wk": _pk(inputs["L0_attn_Wk"]),
        "Wv": _pk(inputs["L0_attn_Wv"], 512), "Wf": _pk(inputs["L0_attn_Wf"], 16),
        "seg0": seg,
        "Wo": _pk(inputs["L0_attn_Wo"]),
        "norm_w": np.asarray(inputs["L0_attn_norm_w"]),
        "wqgT": _pk(np.asarray(inputs["L0_ffn_wq_gate"]).T),
        "wquT": _pk(np.asarray(inputs["L0_ffn_wq_up"]).T),
        "wqdT": _pk(np.asarray(inputs["L0_ffn_wq_down"]).T),
        "s_gate": np.asarray(inputs["L0_ffn_s_gate"]),
        "s_up": np.asarray(inputs["L0_ffn_s_up"]),
        "s_down": np.asarray(inputs["L0_ffn_s_down"]),
        "dtwT": _pk(np.asarray(inputs["L1_attn_dt_proj_w"]).T),
        "dtb": np.asarray(inputs["L1_attn_dt_proj_b"]),
        "xpwT": _pk(np.asarray(inputs["L1_attn_x_proj_w"]).T, 32),
        "xpb": np.asarray(inputs["L1_attn_x_proj_b"]),
        "A_sl": np.ascontiguousarray(np.asarray(inputs["L1_attn_A_log"])[dsl].reshape(2, 128, 16)),
        "opwT": _pk(np.asarray(inputs["L1_attn_out_proj_w"]).T),
        "opb": np.asarray(inputs["L1_attn_out_proj_b"]),
        "ln1_w": np.asarray(inputs["L1_ffn_ln1_w"]),
        "kv_d_w": _pk(inputs["L1_ffn_kv_d_w"]),
        "q_d_w": _pk(inputs["L1_ffn_q_d_w"]),
        "rope_k_w": _pk(inputs["L1_ffn_rope_k_w"]),
        "k_u_w": _pk(inputs["L1_ffn_k_u_w"]),
        "q_u_w": _pk(inputs["L1_ffn_q_u_w"]),
        "v_u_w": _pk(inputs["L1_ffn_v_u_w"], 512),
        "rope_q_w": _pk(inputs["L1_ffn_rope_q_w"]),
        "cos_t": np.concatenate([cos64, cos64], 0), "sin_t": np.concatenate([sin64, sin64], 0),
        "RT": RT,
        "o_w": _pk(inputs["L1_ffn_o_w"]),
        "ln2_w": np.asarray(inputs["L1_ffn_ln2_w"]),
        "sh_gate": _pk(inputs["L1_ffn_sh_gate"]),
        "sh_up": _pk(inputs["L1_ffn_sh_up"]),
        "sh_down": _pk(inputs["L1_ffn_sh_down"]),
        "r_gate": np.stack([_pk(np.asarray(inputs["L1_ffn_r_gate"])[e]) for e in range(7)]),
        "r_up": np.stack([_pk(np.asarray(inputs["L1_ffn_r_up"])[e]) for e in range(7)]),
        "r_down": np.stack([_pk(np.asarray(inputs["L1_ffn_r_down"])[e]) for e in range(7)]),
        "router_w": _pk(inputs["L1_ffn_router_w"], 7),
        "router_b": np.asarray(inputs["L1_ffn_router_b"]),
    }


_BUILT = {}


def kernel(**inputs):
    if "fz" not in _BUILT:
        _BUILT["fz"] = build_fused()
    nc = _BUILT["fz"]
    in_maps = [host_fused_inputs(inputs, c) for c in range(NCORE)]
    res = run_bass_kernel_spmd(nc, in_maps, core_ids=list(range(NCORE))).results
    out = np.empty((B, S, D), np.float32)
    for c in range(NCORE):
        b, hh = c // 2, c % 2
        out[b, hh * TOK:(hh + 1) * TOK] = np.asarray(res[c]["x5_dm"]).T
    return out
